# version 01
# baseline (speedup 1.0000x reference)
import numpy as np
import ml_dtypes
import concourse.bass as bass
import concourse.mybir as mybir
from concourse.bass_utils import run_bass_kernel_spmd

F32 = mybir.dt.float32
BF16 = mybir.dt.bfloat16
FP8 = mybir.dt.float8e4
AF = mybir.ActivationFunctionType
ALU = mybir.AluOpType
AX = mybir.AxisListType

B, N, DM = 2, 8192, 2048
DI = 6144
HPC = 4
OWN = 2048
EPS_LN = 1e-5
EPS_AT = 1e-6

NLT = N // 128
NLA = NLT + OWN // 128
NSC = N // 512
NQC = N // 512
NIC = DI // 128
NTC = OWN // 512
NWTC = N // 512
NEC = DM // 128

ATT_TILES = [(qc, jb) for qc in range(NQC) for jb in range(4 * (qc + 1))]
NT_H = len(ATT_TILES)
FB_PASSES = [(0, 6), (6, 12), (12, 16)]

DEBUG = ()
LEVEL = 7
WOPART = 3
SKIPRS = False
COMBPART = 3


def build():
    nc = bass.Bass("TRN2", target_bir_lowering=False, debug=False)
    xb = nc.dram_tensor("xb", [N, DM], F32, kind="ExternalInput").ap()
    x_own = nc.dram_tensor("x_own", [OWN, DM], F32, kind="ExternalInput").ap()
    wq = nc.dram_tensor("wq", [HPC, 128, NEC, 128], F32, kind="ExternalInput").ap()
    wk = nc.dram_tensor("wk", [HPC, 128, NEC, 128], F32, kind="ExternalInput").ap()
    wv = nc.dram_tensor("wv", [HPC, 128, NEC, 128], F32, kind="ExternalInput").ap()
    wo = nc.dram_tensor("wo", [128, HPC, DM], F32, kind="ExternalInput").ap()
    gw = nc.dram_tensor("gw", [NIC, 128, NEC, 128], F32, kind="ExternalInput").ap()
    uw = nc.dram_tensor("uw", [NIC, 128, NEC, 128], F32, kind="ExternalInput").ap()
    dwt = nc.dram_tensor("dwt", [NIC, 128, NEC, 128], F32, kind="ExternalInput").ap()
    gsc = nc.dram_tensor("gsc", [128, NIC], F32, kind="ExternalInput").ap()
    usc = nc.dram_tensor("usc", [128, NIC], F32, kind="ExternalInput").ap()
    dsc = nc.dram_tensor("dsc", [128, NEC], F32, kind="ExternalInput").ap()
    cst = nc.dram_tensor("cst", [128, N], F32, kind="ExternalInput").ap()
    snt = nc.dram_tensor("snt", [128, N], F32, kind="ExternalInput").ap()
    onesd = nc.dram_tensor("onesd", [128, 128], BF16, kind="ExternalInput").ap()
    eyebd = nc.dram_tensor("eyebd", [128, 128], BF16, kind="ExternalInput").ap()
    eyefd = nc.dram_tensor("eyefd", [128, 128], F32, kind="ExternalInput").ap()
    swpd = nc.dram_tensor("swpd", [128, 128], F32, kind="ExternalInput").ap()
    maskd = nc.dram_tensor("maskd", [4, 128, 512], BF16, kind="ExternalInput").ap()

    y_own = nc.dram_tensor("y_own", [OWN, DM], F32, kind="ExternalOutput").ap()

    xn_sc = nc.dram_tensor("xn_sc", [N, DM], BF16, kind="Internal").ap()
    xno_sc = nc.dram_tensor("xno_sc", [OWN, DM], BF16, kind="Internal").ap()
    od_sc = nc.dram_tensor("od_sc", [HPC, 128, N], BF16, kind="Internal").ap()
    fuse_sc = nc.dram_tensor("fuse_sc", [NIC, 128, OWN], BF16, kind="Internal").ap()
    dw8_sc = nc.dram_tensor("dw8_sc", [NIC, 128, DM], FP8, kind="Internal").ap()
    gw8_sc = nc.dram_tensor("gw8_sc", [NIC, 128, DM], FP8, kind="Internal").ap()
    uw8_sc = nc.dram_tensor("uw8_sc", [NIC, 128, DM], FP8, kind="Internal").ap()
    rs_in_t = nc.dram_tensor("rs_in", [N, DM], BF16, kind="Internal")
    rs_out_t = nc.dram_tensor("rs_out", [OWN, DM], BF16, kind="Internal")
    rs_in = rs_in_t.ap()
    rs_out = rs_out_t.ap()

    dbg = {}
    if "xn" in DEBUG:
        dbg["d_xn"] = nc.dram_tensor("d_xn", [N, DM], BF16, kind="ExternalOutput").ap()
    if "od" in DEBUG:
        dbg["d_od"] = nc.dram_tensor("d_od", [HPC * 128, N], BF16, kind="ExternalOutput").ap()
    if "rsin" in DEBUG:
        dbg["d_rsin"] = nc.dram_tensor("d_rsin", [N, DM], BF16, kind="ExternalOutput").ap()
    if "rsout" in DEBUG:
        dbg["d_rsout"] = nc.dram_tensor("d_rsout", [OWN, DM], BF16, kind="ExternalOutput").ap()
    if "fuse" in DEBUG:
        dbg["d_fuse"] = nc.dram_tensor("d_fuse", [NIC * 128, OWN], BF16, kind="ExternalOutput").ap()
    NDBG = len(dbg)

    from contextlib import ExitStack
    with ExitStack() as stack:
        en = stack.enter_context

        A = en(nc.sbuf_tensor("A", [128, 32768], BF16))
        Af = A.bitcast(F32)
        q_bf = A[:, 0:8192]
        k_bf = A[:, 8192:16384]
        v_tm = A[:, 16384:24576]
        xnoT = [A[:, ec * 2048:(ec + 1) * 2048] for ec in range(NEC)]
        q_sb = Af[:, 12288:12800]
        k_sb = Af[:, 12800:13312]
        t1 = Af[:, 13312:13824]
        t2 = Af[:, 13824:14336]
        v_tmp = [A[:, 28672 + i * 512:28672 + (i + 1) * 512] for i in range(2)]
        s2 = [Af[:, 12288:12800], Af[:, 12800:13312]]
        s4 = [A[:, 26624 + i * 512:26624 + (i + 1) * 512] for i in range(2)]
        recip = Af[:, 14848:15360]
        e_out = [A[:, 30720 + i * 512:30720 + (i + 1) * 512] for i in range(2)]

        Bx = en(nc.sbuf_tensor("Bx", [128, 32768], BF16))
        Bf = Bx.bitcast(F32)
        xnt = [[Bx[:, bi * 8192 + ec * 512: bi * 8192 + (ec + 1) * 512]
                for ec in range(NEC)] for bi in range(2)]
        cs_buf = [Bf[:, 8192 + i * 512:8192 + (i + 1) * 512] for i in range(2)]
        sn_buf = [Bf[:, 9216 + i * 512:9216 + (i + 1) * 512] for i in range(2)]
        wq_sb = [Bx[:, 20480 + bi * 6144:20480 + bi * 6144 + 2048] for bi in range(2)]
        wk_sb = [Bx[:, 22528 + bi * 6144:22528 + bi * 6144 + 2048] for bi in range(2)]
        wv_sb = [Bx[:, 24576 + bi * 6144:24576 + bi * 6144 + 2048] for bi in range(2)]
        attn_T = [Bf[:, fc * 512:(fc + 1) * 512] for fc in range(16)]
        stage = [Bx[:, 16384 + i * 2048:16384 + (i + 1) * 2048] for i in range(2)]
        wo_sb = Bx[:, 24576:32768]
        ffn_T = [Bx[:, hc * 2048:(hc + 1) * 2048] for hc in range(NEC)]

        C = en(nc.sbuf_tensor("C", [128, 24576], BF16))
        Cf = C.bitcast(F32)
        x_sb = [Cf[:, i * 2048:(i + 1) * 2048] for i in range(2)]
        xn_sb = [C[:, 8192 + i * 2048:8192 + (i + 1) * 2048] for i in range(2)]
        od_buf = [C[:, i * 2048:(i + 1) * 2048] for i in range(2)]
        g_s = [Cf[:, 1024 + i * 512:1024 + (i + 1) * 512] for i in range(2)]
        fuse_sb = [C[:, 4096 + i * 512:4096 + (i + 1) * 512] for i in range(2)]
        fuse_tc = [C[:, ic * 512:(ic + 1) * 512] for ic in range(NIC)]
        xcomb = [Cf[:, i * 2048:(i + 1) * 2048] for i in range(2)]
        rcomb = [C[:, 8192 + i * 2048 + 0:8192 + i * 2048 + 2048] for i in range(2)]
        ycomb = [Cf[:, 8192 + i * 2048:8192 + (i + 1) * 2048] for i in range(2)]
        ftm = [C[:, 12288 + i * 512:12288 + (i + 1) * 512] for i in range(2)]

        masks = en(nc.sbuf_tensor("masks", [128, 2048], BF16))
        gu8t = en(nc.sbuf_tensor("gu8t", [128, 8192], FP8))
        gu8 = [[gu8t[:, (bi * 2 + j) * 2048:(bi * 2 + j + 1) * 2048] for j in range(2)]
               for bi in range(2)]
        dw8t = en(nc.sbuf_tensor("dw8t", [128, 6144], FP8))
        dw8b = [dw8t[:, i * 3072:(i + 1) * 3072] for i in range(2)]
        dw8p = en(nc.sbuf_tensor("dw8p", [128, 2048], FP8))
        pp8 = [en(nc.sbuf_tensor(f"pp8_{i}", [128, 2048], FP8)) for i in range(2)]
        ones_s = en(nc.sbuf_tensor("ones_s", [128, 128], BF16))
        eyeb_s = en(nc.sbuf_tensor("eyeb_s", [128, 128], BF16))
        eyef_s = en(nc.sbuf_tensor("eyef_s", [128, 128], F32))
        swp_s = en(nc.sbuf_tensor("swp_s", [128, 128], F32))
        gs_sb = en(nc.sbuf_tensor("gs_sb", [128, NIC], F32))
        us_sb = en(nc.sbuf_tensor("us_sb", [128, NIC], F32))
        ds_sb = en(nc.sbuf_tensor("ds_sb", [128, NEC], F32))
        stat6 = [en(nc.sbuf_tensor(f"stat6_{i}", [128, 24], F32)) for i in range(2)]
        mvc = [en(nc.sbuf_tensor(f"mvc_{i}", [128, 8], F32)) for i in range(2)]
        tmp4 = [en(nc.sbuf_tensor(f"tmp4_{i}", [128, 4], F32)) for i in range(2)]
        pq8 = [en(nc.sbuf_tensor(f"pq8_{i}", [128, 8], F32)) for i in range(2)]
        sq8 = [en(nc.sbuf_tensor(f"sq8_{i}", [128, 8], F32)) for i in range(2)]
        padb = en(nc.sbuf_tensor("padb", [128, 1], F32))
        mv = [en(nc.sbuf_tensor(f"mv_{i}", [128, 2], F32)) for i in range(2)]
        veps = [en(nc.sbuf_tensor(f"veps_{i}", [128, 1], F32)) for i in range(2)]
        stdv = [en(nc.sbuf_tensor(f"stdv_{i}", [128, 1], F32)) for i in range(2)]
        rstd = [en(nc.sbuf_tensor(f"rstd_{i}", [128, 1], F32)) for i in range(2)]

        P = [en(nc.psum_tensor(f"P{i}", [128, 512], F32)) for i in range(7)]
        PTt = en(nc.psum_tensor("PTt", [128, 1024], BF16))
        PT = [PTt[:, 0:512], PTt[:, 512:1024]]

        names = ["sL", "sW", "V0", "D0", "A0", "sT", "gQ", "mQ", "aQ", "vR",
                 "mS", "aS", "vS", "mV", "vF", "sA", "gWo", "gO", "mW", "aW",
                 "mT", "aG", "sR", "CC", "gD", "gF", "sTF", "mA", "aA", "vA",
                 "sFu", "sFB", "gB", "mBc", "aB", "sX", "mC", "vC",
                 "ST", "gC", "aC"]
        S = {nm: en(nc.semaphore(nm)) for nm in names}
        block = en(nc.Block())

        def ln_src(ti):
            if ti < NLT:
                return xb[ti * 128:(ti + 1) * 128, :]
            t = ti - NLT
            return x_own[t * 128:(t + 1) * 128, :]

        def ln_dst(ti):
            if ti < NLT:
                return xn_sc[ti * 128:(ti + 1) * 128, :]
            t = ti - NLT
            return xno_sc[t * 128:(t + 1) * 128, :]

        @block.sync
        def _(sync):
            sync.dma_start(out=x_sb[0], in_=ln_src(0)).then_inc(S["sL"], 16)
            sync.dma_start(out=x_sb[1], in_=ln_src(1)).then_inc(S["sL"], 16)
            for ti in range(NLA):
                sync.wait_ge(S["V0"], ti + 1)
                sync.dma_start(out=ln_dst(ti), in_=xn_sb[ti % 2]).then_inc(S["sW"], 16)
                if ti + 2 < NLA:
                    sync.dma_start(out=x_sb[ti % 2], in_=ln_src(ti + 2)).then_inc(S["sL"], 16)
            for h in range(HPC if LEVEL >= 2 else 0):
                for sc in range(NSC):
                    Sg = h * NSC + sc
                    sync.wait_ge(S["sW"], 16 * min(4 * sc + 4, NLT))
                    if Sg >= 2:
                        sync.wait_ge(S["mQ"], 6 * (Sg - 2) + 3)
                        sync.wait_ge(S["vR"], 2 * (Sg - 1))
                    for ec in range(NEC):
                        sync.dma_start(
                            out=xnt[Sg % 2][ec],
                            in_=xn_sc[sc * 512:(sc + 1) * 512, ec * 128:(ec + 1) * 128],
                            transpose=True).then_inc(S["sT"], 16)
                    sync.dma_start(out=cs_buf[Sg % 2],
                                   in_=cst[:, sc * 512:(sc + 1) * 512]).then_inc(S["sT"], 16)
                    sync.dma_start(out=sn_buf[Sg % 2],
                                   in_=snt[:, sc * 512:(sc + 1) * 512]).then_inc(S["sT"], 16)
                for qc in range(NQC):
                    g = h * NQC + qc
                    sync.wait_ge(S["vF"], g + 1)
                    sync.dma_start(out=od_sc[h, :, qc * 512:(qc + 1) * 512],
                                   in_=e_out[g % 2]).then_inc(S["sA"], 16)
            if LEVEL < 2:
                sync.wait_ge(S["sW"], 16 * NLA)
            if LEVEL >= 5:
                sync.wait_ge(S["vF"], HPC * NQC)
            if LEVEL >= 5:
                sync.wait_ge(S["sA"], 16 * HPC * NQC)
                sync.wait_ge(S["sW"], 16 * NLA)
                for ec in range(NEC):
                    sync.dma_start(out=xnoT[ec], in_=xno_sc[:, ec * 128:(ec + 1) * 128],
                                   transpose=True).then_inc(S["sTF"], 16)
            for tc in range(NWTC if (LEVEL >= 3 and WOPART >= 3) else 0):
                for tt in range(4):
                    TT = 4 * tc + tt
                    sync.wait_ge(S["aG"], 4 * TT + 4)
                    sync.dma_start(out=rs_in[TT * 128:(TT + 1) * 128, :],
                                   in_=stage[TT % 2]).then_inc(S["sR"], 16)
            if LEVEL >= 5:
                sync.wait_ge(S["gD"], 32 * 9 * NWTC)
                if LEVEL >= 4 and not SKIPRS:
                    sync.wait_ge(S["CC"], 1)
                for ic0 in (0, 1):
                    sync.dma_start(out=gu8[ic0 % 2][0], in_=gw8_sc[ic0]).then_inc(S["gF"], 16)
                    sync.dma_start(out=gu8[ic0 % 2][1], in_=uw8_sc[ic0]).then_inc(S["gF"], 16)
            for ic in range(NIC if LEVEL >= 5 else 0):
                for tc in range(NTC):
                    k = ic * NTC + tc
                    sync.wait_ge(S["vA"], k + 1)
                    sync.dma_start(out=fuse_sc[ic, :, tc * 512:(tc + 1) * 512],
                                   in_=fuse_sb[k % 2]).then_inc(S["sFu"], 16)
                if ic + 2 < NIC:
                    sync.dma_start(out=gu8[ic % 2][0], in_=gw8_sc[ic + 2]).then_inc(S["gF"], 16)
                    sync.dma_start(out=gu8[ic % 2][1], in_=uw8_sc[ic + 2]).then_inc(S["gF"], 16)
            if LEVEL >= 6:
                sync.wait_ge(S["sFu"], 16 * NIC * NTC)
            for tc in range(NTC if LEVEL >= 6 else 0):
                if tc > 0:
                    sync.wait_ge(S["mBc"], 36 * tc)
                for cg in range(6):
                    ic0 = cg * 8
                    sync.dma_start(
                        out=C[:, ic0 * 512:(ic0 + 8) * 512].rearrange("p (i t) -> p i t", t=512),
                        in_=fuse_sc[ic0:ic0 + 8, :, tc * 512:(tc + 1) * 512].rearrange("i p t -> p i t"),
                    ).then_inc(S["sFB"], 16)
            if LEVEL >= 7:
                sync.wait_ge(S["mBc"], 36 * NTC)
                sync.wait_ge(S["CC"], 1)
            for tt in ((0, 1) if (LEVEL >= 7 and COMBPART >= 2) else ()):
                sync.dma_start(out=xcomb[tt], in_=x_own[tt * 128:(tt + 1) * 128, :]).then_inc(S["sX"], 16)
                sync.dma_start(out=rcomb[tt], in_=rs_out[tt * 128:(tt + 1) * 128, :]).then_inc(S["sX"], 16)
            for tt in range(16 if (LEVEL >= 7 and COMBPART >= 3) else 0):
                sync.wait_ge(S["vC"], 5 * (tt + 1))
                sync.dma_start(out=y_own[tt * 128:(tt + 1) * 128, :],
                               in_=ycomb[tt % 2]).then_inc(S["ST"], 16)
                if tt + 2 < 16:
                    sync.dma_start(out=xcomb[tt % 2],
                                   in_=x_own[(tt + 2) * 128:(tt + 3) * 128, :]).then_inc(S["sX"], 16)
                    sync.dma_start(out=rcomb[tt % 2],
                                   in_=rs_out[(tt + 2) * 128:(tt + 3) * 128, :]).then_inc(S["sX"], 16)
            nst = 16 if (LEVEL >= 7 and COMBPART >= 3) else 0
            if LEVEL >= 7 and COMBPART == 0:
                sync.wait_ge(S["mC"], 64)
            elif LEVEL >= 7 and COMBPART == 1:
                sync.wait_ge(S["aC"], 64)
            elif LEVEL >= 7 and COMBPART == 2:
                sync.wait_ge(S["vC"], 80)
            if LEVEL < 7:
                if LEVEL == 1:
                    sync.wait_ge(S["sW"], 16 * NLA)
                elif LEVEL == 2:
                    sync.wait_ge(S["sA"], 16 * HPC * NQC)
                elif LEVEL == 3:
                    if WOPART >= 3:
                        sync.wait_ge(S["sR"], 16 * NWTC * 4)
                    elif WOPART == 2:
                        sync.wait_ge(S["mT"], 256)
                    else:
                        sync.wait_ge(S["aW"], 256)
                elif LEVEL == 4:
                    sync.wait_ge(S["CC"], 1)
                elif LEVEL == 5:
                    sync.wait_ge(S["sFu"], 16 * NIC * NTC)
                elif LEVEL == 6:
                    sync.wait_ge(S["aB"], 3 * NTC)
            if "xn" in DEBUG:
                sync.dma_start(out=dbg["d_xn"], in_=xn_sc).then_inc(S["ST"], 16)
                nst += 1
            if "od" in DEBUG:
                sync.dma_start(out=dbg["d_od"], in_=od_sc.rearrange("h p t -> (h p) t")).then_inc(S["ST"], 16)
                nst += 1
            if "rsin" in DEBUG:
                sync.dma_start(out=dbg["d_rsin"], in_=rs_in).then_inc(S["ST"], 16)
                nst += 1
            if "rsout" in DEBUG:
                sync.dma_start(out=dbg["d_rsout"], in_=rs_out).then_inc(S["ST"], 16)
                nst += 1
            if "fuse" in DEBUG:
                sync.dma_start(out=dbg["d_fuse"], in_=fuse_sc.rearrange("i p t -> (i p) t")).then_inc(S["ST"], 16)
                nst += 1
            sync.wait_ge(S["ST"], 16 * nst)

        @block.gpsimd
        def _(g):
            g.dma_start(out=ones_s[:], in_=onesd).then_inc(S["gC"], 16)
            g.dma_start(out=eyeb_s[:], in_=eyebd).then_inc(S["gC"], 16)
            g.dma_start(out=eyef_s[:], in_=eyefd).then_inc(S["gC"], 16)
            g.dma_start(out=swp_s[:], in_=swpd).then_inc(S["gC"], 16)
            g.dma_start(out=masks[:].rearrange("p (r q) -> p r q", q=512),
                        in_=maskd.rearrange("r p q -> p r q")).then_inc(S["gC"], 16)
            g.dma_start(out=gs_sb[:], in_=gsc).then_inc(S["gC"], 16)
            g.dma_start(out=us_sb[:], in_=usc).then_inc(S["gC"], 16)
            g.dma_start(out=ds_sb[:], in_=dsc).then_inc(S["gC"], 16)
            for h in range(HPC if LEVEL >= 2 else 0):
                if h >= 2:
                    g.wait_ge(S["mQ"], 6 * NSC * (h - 1))
                for wsrc, wdst in [(wq, wq_sb), (wk, wk_sb), (wv, wv_sb)]:
                    g.dma_start(out=wdst[h % 2].rearrange("p (ec d) -> p ec d", d=128),
                                in_=wsrc[h]).then_inc(S["gQ"], 16)
            if LEVEL >= 3:
                g.wait_ge(S["vF"], HPC * NQC)
                g.dma_start(out=wo_sb.rearrange("p (dh f) -> p dh f", f=DM),
                            in_=wo).then_inc(S["gWo"], 16)
            for tc in range(NWTC if LEVEL >= 3 else 0):
                if tc == 0:
                    g.wait_ge(S["sA"], 16 * HPC * NQC)
                if tc >= 2:
                    g.wait_ge(S["mW"], 16 * (tc - 1))
                g.dma_start(out=od_buf[tc % 2].rearrange("p (dh t) -> p dh t", t=512),
                            in_=od_sc[:, :, tc * 512:(tc + 1) * 512].rearrange("h p t -> p h t"),
                            ).then_inc(S["gO"], 16)
                for j in range(9 if LEVEL >= 5 else 0):
                    p = tc * 9 + j
                    ic = p // 3
                    srcw, dstw = ((dwt, dw8_sc), (gw, gw8_sc), (uw, uw8_sc))[p % 3]
                    if p >= 2:
                        g.wait_ge(S["gD"], 32 * (p - 1))
                    g.dma_start(out=pp8[p % 2][:],
                                in_=srcw[ic].rearrange("p a b -> p (a b)")).then_inc(S["gD"], 16)
                    g.wait_ge(S["gD"], 32 * p + 16)
                    g.dma_start(out=dstw[ic], in_=pp8[p % 2][:]).then_inc(S["gD"], 16)

            if LEVEL >= 4 and not SKIPRS:
                g.wait_ge(S["sR"], 16 * NWTC * 4)
                if LEVEL >= 5:
                    g.wait_ge(S["gD"], 32 * 9 * NWTC)
                g.collective_compute(
                    "ReduceScatter", ALU.add,
                    replica_groups=[[0, 1, 2, 3], [4, 5, 6, 7]],
                    ins=[rs_in_t.ap().opt()], outs=[rs_out_t.ap().opt()],
                ).then_inc(S["CC"], 1)
            if LEVEL >= 6:
                g.wait_ge(S["gD"], 32 * 9 * NWTC)
                g.wait_ge(S["CC"], 1)
            for tc in range(NTC if LEVEL >= 6 else 0):
                for pi, (h0, h1) in enumerate(FB_PASSES):
                    gidx = tc * 3 + pi
                    w = (h1 - h0) * 128
                    for cg in range(12):
                        CH = gidx * 12 + cg
                        if CH >= 2:
                            g.wait_ge(S["mBc"], CH - 1)
                        g.dma_start(
                            out=dw8b[CH % 2][:, 0:4 * w].rearrange("p (i t) -> p i t", t=w),
                            in_=dw8_sc[cg * 4:cg * 4 + 4, :, h0 * 128:h1 * 128].rearrange("i p t -> p i t"),
                        ).then_inc(S["gB"], 16)

        @block.tensor
        def _(te):
            te.wait_ge(S["gC"], 16 * 8)
            for h in range(HPC if LEVEL >= 2 else 0):
                hb = h % 2
                for sc in range(NSC):
                    Sg = h * NSC + sc
                    te.wait_ge(S["sT"], 16 * 18 * (Sg + 1))
                    te.wait_ge(S["gQ"], 16 * 3 * (h + 1))
                    if h >= 1 and sc == 0:
                        te.wait_ge(S["vF"], NQC * h)
                    if Sg >= 1:
                        te.wait_ge(S["aQ"], 4 * (Sg - 1) + 3)
                    for wsb, pb in ((wq_sb, P[0]), (wk_sb, P[1]), (wv_sb, P[2])):
                        for ec in range(NEC):
                            ins = te.matmul(pb[:], wsb[hb][:, ec * 128:(ec + 1) * 128],
                                            xnt[Sg % 2][ec], start=(ec == 0), stop=(ec == NEC - 1))
                        ins.then_inc(S["mQ"], 1)
                    te.wait_ge(S["aQ"], 4 * Sg + 1)
                    if Sg >= 1:
                        te.wait_ge(S["vR"], 2 * (Sg - 1) + 1)
                    te.matmul(P[3][:], swp_s[:], q_sb, start=True, stop=True).then_inc(S["mQ"], 1)
                    te.wait_ge(S["aQ"], 4 * Sg + 2)
                    if Sg >= 1:
                        te.wait_ge(S["vR"], 2 * Sg)
                    te.matmul(P[4][:], swp_s[:], k_sb, start=True, stop=True).then_inc(S["mQ"], 1)
                    te.wait_ge(S["aQ"], 4 * Sg + 3)
                    if Sg >= 1:
                        te.wait_ge(S["aQ"], 4 * (Sg - 1) + 4)
                    for j in range(4):
                        ins = te.transpose(PT[0][:, j * 128:(j + 1) * 128],
                                           v_tmp[Sg % 2][:, j * 128:(j + 1) * 128],
                                           eyeb_s[:])
                    ins.then_inc(S["mQ"], 1)
                T0 = NT_H * h
                for t, (qc, jb) in enumerate(ATT_TILES):
                    T = T0 + t
                    buf = T % 2
                    pS = P[5 + buf]
                    if T >= 2:
                        te.wait_ge(S["aS"], T - 1)
                    te.wait_ge(S["vR"], 2 * NSC * h + 2 * (qc + 1))
                    te.matmul(pS[:], k_bf[:, jb * 128:(jb + 1) * 128],
                              q_bf[:, qc * 512:(qc + 1) * 512],
                              start=True, stop=True).then_inc(S["mS"], 1)
                    te.wait_ge(S["vS"], T + 1)
                    first = (jb == 0)
                    last = (jb == 4 * (qc + 1) - 1)
                    if first:
                        gq = h * NQC + qc
                        if gq > 0:
                            te.wait_ge(S["vF"], gq)
                        if qc == 0:
                            te.wait_ge(S["aQ"], 64 * h + 63)
                    if jb % 4 == 0:
                        te.wait_ge(S["aQ"], 64 * h + 4 * (jb // 4 + 1))
                    te.matmul(P[2][:], v_tm[:, jb * 128:(jb + 1) * 128], s4[buf],
                              start=first, stop=last)
                    te.matmul(P[1][:], ones_s[:], s4[buf],
                              start=first, stop=last).then_inc(S["mV"], 1)
            if LEVEL >= 3:
                te.wait_ge(S["vF"], HPC * NQC)
                te.wait_ge(S["gWo"], 16)
            for tc in range(NWTC if LEVEL >= 3 else 0):
                te.wait_ge(S["gO"], 16 * (tc + 1))
                for fc in range(16):
                    idx = 16 * tc + fc
                    bank = P[fc % 2]
                    if idx >= 2:
                        te.wait_ge(S["aW"], idx - 1)
                    for dh in range(HPC):
                        ins = te.matmul(bank[:],
                                        wo_sb[:, dh * 2048 + fc * 128: dh * 2048 + fc * 128 + 128],
                                        od_buf[tc % 2][:, dh * 512:(dh + 1) * 512],
                                        start=(dh == 0), stop=(dh == HPC - 1))
                    ins.then_inc(S["mW"], 1)
                if WOPART < 2:
                    continue
                te.wait_ge(S["aW"], 16 * tc + 16)
                for tt in range(4):
                    for fcg in range(4):
                        gi = 16 * tc + 4 * tt + fcg
                        if gi >= 1 and WOPART >= 3:
                            te.wait_ge(S["aG"], gi)
                        for j in range(4):
                            fc = 4 * fcg + j
                            ins = te.transpose(P[2][:, j * 128:(j + 1) * 128],
                                               attn_T[fc][:, tt * 128:(tt + 1) * 128],
                                               eyef_s[:])
                        ins.then_inc(S["mT"], 1)
            if LEVEL >= 5:
                te.wait_ge(S["sTF"], 16 * NEC)
                if LEVEL >= 3:
                    te.wait_ge(S["aW"], 256)
                    te.wait_ge(S["aG"], 256)
            for ic in range(NIC if LEVEL >= 5 else 0):
                te.wait_ge(S["gF"], 32 * (ic + 1))
                for tc in range(NTC):
                    k = ic * NTC + tc
                    pg, pu = (P[0], P[1]) if k % 2 == 0 else (P[2], P[3])
                    if k >= 2:
                        te.wait_ge(S["aA"], k - 1)
                        te.wait_ge(S["vA"], k - 1)
                    for gj, pb in ((0, pg), (1, pu)):
                        for ec in range(NEC):
                            ins = te.matmul(pb[:], gu8[ic % 2][gj][:, ec * 128:(ec + 1) * 128],
                                            xnoT[ec][:, tc * 512:(tc + 1) * 512],
                                            start=(ec == 0), stop=(ec == NEC - 1))
                        ins.then_inc(S["mA"], 1)
            for tc in range(NTC if LEVEL >= 6 else 0):
                te.wait_ge(S["sFB"], 16 * 6 * (tc + 1))
                for pi, (h0, h1) in enumerate(FB_PASSES):
                    gidx = tc * 3 + pi
                    nh = h1 - h0
                    w = nh * 128
                    if gidx == 0:
                        te.wait_ge(S["aA"], NIC * NTC)
                        te.wait_ge(S["vA"], NIC * NTC)
                    else:
                        te.wait_ge(S["aB"], gidx)
                    for cg in range(12):
                        CH = gidx * 12 + cg
                        te.wait_ge(S["gB"], 16 * (CH + 1))
                        for j in range(4):
                            ic = cg * 4 + j
                            for hi in range(nh):
                                ins = te.matmul(P[hi][:],
                                                dw8b[CH % 2][:, j * w + hi * 128: j * w + hi * 128 + 128],
                                                fuse_tc[ic],
                                                start=(ic == 0), stop=(ic == NIC - 1))
                        ins.then_inc(S["mBc"], 1)
            if LEVEL >= 7:
                te.wait_ge(S["aB"], 3 * NTC)
            for tt in range(16 if LEVEL >= 7 else 0):
                for fcg in range(4):
                    gi = 4 * tt + fcg
                    pt = PT[0]
                    if gi >= 1 and COMBPART >= 1:
                        te.wait_ge(S["aC"], gi)
                    for j in range(4):
                        hc = fcg * 4 + j
                        ins = te.transpose(pt[:, j * 128:(j + 1) * 128],
                                           ffn_T[hc][:, tt * 128:(tt + 1) * 128],
                                           eyeb_s[:])
                    ins.then_inc(S["mC"], 1)

        @block.scalar
        def _(sc_):
            sc_.wait_ge(S["gC"], 16 * 8)
            for ti in range(NLA):
                sc_.wait_ge(S["D0"], ti + 1)
                if ti >= 2:
                    sc_.wait_ge(S["V0"], ti - 1)
                sc_.activation(stdv[ti % 2][:], veps[ti % 2][:], AF.Sqrt).then_inc(S["A0"], 1)
            for h in range(HPC if LEVEL >= 2 else 0):
                for sc in range(NSC):
                    Sg = h * NSC + sc
                    sc_.wait_ge(S["mQ"], 6 * Sg + 1)
                    if Sg >= 1:
                        sc_.wait_ge(S["vR"], 2 * Sg - 1)
                    sc_.activation(q_sb, P[0][:], AF.Copy).then_inc(S["aQ"], 1)
                    sc_.wait_ge(S["mQ"], 6 * Sg + 2)
                    if Sg >= 1:
                        sc_.wait_ge(S["vR"], 2 * Sg)
                    sc_.activation(k_sb, P[1][:], AF.Copy).then_inc(S["aQ"], 1)
                    sc_.wait_ge(S["mQ"], 6 * Sg + 3)
                    sc_.activation(v_tmp[Sg % 2], P[2][:], AF.Copy).then_inc(S["aQ"], 1)
                    sc_.wait_ge(S["mQ"], 6 * Sg + 6)
                    sc_.activation(v_tm[:, sc * 512:(sc + 1) * 512], PT[0], AF.Copy).then_inc(S["aQ"], 1)
                T0 = NT_H * h
                for t in range(NT_H):
                    T = T0 + t
                    buf = T % 2
                    sc_.wait_ge(S["mS"], T + 1)
                    if t == 0:
                        sc_.wait_ge(S["vR"], 2 * NSC * (h + 1))
                    if T >= 2:
                        sc_.wait_ge(S["vS"], T - 1)
                    sc_.activation(s2[buf], P[5 + buf][:], AF.Square).then_inc(S["aS"], 1)
            for tc in range(NWTC if LEVEL >= 3 else 0):
                for fc in range(16):
                    idx = 16 * tc + fc
                    sc_.wait_ge(S["mW"], idx + 1)
                    if tc >= 1 and fc == 0 and WOPART >= 2:
                        sc_.wait_ge(S["mT"], 16 * tc)
                    sc_.activation(attn_T[fc], P[fc % 2][:], AF.Copy).then_inc(S["aW"], 1)
                for tt in range(4 if WOPART >= 3 else 0):
                    TT = 4 * tc + tt
                    for fcg in range(4):
                        gi = 16 * tc + 4 * tt + fcg
                        sc_.wait_ge(S["mT"], gi + 1)
                        if fcg == 0 and TT >= 2:
                            sc_.wait_ge(S["sR"], 16 * (TT - 1))
                        sc_.activation(stage[TT % 2][:, fcg * 512:(fcg + 1) * 512],
                                       P[2][:], AF.Copy).then_inc(S["aG"], 1)
            for ic in range(NIC if LEVEL >= 5 else 0):
                for tc in range(NTC):
                    k = ic * NTC + tc
                    pg = P[0] if k % 2 == 0 else P[2]
                    sc_.wait_ge(S["mA"], 2 * k + 1)
                    if k >= 2:
                        sc_.wait_ge(S["vA"], k - 1)
                    sc_.activation(g_s[k % 2], pg[:], AF.Silu,
                                   scale=gs_sb[:, ic:ic + 1]).then_inc(S["aA"], 1)
            for tc in range(NTC if LEVEL >= 6 else 0):
                for pi, (h0, h1) in enumerate(FB_PASSES):
                    gidx = tc * 3 + pi
                    sc_.wait_ge(S["mBc"], 12 * (gidx + 1))
                    for hi, hc in enumerate(range(h0, h1)):
                        ins = sc_.activation(ffn_T[hc][:, tc * 512:(tc + 1) * 512], P[hi][:],
                                             AF.Identity, scale=ds_sb[:, hc:hc + 1])
                    ins.then_inc(S["aB"], 1)
            for tt in range(16 if (LEVEL >= 7 and COMBPART >= 1) else 0):
                for fcg in range(4):
                    gi = 4 * tt + fcg
                    sc_.wait_ge(S["mC"], gi + 1)
                    if gi >= 2 and COMBPART >= 2:
                        pg = gi - 2
                        sc_.wait_ge(S["vC"], 5 * (pg // 4) + (pg % 4) + 2)
                    sc_.activation(ftm[gi % 2], PT[0], AF.Copy).then_inc(S["aC"], 1)

        @block.vector
        def _(ve):
            ve.wait_ge(S["gC"], 16 * 8)
            for ti in range(NLA):
                ve.wait_ge(S["sL"], 16 * (ti + 1))
                p = ti % 2
                for kk in range(4):
                    ve.bn_stats(stat6[p][:, 6 * kk:6 * kk + 6],
                                x_sb[p][:, 512 * kk:512 * (kk + 1)])
                for kk in range(4):
                    ve.bn_aggr(mvc[p][:, 2 * kk:2 * kk + 2], stat6[p][:, 6 * kk:6 * kk + 6])
                ve.tensor_add(pq8[p][:, 0:2], mvc[p][:, 0:2], mvc[p][:, 2:4])
                ve.tensor_add(pq8[p][:, 2:4], mvc[p][:, 4:6], mvc[p][:, 6:8])
                ve.tensor_mul(sq8[p][:], mvc[p][:], mvc[p][:])
                ve.tensor_add(pq8[p][:, 4:6], pq8[p][:, 0:2], pq8[p][:, 2:4])
                ve.tensor_add(tmp4[p][:, 0:2], sq8[p][:, 0:2], sq8[p][:, 2:4])
                ve.tensor_add(tmp4[p][:, 2:4], sq8[p][:, 4:6], sq8[p][:, 6:8])
                ve.tensor_scalar_mul(mv[p][:, 0:1], pq8[p][:, 4:5], 0.25)
                ve.tensor_add(pq8[p][:, 6:8], tmp4[p][:, 0:2], tmp4[p][:, 2:4])
                ve.tensor_mul(tmp4[p][:, 0:1], mv[p][:, 0:1], mv[p][:, 0:1])
                ve.tensor_add(pq8[p][:, 0:1], pq8[p][:, 5:6], pq8[p][:, 6:7])
                ve.tensor_copy(padb[:], gs_sb[:, 0:1])
                ve.scalar_tensor_tensor(tmp4[p][:, 1:2], pq8[p][:, 0:1], 0.25,
                                        tmp4[p][:, 0:1], ALU.mult, ALU.subtract)
                ve.tensor_copy(padb[:], gs_sb[:, 0:1])
                ve.tensor_scalar_add(veps[p][:], tmp4[p][:, 1:2], EPS_LN).then_inc(S["D0"], 1)
                ve.wait_ge(S["A0"], ti + 1)
                ve.reciprocal(rstd[ti % 2][:], stdv[ti % 2][:])
                if ti >= 2:
                    ve.wait_ge(S["sW"], 16 * (ti - 1))
                ve.tensor_copy(padb[:], gs_sb[:, 0:1])
                ve.tensor_scalar(xn_sb[ti % 2], x_sb[ti % 2], mv[ti % 2][:, 0:1],
                                 rstd[ti % 2][:], ALU.subtract, ALU.mult).then_inc(S["V0"], 1)
            for h in range(HPC if LEVEL >= 2 else 0):
                for sc in range(NSC):
                    Sg = h * NSC + sc
                    ve.wait_ge(S["sT"], 16 * 18 * (Sg + 1))
                    if sc == 0 and h >= 1:
                        ve.wait_ge(S["mV"], NT_H * h)
                    ve.wait_ge(S["aQ"], 4 * Sg + 1)
                    ve.wait_ge(S["mQ"], 6 * Sg + 4)
                    ve.tensor_mul(t1, q_sb, cs_buf[Sg % 2])
                    ve.tensor_mul(t2, P[3][:], sn_buf[Sg % 2])
                    ve.tensor_add(q_bf[:, sc * 512:(sc + 1) * 512], t1, t2).then_inc(S["vR"], 1)
                    ve.wait_ge(S["aQ"], 4 * Sg + 2)
                    ve.wait_ge(S["mQ"], 6 * Sg + 5)
                    ve.tensor_mul(t1, k_sb, cs_buf[Sg % 2])
                    ve.tensor_mul(t2, P[4][:], sn_buf[Sg % 2])
                    ve.tensor_add(k_bf[:, sc * 512:(sc + 1) * 512], t1, t2).then_inc(S["vR"], 1)
                T0 = NT_H * h
                for t, (qc, jb) in enumerate(ATT_TILES):
                    T = T0 + t
                    buf = T % 2
                    ve.wait_ge(S["aS"], T + 1)
                    if T >= 2:
                        ve.wait_ge(S["mV"], T - 1)
                    if jb >= 4 * qc:
                        r = jb - 4 * qc
                        ve.tensor_mul(s2[buf], s2[buf], masks[:, r * 512:(r + 1) * 512])
                    ve.tensor_mul(s4[buf], s2[buf], s2[buf]).then_inc(S["vS"], 1)
                    if jb == 4 * (qc + 1) - 1:
                        gq = h * NQC + qc
                        ve.wait_ge(S["mV"], T + 1)
                        if gq >= 2:
                            ve.wait_ge(S["sA"], 16 * (gq - 1))
                        ve.tensor_scalar_max(recip, P[1][:], EPS_AT)
                        ve.reciprocal(recip, recip)
                        ve.tensor_mul(e_out[gq % 2], P[2][:], recip).then_inc(S["vF"], 1)
            for ic in range(NIC if LEVEL >= 5 else 0):
                for tc in range(NTC):
                    k = ic * NTC + tc
                    pu = P[1] if k % 2 == 0 else P[3]
                    ve.wait_ge(S["mA"], 2 * k + 2)
                    ve.wait_ge(S["aA"], k + 1)
                    if k >= 2:
                        ve.wait_ge(S["sFu"], 16 * (k - 1))
                    ve.scalar_tensor_tensor(fuse_sb[k % 2], pu[:], us_sb[:, ic:ic + 1],
                                            g_s[k % 2], ALU.mult, ALU.mult).then_inc(S["vA"], 1)
            for tt in range(16 if (LEVEL >= 7 and COMBPART >= 2) else 0):
                ve.wait_ge(S["sX"], 32 * (tt + 1))
                if tt >= 2 and COMBPART >= 3:
                    ve.wait_ge(S["ST"], 16 * (tt - 1))
                ve.tensor_add(ycomb[tt % 2], xcomb[tt % 2], rcomb[tt % 2]).then_inc(S["vC"], 1)
                for fcg in range(4):
                    gi = 4 * tt + fcg
                    ve.wait_ge(S["aC"], gi + 1)
                    ve.tensor_add(ycomb[tt % 2][:, fcg * 512:(fcg + 1) * 512],
                                  ycomb[tt % 2][:, fcg * 512:(fcg + 1) * 512],
                                  ftm[gi % 2]).then_inc(S["vC"], 1)

    return nc



def _layout_inputs(inputs):
    x = np.ascontiguousarray(np.asarray(inputs["L0_attn_x"], dtype=np.float32))
    Wq = np.asarray(inputs["L0_attn_Wq"], dtype=np.float32)
    Wk = np.asarray(inputs["L0_attn_Wk"], dtype=np.float32)
    Wv = np.asarray(inputs["L0_attn_Wv"], dtype=np.float32)
    Wo = np.asarray(inputs["L0_attn_Wo"], dtype=np.float32)
    gwf = np.asarray(inputs["L0_ffn_gate_w"], dtype=np.float32)
    uwf = np.asarray(inputs["L0_ffn_up_w"], dtype=np.float32)
    dwf = np.asarray(inputs["L0_ffn_down_w"], dtype=np.float32)
    gs = np.asarray(inputs["L0_ffn_gate_scale"], dtype=np.float32)
    us = np.asarray(inputs["L0_ffn_up_scale"], dtype=np.float32)
    ds = np.asarray(inputs["L0_ffn_down_scale"], dtype=np.float32)

    gw_t = np.ascontiguousarray(gwf.reshape(16, 128, 48, 128).transpose(2, 1, 0, 3))
    uw_t = np.ascontiguousarray(uwf.reshape(16, 128, 48, 128).transpose(2, 1, 0, 3))
    dw_t = np.ascontiguousarray(dwf.reshape(48, 128, 16, 128))
    gs_t = np.ascontiguousarray(gs.reshape(48, 128).T)
    us_t = np.ascontiguousarray(us.reshape(48, 128).T)
    ds_t = np.ascontiguousarray(ds.reshape(16, 128).T)

    half = 64
    theta = np.power(np.float32(10000.0),
                     -(np.arange(half, dtype=np.float32) * np.float32(2.0) / np.float32(128)))
    pos = np.arange(N, dtype=np.float32)
    ang = (theta[:, None] * pos[None, :]).astype(np.float32)
    cos = np.cos(ang).astype(np.float32)
    sin = np.sin(ang).astype(np.float32)
    cst = np.ascontiguousarray(np.vstack([cos, cos]))
    snt = np.ascontiguousarray(np.vstack([-sin, sin]))

    ones = np.ones((128, 128), dtype=ml_dtypes.bfloat16)
    eyeb = np.eye(128, dtype=ml_dtypes.bfloat16)
    eyef = np.eye(128, dtype=np.float32)
    swp = np.zeros((128, 128), dtype=np.float32)
    swp[(np.arange(128) + 64) % 128, np.arange(128)] = 1.0
    msk = np.zeros((4, 128, 512), dtype=ml_dtypes.bfloat16)
    jj = np.arange(128)[:, None]
    qq = np.arange(512)[None, :]
    for r in range(4):
        msk[r] = ((jj + 128 * r) <= qq).astype(ml_dtypes.bfloat16)

    in_maps = []
    for c in range(NCORE_G):
        b = c // 4
        l = c % 4
        wq_c = np.ascontiguousarray(
            Wq[512 * l:512 * (l + 1), :].T.reshape(16, 128, 4, 128).transpose(2, 1, 0, 3))
        wk_c = np.ascontiguousarray(
            Wk[512 * l:512 * (l + 1), :].T.reshape(16, 128, 4, 128).transpose(2, 1, 0, 3))
        wv_c = np.ascontiguousarray(
            Wv[512 * l:512 * (l + 1), :].T.reshape(16, 128, 4, 128).transpose(2, 1, 0, 3))
        wo_c = np.ascontiguousarray(
            Wo.T[512 * l:512 * (l + 1), :].reshape(4, 128, 2048).transpose(1, 0, 2))
        in_maps.append({
            "xb": x[b], "x_own": np.ascontiguousarray(x[b, 2048 * l:2048 * (l + 1)]),
            "wq": wq_c, "wk": wk_c, "wv": wv_c, "wo": wo_c,
            "gw": gw_t, "uw": uw_t, "dwt": dw_t,
            "gsc": gs_t, "usc": us_t, "dsc": ds_t,
            "cst": cst, "snt": snt,
            "onesd": ones, "eyebd": eyeb, "eyefd": eyef, "swpd": swp,
            "maskd": msk,
        })
    return in_maps


NCORE_G = 8


def kernel(**inputs):
    in_maps = _layout_inputs(inputs)
    nc = build()
    res = run_bass_kernel_spmd(nc, in_maps, core_ids=list(range(NCORE_G)))
    y = np.empty((B, N, DM), dtype=np.float32)
    for c in range(NCORE_G):
        b = c // 4
        l = c % 4
        y[b, 2048 * l:2048 * (l + 1), :] = res.results[c]["y_own"]
    return y


# version 02
# speedup vs baseline: 1.4242x; 1.4242x over previous
import numpy as np
import ml_dtypes
import concourse.bass as bass
import concourse.mybir as mybir
from concourse.bass_utils import run_bass_kernel_spmd

F32 = mybir.dt.float32
BF16 = mybir.dt.bfloat16
FP8 = mybir.dt.float8e4
AF = mybir.ActivationFunctionType
ALU = mybir.AluOpType
AX = mybir.AxisListType

B, N, DM = 2, 8192, 2048
DI = 6144
HPC = 4
OWN = 2048
EPS_LN = 1e-5
EPS_AT = 1e-6

NLT = N // 128
NLA = NLT + OWN // 128
NSC = N // 512
NQC = N // 512
NIC = DI // 128
NTC = OWN // 512
NWTC = N // 512
NEC = DM // 128

ATT_TILES = [(qc, jb) for qc in range(NQC) for jb in range(4 * (qc + 1))]
NT_H = len(ATT_TILES)
FB_PASSES = [(0, 6), (6, 12), (12, 16)]

DEBUG = ()
LEVEL = 7
WOPART = 3
SKIPRS = False
COMBPART = 3


def build():
    nc = bass.Bass("TRN2", target_bir_lowering=False, debug=False)
    xb = nc.dram_tensor("xb", [N, DM], F32, kind="ExternalInput").ap()
    x_own = nc.dram_tensor("x_own", [OWN, DM], F32, kind="ExternalInput").ap()
    wq = nc.dram_tensor("wq", [HPC, 128, NEC, 128], F32, kind="ExternalInput").ap()
    wk = nc.dram_tensor("wk", [HPC, 128, NEC, 128], F32, kind="ExternalInput").ap()
    wv = nc.dram_tensor("wv", [HPC, 128, NEC, 128], F32, kind="ExternalInput").ap()
    wo = nc.dram_tensor("wo", [128, HPC, DM], F32, kind="ExternalInput").ap()
    gw = nc.dram_tensor("gw", [NIC, 128, NEC, 128], F32, kind="ExternalInput").ap()
    uw = nc.dram_tensor("uw", [NIC, 128, NEC, 128], F32, kind="ExternalInput").ap()
    dwt = nc.dram_tensor("dwt", [NIC, 128, NEC, 128], F32, kind="ExternalInput").ap()
    gsc = nc.dram_tensor("gsc", [128, NIC], F32, kind="ExternalInput").ap()
    usc = nc.dram_tensor("usc", [128, NIC], F32, kind="ExternalInput").ap()
    dsc = nc.dram_tensor("dsc", [128, NEC], F32, kind="ExternalInput").ap()
    cst = nc.dram_tensor("cst", [128, N], F32, kind="ExternalInput").ap()
    snt = nc.dram_tensor("snt", [128, N], F32, kind="ExternalInput").ap()
    onesd = nc.dram_tensor("onesd", [128, 128], BF16, kind="ExternalInput").ap()
    eyebd = nc.dram_tensor("eyebd", [128, 128], BF16, kind="ExternalInput").ap()
    eyefd = nc.dram_tensor("eyefd", [128, 128], F32, kind="ExternalInput").ap()
    swpd = nc.dram_tensor("swpd", [128, 128], F32, kind="ExternalInput").ap()
    maskd = nc.dram_tensor("maskd", [4, 128, 512], BF16, kind="ExternalInput").ap()

    y_own = nc.dram_tensor("y_own", [OWN, DM], F32, kind="ExternalOutput").ap()

    xn_sc = nc.dram_tensor("xn_sc", [N, DM], BF16, kind="Internal").ap()
    xno_sc = nc.dram_tensor("xno_sc", [OWN, DM], BF16, kind="Internal").ap()
    od_sc = nc.dram_tensor("od_sc", [HPC, 128, N], BF16, kind="Internal").ap()
    fuse_sc = nc.dram_tensor("fuse_sc", [NIC, 128, OWN], BF16, kind="Internal").ap()
    dw8_sc = nc.dram_tensor("dw8_sc", [NIC, 128, DM], FP8, kind="Internal").ap()
    gw8_sc = nc.dram_tensor("gw8_sc", [NIC, 128, DM], FP8, kind="Internal").ap()
    uw8_sc = nc.dram_tensor("uw8_sc", [NIC, 128, DM], FP8, kind="Internal").ap()
    rs_in_t = nc.dram_tensor("rs_in", [N, DM], BF16, kind="Internal")
    rs_out_t = nc.dram_tensor("rs_out", [OWN, DM], BF16, kind="Internal")
    rs_in = rs_in_t.ap()
    rs_out = rs_out_t.ap()

    dbg = {}
    if "xn" in DEBUG:
        dbg["d_xn"] = nc.dram_tensor("d_xn", [N, DM], BF16, kind="ExternalOutput").ap()
    if "od" in DEBUG:
        dbg["d_od"] = nc.dram_tensor("d_od", [HPC * 128, N], BF16, kind="ExternalOutput").ap()
    if "rsin" in DEBUG:
        dbg["d_rsin"] = nc.dram_tensor("d_rsin", [N, DM], BF16, kind="ExternalOutput").ap()
    if "rsout" in DEBUG:
        dbg["d_rsout"] = nc.dram_tensor("d_rsout", [OWN, DM], BF16, kind="ExternalOutput").ap()
    if "fuse" in DEBUG:
        dbg["d_fuse"] = nc.dram_tensor("d_fuse", [NIC * 128, OWN], BF16, kind="ExternalOutput").ap()
    NDBG = len(dbg)

    from contextlib import ExitStack
    with ExitStack() as stack:
        en = stack.enter_context

        A = en(nc.sbuf_tensor("A", [128, 32768], BF16))
        Af = A.bitcast(F32)
        q_bf = A[:, 0:8192]
        k_bf = A[:, 8192:16384]
        v_tm = A[:, 16384:24576]
        xnoT = [A[:, ec * 2048:(ec + 1) * 2048] for ec in range(NEC)]
        q_sb = Af[:, 12288:12800]
        k_sb = Af[:, 12800:13312]
        t1 = Af[:, 13312:13824]
        t2 = Af[:, 13824:14336]
        v_tmp = [A[:, 28672 + i * 512:28672 + (i + 1) * 512] for i in range(2)]
        s2 = [Af[:, 12288:12800], Af[:, 12800:13312]]
        s4 = [A[:, 26624 + i * 512:26624 + (i + 1) * 512] for i in range(2)]
        recip = Af[:, 14848:15360]
        e_out = [A[:, 30720 + i * 512:30720 + (i + 1) * 512] for i in range(2)]

        Bx = en(nc.sbuf_tensor("Bx", [128, 32768], BF16))
        Bf = Bx.bitcast(F32)
        xnt = [[Bx[:, bi * 8192 + ec * 512: bi * 8192 + (ec + 1) * 512]
                for ec in range(NEC)] for bi in range(2)]
        cs_buf = [Bf[:, 8192 + i * 512:8192 + (i + 1) * 512] for i in range(2)]
        sn_buf = [Bf[:, 9216 + i * 512:9216 + (i + 1) * 512] for i in range(2)]
        wq_sb = [Bx[:, 20480 + bi * 6144:20480 + bi * 6144 + 2048] for bi in range(2)]
        wk_sb = [Bx[:, 22528 + bi * 6144:22528 + bi * 6144 + 2048] for bi in range(2)]
        wv_sb = [Bx[:, 24576 + bi * 6144:24576 + bi * 6144 + 2048] for bi in range(2)]
        attn_T = [Bf[:, fc * 512:(fc + 1) * 512] for fc in range(16)]
        stage = [Bx[:, 16384 + i * 2048:16384 + (i + 1) * 2048] for i in range(2)]
        wo_sb = Bx[:, 24576:32768]
        ffn_T = [Bx[:, hc * 2048:(hc + 1) * 2048] for hc in range(NEC)]

        C = en(nc.sbuf_tensor("C", [128, 24576], BF16))
        Cf = C.bitcast(F32)
        x_sb = [Cf[:, i * 2048:(i + 1) * 2048] for i in range(2)]
        xn_sb = [C[:, 8192 + i * 2048:8192 + (i + 1) * 2048] for i in range(2)]
        od_buf = [C[:, i * 2048:(i + 1) * 2048] for i in range(2)]
        g_s = [Cf[:, 1024 + i * 512:1024 + (i + 1) * 512] for i in range(2)]
        fuse_sb = [C[:, 4096 + i * 512:4096 + (i + 1) * 512] for i in range(2)]
        fuse_tc = [C[:, ic * 512:(ic + 1) * 512] for ic in range(NIC)]
        xcomb = [Cf[:, i * 2048:(i + 1) * 2048] for i in range(2)]
        rcomb = [C[:, 8192 + i * 2048 + 0:8192 + i * 2048 + 2048] for i in range(2)]
        ycomb = [Cf[:, 8192 + i * 2048:8192 + (i + 1) * 2048] for i in range(2)]
        ftm = [C[:, 12288 + i * 512:12288 + (i + 1) * 512] for i in range(2)]

        masks = en(nc.sbuf_tensor("masks", [128, 2048], BF16))
        gu8t = en(nc.sbuf_tensor("gu8t", [128, 8192], FP8))
        gu8 = [[gu8t[:, (bi * 2 + j) * 2048:(bi * 2 + j + 1) * 2048] for j in range(2)]
               for bi in range(2)]
        dw8t = en(nc.sbuf_tensor("dw8t", [128, 6144], FP8))
        dw8b = [dw8t[:, i * 3072:(i + 1) * 3072] for i in range(2)]
        dw8p = en(nc.sbuf_tensor("dw8p", [128, 2048], FP8))
        pp8 = [en(nc.sbuf_tensor(f"pp8_{i}", [128, 2048], FP8)) for i in range(2)]
        ones_s = en(nc.sbuf_tensor("ones_s", [128, 128], BF16))
        eyeb_s = en(nc.sbuf_tensor("eyeb_s", [128, 128], BF16))
        eyef_s = en(nc.sbuf_tensor("eyef_s", [128, 128], F32))
        swp_s = en(nc.sbuf_tensor("swp_s", [128, 128], F32))
        gs_sb = en(nc.sbuf_tensor("gs_sb", [128, NIC], F32))
        us_sb = en(nc.sbuf_tensor("us_sb", [128, NIC], F32))
        ds_sb = en(nc.sbuf_tensor("ds_sb", [128, NEC], F32))
        stat6 = [en(nc.sbuf_tensor(f"stat6_{i}", [128, 24], F32)) for i in range(2)]
        mvc = [en(nc.sbuf_tensor(f"mvc_{i}", [128, 8], F32)) for i in range(2)]
        tmp4 = [en(nc.sbuf_tensor(f"tmp4_{i}", [128, 4], F32)) for i in range(2)]
        pq8 = [en(nc.sbuf_tensor(f"pq8_{i}", [128, 8], F32)) for i in range(2)]
        sq8 = [en(nc.sbuf_tensor(f"sq8_{i}", [128, 8], F32)) for i in range(2)]
        padb = en(nc.sbuf_tensor("padb", [128, 1], F32))
        mv = [en(nc.sbuf_tensor(f"mv_{i}", [128, 2], F32)) for i in range(2)]
        veps = [en(nc.sbuf_tensor(f"veps_{i}", [128, 1], F32)) for i in range(2)]
        stdv = [en(nc.sbuf_tensor(f"stdv_{i}", [128, 1], F32)) for i in range(2)]
        rstd = [en(nc.sbuf_tensor(f"rstd_{i}", [128, 1], F32)) for i in range(2)]

        P = [en(nc.psum_tensor(f"P{i}", [128, 512], F32)) for i in range(7)]
        PTt = en(nc.psum_tensor("PTt", [128, 1024], BF16))
        PT = [PTt[:, 0:512], PTt[:, 512:1024]]

        names = ["sL", "sW", "V0", "D0", "A0", "sT", "gQ", "mQ", "aQ", "vR",
                 "mS", "aS", "vS", "mV", "vF", "sA", "gWo", "gO", "mW", "aW",
                 "mT", "aG", "sR", "CC", "gD", "gF", "sTF", "mA", "aA", "vA",
                 "sFu", "sFB", "gB", "mBc", "aB", "sX", "mC", "vC",
                 "ST", "gC", "aC"]
        S = {nm: en(nc.semaphore(nm)) for nm in names}
        block = en(nc.Block())

        def ln_src(ti):
            if ti < NLT:
                return xb[ti * 128:(ti + 1) * 128, :]
            t = ti - NLT
            return x_own[t * 128:(t + 1) * 128, :]

        def ln_dst(ti):
            if ti < NLT:
                return xn_sc[ti * 128:(ti + 1) * 128, :]
            t = ti - NLT
            return xno_sc[t * 128:(t + 1) * 128, :]

        @block.sync
        def _(sync):
            sync.dma_start(out=x_sb[0], in_=ln_src(0)).then_inc(S["sL"], 16)
            sync.dma_start(out=x_sb[1], in_=ln_src(1)).then_inc(S["sL"], 16)
            for ti in range(NLA):
                sync.wait_ge(S["V0"], ti + 1)
                sync.dma_start(out=ln_dst(ti), in_=xn_sb[ti % 2]).then_inc(S["sW"], 16)
                if ti + 2 < NLA:
                    sync.dma_start(out=x_sb[ti % 2], in_=ln_src(ti + 2)).then_inc(S["sL"], 16)
            for h in range(HPC if LEVEL >= 2 else 0):
                for sc in range(NSC):
                    Sg = h * NSC + sc
                    sync.wait_ge(S["sW"], 16 * min(4 * sc + 4, NLT))
                    if Sg >= 2:
                        sync.wait_ge(S["mQ"], 6 * (Sg - 2) + 3)
                        sync.wait_ge(S["vR"], 2 * (Sg - 1))
                    for ec in range(NEC):
                        sync.dma_start(
                            out=xnt[Sg % 2][ec],
                            in_=xn_sc[sc * 512:(sc + 1) * 512, ec * 128:(ec + 1) * 128],
                            transpose=True).then_inc(S["sT"], 16)
                    sync.dma_start(out=cs_buf[Sg % 2],
                                   in_=cst[:, sc * 512:(sc + 1) * 512]).then_inc(S["sT"], 16)
                    sync.dma_start(out=sn_buf[Sg % 2],
                                   in_=snt[:, sc * 512:(sc + 1) * 512]).then_inc(S["sT"], 16)
                for qc in range(NQC):
                    g = h * NQC + qc
                    sync.wait_ge(S["vF"], g + 1)
                    sync.dma_start(out=od_sc[h, :, qc * 512:(qc + 1) * 512],
                                   in_=e_out[g % 2]).then_inc(S["sA"], 16)
            if LEVEL < 2:
                sync.wait_ge(S["sW"], 16 * NLA)
            if LEVEL >= 5:
                sync.wait_ge(S["vF"], HPC * NQC)
            if LEVEL >= 5:
                sync.wait_ge(S["sA"], 16 * HPC * NQC)
                sync.wait_ge(S["sW"], 16 * NLA)
                for ec in range(NEC):
                    sync.dma_start(out=xnoT[ec], in_=xno_sc[:, ec * 128:(ec + 1) * 128],
                                   transpose=True).then_inc(S["sTF"], 16)
            for tc in range(NWTC if (LEVEL >= 3 and WOPART >= 3) else 0):
                for tt in range(4):
                    TT = 4 * tc + tt
                    sync.wait_ge(S["aG"], 4 * TT + 4)
                    sync.dma_start(out=rs_in[TT * 128:(TT + 1) * 128, :],
                                   in_=stage[TT % 2]).then_inc(S["sR"], 16)
            if LEVEL >= 5:
                sync.wait_ge(S["gD"], 32 * 9 * NWTC)
                if LEVEL >= 4 and not SKIPRS:
                    sync.wait_ge(S["CC"], 1)
                for ic0 in (0, 1):
                    sync.dma_start(out=gu8[ic0 % 2][0], in_=gw8_sc[ic0]).then_inc(S["gF"], 16)
                    sync.dma_start(out=gu8[ic0 % 2][1], in_=uw8_sc[ic0]).then_inc(S["gF"], 16)
            for ic in range(NIC if LEVEL >= 5 else 0):
                for tc in range(NTC):
                    k = ic * NTC + tc
                    sync.wait_ge(S["vA"], k + 1)
                    sync.dma_start(out=fuse_sc[ic, :, tc * 512:(tc + 1) * 512],
                                   in_=fuse_sb[k % 2]).then_inc(S["sFu"], 16)
                if ic + 2 < NIC:
                    sync.dma_start(out=gu8[ic % 2][0], in_=gw8_sc[ic + 2]).then_inc(S["gF"], 16)
                    sync.dma_start(out=gu8[ic % 2][1], in_=uw8_sc[ic + 2]).then_inc(S["gF"], 16)
            if LEVEL >= 6:
                sync.wait_ge(S["sFu"], 16 * NIC * NTC)
            for tc in range(NTC if LEVEL >= 6 else 0):
                if tc > 0:
                    sync.wait_ge(S["mBc"], 36 * tc)
                for cg in range(6):
                    ic0 = cg * 8
                    sync.dma_start(
                        out=C[:, ic0 * 512:(ic0 + 8) * 512].rearrange("p (i t) -> p i t", t=512),
                        in_=fuse_sc[ic0:ic0 + 8, :, tc * 512:(tc + 1) * 512].rearrange("i p t -> p i t"),
                    ).then_inc(S["sFB"], 16)
            if LEVEL >= 7:
                sync.wait_ge(S["mBc"], 36 * NTC)
                sync.wait_ge(S["CC"], 1)
            for tt in ((0, 1) if (LEVEL >= 7 and COMBPART >= 2) else ()):
                sync.dma_start(out=xcomb[tt], in_=x_own[tt * 128:(tt + 1) * 128, :]).then_inc(S["sX"], 16)
                sync.dma_start(out=rcomb[tt], in_=rs_out[tt * 128:(tt + 1) * 128, :]).then_inc(S["sX"], 16)
            for tt in range(16 if (LEVEL >= 7 and COMBPART >= 3) else 0):
                sync.wait_ge(S["vC"], 5 * (tt + 1))
                sync.dma_start(out=y_own[tt * 128:(tt + 1) * 128, :],
                               in_=ycomb[tt % 2]).then_inc(S["ST"], 16)
                if tt + 2 < 16:
                    sync.dma_start(out=xcomb[tt % 2],
                                   in_=x_own[(tt + 2) * 128:(tt + 3) * 128, :]).then_inc(S["sX"], 16)
                    sync.dma_start(out=rcomb[tt % 2],
                                   in_=rs_out[(tt + 2) * 128:(tt + 3) * 128, :]).then_inc(S["sX"], 16)
            nst = 16 if (LEVEL >= 7 and COMBPART >= 3) else 0
            if LEVEL >= 7 and COMBPART == 0:
                sync.wait_ge(S["mC"], 64)
            elif LEVEL >= 7 and COMBPART == 1:
                sync.wait_ge(S["aC"], 64)
            elif LEVEL >= 7 and COMBPART == 2:
                sync.wait_ge(S["vC"], 80)
            if LEVEL < 7:
                if LEVEL == 1:
                    sync.wait_ge(S["sW"], 16 * NLA)
                elif LEVEL == 2:
                    sync.wait_ge(S["sA"], 16 * HPC * NQC)
                elif LEVEL == 3:
                    if WOPART >= 3:
                        sync.wait_ge(S["sR"], 16 * NWTC * 4)
                    elif WOPART == 2:
                        sync.wait_ge(S["mT"], 256)
                    else:
                        sync.wait_ge(S["aW"], 256)
                elif LEVEL == 4:
                    sync.wait_ge(S["CC"], 1)
                elif LEVEL == 5:
                    sync.wait_ge(S["sFu"], 16 * NIC * NTC)
                elif LEVEL == 6:
                    sync.wait_ge(S["aB"], 3 * NTC)
            if "xn" in DEBUG:
                sync.dma_start(out=dbg["d_xn"], in_=xn_sc).then_inc(S["ST"], 16)
                nst += 1
            if "od" in DEBUG:
                sync.dma_start(out=dbg["d_od"], in_=od_sc.rearrange("h p t -> (h p) t")).then_inc(S["ST"], 16)
                nst += 1
            if "rsin" in DEBUG:
                sync.dma_start(out=dbg["d_rsin"], in_=rs_in).then_inc(S["ST"], 16)
                nst += 1
            if "rsout" in DEBUG:
                sync.dma_start(out=dbg["d_rsout"], in_=rs_out).then_inc(S["ST"], 16)
                nst += 1
            if "fuse" in DEBUG:
                sync.dma_start(out=dbg["d_fuse"], in_=fuse_sc.rearrange("i p t -> (i p) t")).then_inc(S["ST"], 16)
                nst += 1
            sync.wait_ge(S["ST"], 16 * nst)

        @block.gpsimd
        def _(g):
            g.dma_start(out=ones_s[:], in_=onesd).then_inc(S["gC"], 16)
            g.dma_start(out=eyeb_s[:], in_=eyebd).then_inc(S["gC"], 16)
            g.dma_start(out=eyef_s[:], in_=eyefd).then_inc(S["gC"], 16)
            g.dma_start(out=swp_s[:], in_=swpd).then_inc(S["gC"], 16)
            g.dma_start(out=masks[:].rearrange("p (r q) -> p r q", q=512),
                        in_=maskd.rearrange("r p q -> p r q")).then_inc(S["gC"], 16)
            g.dma_start(out=gs_sb[:], in_=gsc).then_inc(S["gC"], 16)
            g.dma_start(out=us_sb[:], in_=usc).then_inc(S["gC"], 16)
            g.dma_start(out=ds_sb[:], in_=dsc).then_inc(S["gC"], 16)
            for h in range(HPC if LEVEL >= 2 else 0):
                if h >= 2:
                    g.wait_ge(S["mQ"], 6 * NSC * (h - 1))
                for wsrc, wdst in [(wq, wq_sb), (wk, wk_sb), (wv, wv_sb)]:
                    g.dma_start(out=wdst[h % 2].rearrange("p (ec d) -> p ec d", d=128),
                                in_=wsrc[h]).then_inc(S["gQ"], 16)
            if LEVEL >= 3:
                g.wait_ge(S["vF"], HPC * NQC)
                g.dma_start(out=wo_sb.rearrange("p (dh f) -> p dh f", f=DM),
                            in_=wo).then_inc(S["gWo"], 16)
            for tc in range(NWTC if LEVEL >= 3 else 0):
                if tc == 0:
                    g.wait_ge(S["sA"], 16 * HPC * NQC)
                if tc >= 2:
                    g.wait_ge(S["mW"], 16 * (tc - 1))
                g.dma_start(out=od_buf[tc % 2].rearrange("p (dh t) -> p dh t", t=512),
                            in_=od_sc[:, :, tc * 512:(tc + 1) * 512].rearrange("h p t -> p h t"),
                            ).then_inc(S["gO"], 16)
                for j in range(9 if LEVEL >= 5 else 0):
                    p = tc * 9 + j
                    ic = p // 3
                    srcw, dstw = ((dwt, dw8_sc), (gw, gw8_sc), (uw, uw8_sc))[p % 3]
                    if p >= 2:
                        g.wait_ge(S["gD"], 32 * (p - 1))
                    g.dma_start(out=pp8[p % 2][:],
                                in_=srcw[ic].rearrange("p a b -> p (a b)")).then_inc(S["gD"], 16)
                    g.wait_ge(S["gD"], 32 * p + 16)
                    g.dma_start(out=dstw[ic], in_=pp8[p % 2][:]).then_inc(S["gD"], 16)

            if LEVEL >= 4 and not SKIPRS:
                g.wait_ge(S["sR"], 16 * NWTC * 4)
                if LEVEL >= 5:
                    g.wait_ge(S["gD"], 32 * 9 * NWTC)
                g.collective_compute(
                    "ReduceScatter", ALU.add,
                    replica_groups=[[0, 1, 2, 3], [4, 5, 6, 7]],
                    ins=[rs_in_t.ap().opt()], outs=[rs_out_t.ap().opt()],
                ).then_inc(S["CC"], 1)
            if LEVEL >= 6:
                g.wait_ge(S["gD"], 32 * 9 * NWTC)
                g.wait_ge(S["CC"], 1)
            for tc in range(NTC if LEVEL >= 6 else 0):
                for pi, (h0, h1) in enumerate(FB_PASSES):
                    gidx = tc * 3 + pi
                    w = (h1 - h0) * 128
                    for cg in range(12):
                        CH = gidx * 12 + cg
                        if CH >= 2:
                            g.wait_ge(S["mBc"], CH - 1)
                        g.dma_start(
                            out=dw8b[CH % 2][:, 0:4 * w].rearrange("p (i t) -> p i t", t=w),
                            in_=dw8_sc[cg * 4:cg * 4 + 4, :, h0 * 128:h1 * 128].rearrange("i p t -> p i t"),
                        ).then_inc(S["gB"], 16)

        @block.tensor
        def _(te):
            te.wait_ge(S["gC"], 16 * 8)
            for h in range(HPC if LEVEL >= 2 else 0):
                hb = h % 2
                for sc in range(NSC):
                    Sg = h * NSC + sc
                    te.wait_ge(S["sT"], 16 * 18 * (Sg + 1))
                    te.wait_ge(S["gQ"], 16 * 3 * (h + 1))
                    if h >= 1 and sc == 0:
                        te.wait_ge(S["vF"], NQC * h)
                    if Sg >= 1:
                        te.wait_ge(S["aQ"], 4 * (Sg - 1) + 3)
                    for wsb, pb in ((wq_sb, P[0]), (wk_sb, P[1]), (wv_sb, P[2])):
                        for ec in range(NEC):
                            ins = te.matmul(pb[:], wsb[hb][:, ec * 128:(ec + 1) * 128],
                                            xnt[Sg % 2][ec], start=(ec == 0), stop=(ec == NEC - 1))
                        ins.then_inc(S["mQ"], 1)
                    te.wait_ge(S["aQ"], 4 * Sg + 1)
                    if Sg >= 1:
                        te.wait_ge(S["vR"], 2 * (Sg - 1) + 1)
                    te.matmul(P[3][:], swp_s[:], q_sb, start=True, stop=True).then_inc(S["mQ"], 1)
                    te.wait_ge(S["aQ"], 4 * Sg + 2)
                    if Sg >= 1:
                        te.wait_ge(S["vR"], 2 * Sg)
                    te.matmul(P[4][:], swp_s[:], k_sb, start=True, stop=True).then_inc(S["mQ"], 1)
                    te.wait_ge(S["aQ"], 4 * Sg + 3)
                    if Sg >= 1:
                        te.wait_ge(S["aQ"], 4 * (Sg - 1) + 4)
                    for j in range(4):
                        ins = te.transpose(PT[0][:, j * 128:(j + 1) * 128],
                                           v_tmp[Sg % 2][:, j * 128:(j + 1) * 128],
                                           eyeb_s[:])
                    ins.then_inc(S["mQ"], 1)
                T0 = NT_H * h
                def emit_st(t, h=h):
                    T = T0 + t
                    qc, jb = ATT_TILES[t]
                    if T >= 2:
                        te.wait_ge(S["aS"], T - 1)
                    te.wait_ge(S["vR"], 2 * NSC * h + 2 * (qc + 1))
                    te.matmul(P[5 + T % 2][:], k_bf[:, jb * 128:(jb + 1) * 128],
                              q_bf[:, qc * 512:(qc + 1) * 512],
                              start=True, stop=True).then_inc(S["mS"], 1)
                emit_st(0)
                for t, (qc, jb) in enumerate(ATT_TILES):
                    T = T0 + t
                    buf = T % 2
                    if t + 1 < NT_H:
                        emit_st(t + 1)
                    te.wait_ge(S["vS"], T + 1)
                    first = (jb == 0)
                    last = (jb == 4 * (qc + 1) - 1)
                    if first:
                        gq = h * NQC + qc
                        if gq > 0:
                            te.wait_ge(S["vF"], gq)
                        if qc == 0:
                            te.wait_ge(S["aQ"], 64 * h + 63)
                    if jb % 4 == 0:
                        te.wait_ge(S["aQ"], 64 * h + 4 * (jb // 4 + 1))
                    te.matmul(P[2][:], v_tm[:, jb * 128:(jb + 1) * 128], s4[buf],
                              start=first, stop=last)
                    te.matmul(P[1][:], ones_s[:], s4[buf],
                              start=first, stop=last).then_inc(S["mV"], 1)
            if LEVEL >= 3:
                te.wait_ge(S["vF"], HPC * NQC)
                te.wait_ge(S["gWo"], 16)
            for tc in range(NWTC if LEVEL >= 3 else 0):
                te.wait_ge(S["gO"], 16 * (tc + 1))
                for fc in range(16):
                    idx = 16 * tc + fc
                    bank = P[fc % 2]
                    if idx >= 2:
                        te.wait_ge(S["aW"], idx - 1)
                    for dh in range(HPC):
                        ins = te.matmul(bank[:],
                                        wo_sb[:, dh * 2048 + fc * 128: dh * 2048 + fc * 128 + 128],
                                        od_buf[tc % 2][:, dh * 512:(dh + 1) * 512],
                                        start=(dh == 0), stop=(dh == HPC - 1))
                    ins.then_inc(S["mW"], 1)
                if WOPART < 2:
                    continue
                te.wait_ge(S["aW"], 16 * tc + 16)
                for tt in range(4):
                    for fcg in range(4):
                        gi = 16 * tc + 4 * tt + fcg
                        if gi >= 1 and WOPART >= 3:
                            te.wait_ge(S["aG"], gi)
                        for j in range(4):
                            fc = 4 * fcg + j
                            ins = te.transpose(P[2][:, j * 128:(j + 1) * 128],
                                               attn_T[fc][:, tt * 128:(tt + 1) * 128],
                                               eyef_s[:])
                        ins.then_inc(S["mT"], 1)
            if LEVEL >= 5:
                te.wait_ge(S["sTF"], 16 * NEC)
                if LEVEL >= 3:
                    te.wait_ge(S["aW"], 256)
                    te.wait_ge(S["aG"], 256)
            for ic in range(NIC if LEVEL >= 5 else 0):
                te.wait_ge(S["gF"], 32 * (ic + 1))
                for tc in range(NTC):
                    k = ic * NTC + tc
                    pg, pu = (P[0], P[1]) if k % 2 == 0 else (P[2], P[3])
                    if k >= 2:
                        te.wait_ge(S["aA"], k - 1)
                        te.wait_ge(S["vA"], k - 1)
                    for gj, pb in ((0, pg), (1, pu)):
                        for ec in range(NEC):
                            ins = te.matmul(pb[:], gu8[ic % 2][gj][:, ec * 128:(ec + 1) * 128],
                                            xnoT[ec][:, tc * 512:(tc + 1) * 512],
                                            start=(ec == 0), stop=(ec == NEC - 1))
                        ins.then_inc(S["mA"], 1)
            for tc in range(NTC if LEVEL >= 6 else 0):
                te.wait_ge(S["sFB"], 16 * 6 * (tc + 1))
                for pi, (h0, h1) in enumerate(FB_PASSES):
                    gidx = tc * 3 + pi
                    nh = h1 - h0
                    w = nh * 128
                    if gidx == 0:
                        te.wait_ge(S["aA"], NIC * NTC)
                        te.wait_ge(S["vA"], NIC * NTC)
                    else:
                        te.wait_ge(S["aB"], gidx)
                    for cg in range(12):
                        CH = gidx * 12 + cg
                        te.wait_ge(S["gB"], 16 * (CH + 1))
                        for j in range(4):
                            ic = cg * 4 + j
                            for hi in range(nh):
                                ins = te.matmul(P[hi][:],
                                                dw8b[CH % 2][:, j * w + hi * 128: j * w + hi * 128 + 128],
                                                fuse_tc[ic],
                                                start=(ic == 0), stop=(ic == NIC - 1))
                        ins.then_inc(S["mBc"], 1)
            if LEVEL >= 7:
                te.wait_ge(S["aB"], 3 * NTC)
            for tt in range(16 if LEVEL >= 7 else 0):
                for fcg in range(4):
                    gi = 4 * tt + fcg
                    pt = PT[0]
                    if gi >= 1 and COMBPART >= 1:
                        te.wait_ge(S["aC"], gi)
                    for j in range(4):
                        hc = fcg * 4 + j
                        ins = te.transpose(pt[:, j * 128:(j + 1) * 128],
                                           ffn_T[hc][:, tt * 128:(tt + 1) * 128],
                                           eyeb_s[:])
                    ins.then_inc(S["mC"], 1)

        @block.scalar
        def _(sc_):
            sc_.wait_ge(S["gC"], 16 * 8)
            for ti in range(NLA):
                sc_.wait_ge(S["D0"], ti + 1)
                if ti >= 2:
                    sc_.wait_ge(S["V0"], ti - 1)
                sc_.activation(stdv[ti % 2][:], veps[ti % 2][:], AF.Sqrt).then_inc(S["A0"], 1)
            for h in range(HPC if LEVEL >= 2 else 0):
                for sc in range(NSC):
                    Sg = h * NSC + sc
                    sc_.wait_ge(S["mQ"], 6 * Sg + 1)
                    if Sg >= 1:
                        sc_.wait_ge(S["vR"], 2 * Sg - 1)
                    sc_.activation(q_sb, P[0][:], AF.Copy).then_inc(S["aQ"], 1)
                    sc_.wait_ge(S["mQ"], 6 * Sg + 2)
                    if Sg >= 1:
                        sc_.wait_ge(S["vR"], 2 * Sg)
                    sc_.activation(k_sb, P[1][:], AF.Copy).then_inc(S["aQ"], 1)
                    sc_.wait_ge(S["mQ"], 6 * Sg + 3)
                    sc_.activation(v_tmp[Sg % 2], P[2][:], AF.Copy).then_inc(S["aQ"], 1)
                    sc_.wait_ge(S["mQ"], 6 * Sg + 6)
                    sc_.activation(v_tm[:, sc * 512:(sc + 1) * 512], PT[0], AF.Copy).then_inc(S["aQ"], 1)
                T0 = NT_H * h
                for t in range(NT_H):
                    T = T0 + t
                    buf = T % 2
                    sc_.wait_ge(S["mS"], T + 1)
                    if t == 0:
                        sc_.wait_ge(S["vR"], 2 * NSC * (h + 1))
                    if T >= 2:
                        sc_.wait_ge(S["vS"], T - 1)
                    sc_.activation(s2[buf], P[5 + buf][:], AF.Square).then_inc(S["aS"], 1)
            for tc in range(NWTC if LEVEL >= 3 else 0):
                for fc in range(16):
                    idx = 16 * tc + fc
                    sc_.wait_ge(S["mW"], idx + 1)
                    if tc >= 1 and fc == 0 and WOPART >= 2:
                        sc_.wait_ge(S["mT"], 16 * tc)
                    sc_.activation(attn_T[fc], P[fc % 2][:], AF.Copy).then_inc(S["aW"], 1)
                for tt in range(4 if WOPART >= 3 else 0):
                    TT = 4 * tc + tt
                    for fcg in range(4):
                        gi = 16 * tc + 4 * tt + fcg
                        sc_.wait_ge(S["mT"], gi + 1)
                        if fcg == 0 and TT >= 2:
                            sc_.wait_ge(S["sR"], 16 * (TT - 1))
                        sc_.activation(stage[TT % 2][:, fcg * 512:(fcg + 1) * 512],
                                       P[2][:], AF.Copy).then_inc(S["aG"], 1)
            for ic in range(NIC if LEVEL >= 5 else 0):
                for tc in range(NTC):
                    k = ic * NTC + tc
                    pg = P[0] if k % 2 == 0 else P[2]
                    sc_.wait_ge(S["mA"], 2 * k + 1)
                    if k >= 2:
                        sc_.wait_ge(S["vA"], k - 1)
                    sc_.activation(g_s[k % 2], pg[:], AF.Silu,
                                   scale=gs_sb[:, ic:ic + 1]).then_inc(S["aA"], 1)
            for tc in range(NTC if LEVEL >= 6 else 0):
                for pi, (h0, h1) in enumerate(FB_PASSES):
                    gidx = tc * 3 + pi
                    sc_.wait_ge(S["mBc"], 12 * (gidx + 1))
                    for hi, hc in enumerate(range(h0, h1)):
                        ins = sc_.activation(ffn_T[hc][:, tc * 512:(tc + 1) * 512], P[hi][:],
                                             AF.Identity, scale=ds_sb[:, hc:hc + 1])
                    ins.then_inc(S["aB"], 1)
            for tt in range(16 if (LEVEL >= 7 and COMBPART >= 1) else 0):
                for fcg in range(4):
                    gi = 4 * tt + fcg
                    sc_.wait_ge(S["mC"], gi + 1)
                    if gi >= 2 and COMBPART >= 2:
                        pg = gi - 2
                        sc_.wait_ge(S["vC"], 5 * (pg // 4) + (pg % 4) + 2)
                    sc_.activation(ftm[gi % 2], PT[0], AF.Copy).then_inc(S["aC"], 1)

        @block.vector
        def _(ve):
            ve.wait_ge(S["gC"], 16 * 8)
            for ti in range(NLA):
                ve.wait_ge(S["sL"], 16 * (ti + 1))
                p = ti % 2
                for kk in range(4):
                    ve.bn_stats(stat6[p][:, 6 * kk:6 * kk + 6],
                                x_sb[p][:, 512 * kk:512 * (kk + 1)])
                for kk in range(4):
                    ve.bn_aggr(mvc[p][:, 2 * kk:2 * kk + 2], stat6[p][:, 6 * kk:6 * kk + 6])
                ve.tensor_add(pq8[p][:, 0:2], mvc[p][:, 0:2], mvc[p][:, 2:4])
                ve.tensor_add(pq8[p][:, 2:4], mvc[p][:, 4:6], mvc[p][:, 6:8])
                ve.tensor_mul(sq8[p][:], mvc[p][:], mvc[p][:])
                ve.tensor_add(pq8[p][:, 4:6], pq8[p][:, 0:2], pq8[p][:, 2:4])
                ve.tensor_add(tmp4[p][:, 0:2], sq8[p][:, 0:2], sq8[p][:, 2:4])
                ve.tensor_add(tmp4[p][:, 2:4], sq8[p][:, 4:6], sq8[p][:, 6:8])
                ve.tensor_scalar_mul(mv[p][:, 0:1], pq8[p][:, 4:5], 0.25)
                ve.tensor_add(pq8[p][:, 6:8], tmp4[p][:, 0:2], tmp4[p][:, 2:4])
                ve.tensor_mul(tmp4[p][:, 0:1], mv[p][:, 0:1], mv[p][:, 0:1])
                ve.tensor_add(pq8[p][:, 0:1], pq8[p][:, 5:6], pq8[p][:, 6:7])
                ve.tensor_copy(padb[:], gs_sb[:, 0:1])
                ve.scalar_tensor_tensor(tmp4[p][:, 1:2], pq8[p][:, 0:1], 0.25,
                                        tmp4[p][:, 0:1], ALU.mult, ALU.subtract)
                ve.tensor_copy(padb[:], gs_sb[:, 0:1])
                ve.tensor_scalar_add(veps[p][:], tmp4[p][:, 1:2], EPS_LN).then_inc(S["D0"], 1)
                ve.wait_ge(S["A0"], ti + 1)
                ve.reciprocal(rstd[ti % 2][:], stdv[ti % 2][:])
                if ti >= 2:
                    ve.wait_ge(S["sW"], 16 * (ti - 1))
                ve.tensor_copy(padb[:], gs_sb[:, 0:1])
                ve.tensor_scalar(xn_sb[ti % 2], x_sb[ti % 2], mv[ti % 2][:, 0:1],
                                 rstd[ti % 2][:], ALU.subtract, ALU.mult).then_inc(S["V0"], 1)
            for h in range(HPC if LEVEL >= 2 else 0):
                for sc in range(NSC):
                    Sg = h * NSC + sc
                    ve.wait_ge(S["sT"], 16 * 18 * (Sg + 1))
                    if sc == 0 and h >= 1:
                        ve.wait_ge(S["mV"], NT_H * h)
                    ve.wait_ge(S["aQ"], 4 * Sg + 1)
                    ve.wait_ge(S["mQ"], 6 * Sg + 4)
                    ve.tensor_mul(t1, q_sb, cs_buf[Sg % 2])
                    ve.tensor_mul(t2, P[3][:], sn_buf[Sg % 2])
                    ve.tensor_add(q_bf[:, sc * 512:(sc + 1) * 512], t1, t2).then_inc(S["vR"], 1)
                    ve.wait_ge(S["aQ"], 4 * Sg + 2)
                    ve.wait_ge(S["mQ"], 6 * Sg + 5)
                    ve.tensor_mul(t1, k_sb, cs_buf[Sg % 2])
                    ve.tensor_mul(t2, P[4][:], sn_buf[Sg % 2])
                    ve.tensor_add(k_bf[:, sc * 512:(sc + 1) * 512], t1, t2).then_inc(S["vR"], 1)
                T0 = NT_H * h
                for t, (qc, jb) in enumerate(ATT_TILES):
                    T = T0 + t
                    buf = T % 2
                    ve.wait_ge(S["aS"], T + 1)
                    if T >= 2:
                        ve.wait_ge(S["mV"], T - 1)
                    if jb >= 4 * qc:
                        r = jb - 4 * qc
                        ve.tensor_mul(s2[buf], s2[buf], masks[:, r * 512:(r + 1) * 512])
                    ve.tensor_mul(s4[buf], s2[buf], s2[buf]).then_inc(S["vS"], 1)
                    if jb == 4 * (qc + 1) - 1:
                        gq = h * NQC + qc
                        ve.wait_ge(S["mV"], T + 1)
                        if gq >= 2:
                            ve.wait_ge(S["sA"], 16 * (gq - 1))
                        ve.tensor_scalar_max(recip, P[1][:], EPS_AT)
                        ve.reciprocal(recip, recip)
                        ve.tensor_mul(e_out[gq % 2], P[2][:], recip).then_inc(S["vF"], 1)
            for ic in range(NIC if LEVEL >= 5 else 0):
                for tc in range(NTC):
                    k = ic * NTC + tc
                    pu = P[1] if k % 2 == 0 else P[3]
                    ve.wait_ge(S["mA"], 2 * k + 2)
                    ve.wait_ge(S["aA"], k + 1)
                    if k >= 2:
                        ve.wait_ge(S["sFu"], 16 * (k - 1))
                    ve.scalar_tensor_tensor(fuse_sb[k % 2], pu[:], us_sb[:, ic:ic + 1],
                                            g_s[k % 2], ALU.mult, ALU.mult).then_inc(S["vA"], 1)
            for tt in range(16 if (LEVEL >= 7 and COMBPART >= 2) else 0):
                ve.wait_ge(S["sX"], 32 * (tt + 1))
                if tt >= 2 and COMBPART >= 3:
                    ve.wait_ge(S["ST"], 16 * (tt - 1))
                ve.tensor_add(ycomb[tt % 2], xcomb[tt % 2], rcomb[tt % 2]).then_inc(S["vC"], 1)
                for fcg in range(4):
                    gi = 4 * tt + fcg
                    ve.wait_ge(S["aC"], gi + 1)
                    ve.tensor_add(ycomb[tt % 2][:, fcg * 512:(fcg + 1) * 512],
                                  ycomb[tt % 2][:, fcg * 512:(fcg + 1) * 512],
                                  ftm[gi % 2]).then_inc(S["vC"], 1)

    return nc



def _layout_inputs(inputs):
    x = np.ascontiguousarray(np.asarray(inputs["L0_attn_x"], dtype=np.float32))
    Wq = np.asarray(inputs["L0_attn_Wq"], dtype=np.float32)
    Wk = np.asarray(inputs["L0_attn_Wk"], dtype=np.float32)
    Wv = np.asarray(inputs["L0_attn_Wv"], dtype=np.float32)
    Wo = np.asarray(inputs["L0_attn_Wo"], dtype=np.float32)
    gwf = np.asarray(inputs["L0_ffn_gate_w"], dtype=np.float32)
    uwf = np.asarray(inputs["L0_ffn_up_w"], dtype=np.float32)
    dwf = np.asarray(inputs["L0_ffn_down_w"], dtype=np.float32)
    gs = np.asarray(inputs["L0_ffn_gate_scale"], dtype=np.float32)
    us = np.asarray(inputs["L0_ffn_up_scale"], dtype=np.float32)
    ds = np.asarray(inputs["L0_ffn_down_scale"], dtype=np.float32)

    gw_t = np.ascontiguousarray(gwf.reshape(16, 128, 48, 128).transpose(2, 1, 0, 3))
    uw_t = np.ascontiguousarray(uwf.reshape(16, 128, 48, 128).transpose(2, 1, 0, 3))
    dw_t = np.ascontiguousarray(dwf.reshape(48, 128, 16, 128))
    gs_t = np.ascontiguousarray(gs.reshape(48, 128).T)
    us_t = np.ascontiguousarray(us.reshape(48, 128).T)
    ds_t = np.ascontiguousarray(ds.reshape(16, 128).T)

    half = 64
    theta = np.power(np.float32(10000.0),
                     -(np.arange(half, dtype=np.float32) * np.float32(2.0) / np.float32(128)))
    pos = np.arange(N, dtype=np.float32)
    ang = (theta[:, None] * pos[None, :]).astype(np.float32)
    cos = np.cos(ang).astype(np.float32)
    sin = np.sin(ang).astype(np.float32)
    cst = np.ascontiguousarray(np.vstack([cos, cos]))
    snt = np.ascontiguousarray(np.vstack([-sin, sin]))

    ones = np.ones((128, 128), dtype=ml_dtypes.bfloat16)
    eyeb = np.eye(128, dtype=ml_dtypes.bfloat16)
    eyef = np.eye(128, dtype=np.float32)
    swp = np.zeros((128, 128), dtype=np.float32)
    swp[(np.arange(128) + 64) % 128, np.arange(128)] = 1.0
    msk = np.zeros((4, 128, 512), dtype=ml_dtypes.bfloat16)
    jj = np.arange(128)[:, None]
    qq = np.arange(512)[None, :]
    for r in range(4):
        msk[r] = ((jj + 128 * r) <= qq).astype(ml_dtypes.bfloat16)

    in_maps = []
    for c in range(NCORE_G):
        b = c // 4
        l = c % 4
        wq_c = np.ascontiguousarray(
            Wq[512 * l:512 * (l + 1), :].T.reshape(16, 128, 4, 128).transpose(2, 1, 0, 3))
        wk_c = np.ascontiguousarray(
            Wk[512 * l:512 * (l + 1), :].T.reshape(16, 128, 4, 128).transpose(2, 1, 0, 3))
        wv_c = np.ascontiguousarray(
            Wv[512 * l:512 * (l + 1), :].T.reshape(16, 128, 4, 128).transpose(2, 1, 0, 3))
        wo_c = np.ascontiguousarray(
            Wo.T[512 * l:512 * (l + 1), :].reshape(4, 128, 2048).transpose(1, 0, 2))
        in_maps.append({
            "xb": x[b], "x_own": np.ascontiguousarray(x[b, 2048 * l:2048 * (l + 1)]),
            "wq": wq_c, "wk": wk_c, "wv": wv_c, "wo": wo_c,
            "gw": gw_t, "uw": uw_t, "dwt": dw_t,
            "gsc": gs_t, "usc": us_t, "dsc": ds_t,
            "cst": cst, "snt": snt,
            "onesd": ones, "eyebd": eyeb, "eyefd": eyef, "swpd": swp,
            "maskd": msk,
        })
    return in_maps


NCORE_G = 8


def kernel(**inputs):
    in_maps = _layout_inputs(inputs)
    nc = build()
    res = run_bass_kernel_spmd(nc, in_maps, core_ids=list(range(NCORE_G)))
    y = np.empty((B, N, DM), dtype=np.float32)
    for c in range(NCORE_G):
        b = c // 4
        l = c % 4
        y[b, 2048 * l:2048 * (l + 1), :] = res.results[c]["y_own"]
    return y


# version 04
# speedup vs baseline: 1.5695x; 1.1020x over previous
import numpy as np
import ml_dtypes
import concourse.bass as bass
import concourse.mybir as mybir
from concourse.bass_utils import run_bass_kernel_spmd

F32 = mybir.dt.float32
BF16 = mybir.dt.bfloat16
FP8 = mybir.dt.float8e4
AF = mybir.ActivationFunctionType
ALU = mybir.AluOpType
AX = mybir.AxisListType

B, N, DM = 2, 8192, 2048
DI = 6144
HPC = 4
OWN = 2048
EPS_LN = 1e-5
EPS_AT = 1e-6

NLT = N // 128
NLA = NLT + OWN // 128
NSC = N // 512
NQC = N // 512
NIC = DI // 128
NTC = OWN // 512
NWTC = N // 512
NEC = DM // 128

ATT_TILES = [(qc, jb) for qc in range(NQC) for jb in range(4 * (qc + 1))]
NT_H = len(ATT_TILES)
FB_PASSES = [(0, 6), (6, 12), (12, 16)]

DEBUG = ()
LEVEL = 7
WOPART = 3
SKIPRS = False
COMBPART = 3


def build():
    nc = bass.Bass("TRN2", target_bir_lowering=False, debug=False)
    xb = nc.dram_tensor("xb", [N, DM], F32, kind="ExternalInput").ap()
    x_own = nc.dram_tensor("x_own", [OWN, DM], F32, kind="ExternalInput").ap()
    wq = nc.dram_tensor("wq", [HPC, 128, NEC, 128], F32, kind="ExternalInput").ap()
    wk = nc.dram_tensor("wk", [HPC, 128, NEC, 128], F32, kind="ExternalInput").ap()
    wv = nc.dram_tensor("wv", [HPC, 128, NEC, 128], F32, kind="ExternalInput").ap()
    wo = nc.dram_tensor("wo", [128, HPC, DM], F32, kind="ExternalInput").ap()
    gw = nc.dram_tensor("gw", [NIC, 128, NEC, 128], F32, kind="ExternalInput").ap()
    uw = nc.dram_tensor("uw", [NIC, 128, NEC, 128], F32, kind="ExternalInput").ap()
    dwt = nc.dram_tensor("dwt", [NIC, 128, NEC, 128], F32, kind="ExternalInput").ap()
    gsc = nc.dram_tensor("gsc", [128, NIC], F32, kind="ExternalInput").ap()
    usc = nc.dram_tensor("usc", [128, NIC], F32, kind="ExternalInput").ap()
    dsc = nc.dram_tensor("dsc", [128, NEC], F32, kind="ExternalInput").ap()
    cst = nc.dram_tensor("cst", [128, N], F32, kind="ExternalInput").ap()
    snt = nc.dram_tensor("snt", [128, N], F32, kind="ExternalInput").ap()
    onesd = nc.dram_tensor("onesd", [128, 128], BF16, kind="ExternalInput").ap()
    eyebd = nc.dram_tensor("eyebd", [128, 128], BF16, kind="ExternalInput").ap()
    eyefd = nc.dram_tensor("eyefd", [128, 128], F32, kind="ExternalInput").ap()
    swpd = nc.dram_tensor("swpd", [128, 128], F32, kind="ExternalInput").ap()
    maskd = nc.dram_tensor("maskd", [4, 128, 512], BF16, kind="ExternalInput").ap()

    y_own = nc.dram_tensor("y_own", [OWN, DM], F32, kind="ExternalOutput").ap()

    xn_sc = nc.dram_tensor("xn_sc", [N, DM], BF16, kind="Internal").ap()
    xno_sc = nc.dram_tensor("xno_sc", [OWN, DM], BF16, kind="Internal").ap()
    od_sc = nc.dram_tensor("od_sc", [HPC, 128, N], BF16, kind="Internal").ap()
    fuse_sc = nc.dram_tensor("fuse_sc", [NIC, 128, OWN], BF16, kind="Internal").ap()
    dw8_sc = nc.dram_tensor("dw8_sc", [NIC, 128, DM], FP8, kind="Internal").ap()
    gw8_sc = nc.dram_tensor("gw8_sc", [NIC, 128, DM], FP8, kind="Internal").ap()
    uw8_sc = nc.dram_tensor("uw8_sc", [NIC, 128, DM], FP8, kind="Internal").ap()
    rs_in_t = nc.dram_tensor("rs_in", [N, DM], BF16, kind="Internal")
    rs_out_t = nc.dram_tensor("rs_out", [OWN, DM], BF16, kind="Internal")
    rs_in = rs_in_t.ap()
    rs_out = rs_out_t.ap()

    dbg = {}
    if "xn" in DEBUG:
        dbg["d_xn"] = nc.dram_tensor("d_xn", [N, DM], BF16, kind="ExternalOutput").ap()
    if "od" in DEBUG:
        dbg["d_od"] = nc.dram_tensor("d_od", [HPC * 128, N], BF16, kind="ExternalOutput").ap()
    if "rsin" in DEBUG:
        dbg["d_rsin"] = nc.dram_tensor("d_rsin", [N, DM], BF16, kind="ExternalOutput").ap()
    if "rsout" in DEBUG:
        dbg["d_rsout"] = nc.dram_tensor("d_rsout", [OWN, DM], BF16, kind="ExternalOutput").ap()
    if "fuse" in DEBUG:
        dbg["d_fuse"] = nc.dram_tensor("d_fuse", [NIC * 128, OWN], BF16, kind="ExternalOutput").ap()
    NDBG = len(dbg)

    from contextlib import ExitStack
    with ExitStack() as stack:
        en = stack.enter_context

        A = en(nc.sbuf_tensor("A", [128, 32768], BF16))
        Af = A.bitcast(F32)
        q_bf = A[:, 0:8192]
        k_bf = A[:, 8192:16384]
        v_tm = A[:, 16384:24576]
        xnoT = [A[:, ec * 2048:(ec + 1) * 2048] for ec in range(NEC)]
        q_sb = Af[:, 12288:12800]
        k_sb = Af[:, 12800:13312]
        t1 = Af[:, 13312:13824]
        t2 = Af[:, 13824:14336]
        v_tmp = [A[:, 28672 + i * 512:28672 + (i + 1) * 512] for i in range(2)]
        s2 = [Af[:, 12288:12800], Af[:, 12800:13312]]
        s4 = [A[:, 26624 + i * 512:26624 + (i + 1) * 512] for i in range(2)]
        recip = Af[:, 14848:15360]
        e_out = [A[:, 30720 + i * 512:30720 + (i + 1) * 512] for i in range(2)]

        Bx = en(nc.sbuf_tensor("Bx", [128, 32768], BF16))
        Bf = Bx.bitcast(F32)
        xnt = [[Bx[:, bi * 8192 + ec * 512: bi * 8192 + (ec + 1) * 512]
                for ec in range(NEC)] for bi in range(2)]
        cs_buf = [Bf[:, 8192 + i * 512:8192 + (i + 1) * 512] for i in range(2)]
        sn_buf = [Bf[:, 9216 + i * 512:9216 + (i + 1) * 512] for i in range(2)]
        wq_sb = [Bx[:, 20480 + bi * 6144:20480 + bi * 6144 + 2048] for bi in range(2)]
        wk_sb = [Bx[:, 22528 + bi * 6144:22528 + bi * 6144 + 2048] for bi in range(2)]
        wv_sb = [Bx[:, 24576 + bi * 6144:24576 + bi * 6144 + 2048] for bi in range(2)]
        attn_T = [Bf[:, fc * 512:(fc + 1) * 512] for fc in range(16)]
        stage = [Bx[:, 16384 + i * 2048:16384 + (i + 1) * 2048] for i in range(2)]
        wo_sb = Bx[:, 24576:32768]
        ffn_T = [Bx[:, hc * 2048:(hc + 1) * 2048] for hc in range(NEC)]

        C = en(nc.sbuf_tensor("C", [128, 24576], BF16))
        Cf = C.bitcast(F32)
        x_sb = [Cf[:, i * 2048:(i + 1) * 2048] for i in range(2)]
        xn_sb = [C[:, 8192 + i * 2048:8192 + (i + 1) * 2048] for i in range(2)]
        od_buf = [C[:, i * 2048:(i + 1) * 2048] for i in range(2)]
        g_s = [Cf[:, 1024 + i * 512:1024 + (i + 1) * 512] for i in range(2)]
        fuse_sb = [C[:, 4096 + i * 512:4096 + (i + 1) * 512] for i in range(2)]
        fuse_tc = [C[:, ic * 512:(ic + 1) * 512] for ic in range(NIC)]
        xcomb = [Cf[:, i * 2048:(i + 1) * 2048] for i in range(2)]
        rcomb = [C[:, 8192 + i * 2048 + 0:8192 + i * 2048 + 2048] for i in range(2)]
        ycomb = [Cf[:, 8192 + i * 2048:8192 + (i + 1) * 2048] for i in range(2)]
        ftm = [C[:, 12288 + i * 512:12288 + (i + 1) * 512] for i in range(2)]

        masks = en(nc.sbuf_tensor("masks", [128, 2048], BF16))
        gu8t = en(nc.sbuf_tensor("gu8t", [128, 8192], FP8))
        gu8 = [[gu8t[:, (bi * 2 + j) * 2048:(bi * 2 + j + 1) * 2048] for j in range(2)]
               for bi in range(2)]
        dw8t = en(nc.sbuf_tensor("dw8t", [128, 6144], FP8))
        dw8b = [dw8t[:, i * 3072:(i + 1) * 3072] for i in range(2)]
        dw8p = en(nc.sbuf_tensor("dw8p", [128, 2048], FP8))
        pp8 = [en(nc.sbuf_tensor(f"pp8_{i}", [128, 2048], FP8)) for i in range(2)]
        ones_s = en(nc.sbuf_tensor("ones_s", [128, 128], BF16))
        eyeb_s = en(nc.sbuf_tensor("eyeb_s", [128, 128], BF16))
        eyef_s = en(nc.sbuf_tensor("eyef_s", [128, 128], F32))
        swp_s = en(nc.sbuf_tensor("swp_s", [128, 128], F32))
        gs_sb = en(nc.sbuf_tensor("gs_sb", [128, NIC], F32))
        us_sb = en(nc.sbuf_tensor("us_sb", [128, NIC], F32))
        ds_sb = en(nc.sbuf_tensor("ds_sb", [128, NEC], F32))
        stat6 = [en(nc.sbuf_tensor(f"stat6_{i}", [128, 24], F32)) for i in range(2)]
        mvc = [en(nc.sbuf_tensor(f"mvc_{i}", [128, 8], F32)) for i in range(2)]
        tmp4 = [en(nc.sbuf_tensor(f"tmp4_{i}", [128, 4], F32)) for i in range(2)]
        pq8 = [en(nc.sbuf_tensor(f"pq8_{i}", [128, 8], F32)) for i in range(2)]
        sq8 = [en(nc.sbuf_tensor(f"sq8_{i}", [128, 8], F32)) for i in range(2)]
        padb = en(nc.sbuf_tensor("padb", [128, 1], F32))
        mv = [en(nc.sbuf_tensor(f"mv_{i}", [128, 2], F32)) for i in range(2)]
        veps = [en(nc.sbuf_tensor(f"veps_{i}", [128, 1], F32)) for i in range(2)]
        stdv = [en(nc.sbuf_tensor(f"stdv_{i}", [128, 1], F32)) for i in range(2)]
        rstd = [en(nc.sbuf_tensor(f"rstd_{i}", [128, 1], F32)) for i in range(2)]

        P = [en(nc.psum_tensor(f"P{i}", [128, 512], F32)) for i in range(7)]
        PTt = en(nc.psum_tensor("PTt", [128, 1024], BF16))
        PT = [PTt[:, 0:512], PTt[:, 512:1024]]

        names = ["sL", "sW", "V0", "D0", "A0", "sT", "gQ", "mQ", "aQ", "vR",
                 "mS", "aS", "vS", "mV", "vF", "sA", "gWo", "gO", "mW", "aW",
                 "mT", "aG", "sR", "CC", "gD", "gF", "sTF", "mA", "aA", "vA",
                 "sFu", "sFB", "gB", "mBc", "aB", "sX", "mC", "vC",
                 "ST", "gC", "aC"]
        S = {nm: en(nc.semaphore(nm)) for nm in names}
        block = en(nc.Block())

        def ln_src(ti):
            if ti < NLT:
                return xb[ti * 128:(ti + 1) * 128, :]
            t = ti - NLT
            return x_own[t * 128:(t + 1) * 128, :]

        def ln_dst(ti):
            if ti < NLT:
                return xn_sc[ti * 128:(ti + 1) * 128, :]
            t = ti - NLT
            return xno_sc[t * 128:(t + 1) * 128, :]

        @block.sync
        def _(sync):
            sync.dma_start(out=x_sb[0], in_=ln_src(0)).then_inc(S["sL"], 16)
            sync.dma_start(out=x_sb[1], in_=ln_src(1)).then_inc(S["sL"], 16)
            for ti in range(NLA):
                sync.wait_ge(S["V0"], ti + 1)
                sync.dma_start(out=ln_dst(ti), in_=xn_sb[ti % 2]).then_inc(S["sW"], 16)
                if ti + 2 < NLA:
                    sync.dma_start(out=x_sb[ti % 2], in_=ln_src(ti + 2)).then_inc(S["sL"], 16)
            for h in range(HPC if LEVEL >= 2 else 0):
                for sc in range(NSC):
                    Sg = h * NSC + sc
                    sync.wait_ge(S["sW"], 16 * min(4 * sc + 4, NLT))
                    if Sg >= 2:
                        sync.wait_ge(S["mQ"], 6 * (Sg - 2) + 3)
                        sync.wait_ge(S["vR"], 2 * (Sg - 1))
                    for ec in range(NEC):
                        sync.dma_start(
                            out=xnt[Sg % 2][ec],
                            in_=xn_sc[sc * 512:(sc + 1) * 512, ec * 128:(ec + 1) * 128],
                            transpose=True).then_inc(S["sT"], 16)
                    sync.dma_start(out=cs_buf[Sg % 2],
                                   in_=cst[:, sc * 512:(sc + 1) * 512]).then_inc(S["sT"], 16)
                    sync.dma_start(out=sn_buf[Sg % 2],
                                   in_=snt[:, sc * 512:(sc + 1) * 512]).then_inc(S["sT"], 16)
                for qc in range(NQC):
                    g = h * NQC + qc
                    sync.wait_ge(S["vF"], g + 1)
                    sync.dma_start(out=od_sc[h, :, qc * 512:(qc + 1) * 512],
                                   in_=e_out[g % 2]).then_inc(S["sA"], 16)
            if LEVEL < 2:
                sync.wait_ge(S["sW"], 16 * NLA)
            if LEVEL >= 5:
                sync.wait_ge(S["vF"], HPC * NQC)
            if LEVEL >= 5:
                sync.wait_ge(S["sA"], 16 * HPC * NQC)
                sync.wait_ge(S["sW"], 16 * NLA)
                for ec in range(NEC):
                    sync.dma_start(out=xnoT[ec], in_=xno_sc[:, ec * 128:(ec + 1) * 128],
                                   transpose=True).then_inc(S["sTF"], 16)
            for tc in range(NWTC if (LEVEL >= 3 and WOPART >= 3) else 0):
                for tt in range(4):
                    TT = 4 * tc + tt
                    sync.wait_ge(S["aG"], 4 * TT + 4)
                    sync.dma_start(out=rs_in[TT * 128:(TT + 1) * 128, :],
                                   in_=stage[TT % 2]).then_inc(S["sR"], 16)
            if LEVEL >= 5:
                sync.wait_ge(S["gD"], 32 * 9 * NWTC)
                if LEVEL >= 4 and not SKIPRS:
                    sync.wait_ge(S["CC"], 1)
                for ic0 in (0, 1):
                    sync.dma_start(out=gu8[ic0 % 2][0], in_=gw8_sc[ic0]).then_inc(S["gF"], 16)
                    sync.dma_start(out=gu8[ic0 % 2][1], in_=uw8_sc[ic0]).then_inc(S["gF"], 16)
            for ic in range(NIC if LEVEL >= 5 else 0):
                for tc in range(NTC):
                    k = ic * NTC + tc
                    sync.wait_ge(S["vA"], k + 1)
                    sync.dma_start(out=fuse_sc[ic, :, tc * 512:(tc + 1) * 512],
                                   in_=fuse_sb[k % 2]).then_inc(S["sFu"], 16)
                if ic + 2 < NIC:
                    sync.dma_start(out=gu8[ic % 2][0], in_=gw8_sc[ic + 2]).then_inc(S["gF"], 16)
                    sync.dma_start(out=gu8[ic % 2][1], in_=uw8_sc[ic + 2]).then_inc(S["gF"], 16)
            if LEVEL >= 6:
                sync.wait_ge(S["sFu"], 16 * NIC * NTC)
            for tc in range(NTC if LEVEL >= 6 else 0):
                if tc > 0:
                    sync.wait_ge(S["mBc"], 36 * tc)
                for cg in range(6):
                    ic0 = cg * 8
                    sync.dma_start(
                        out=C[:, ic0 * 512:(ic0 + 8) * 512].rearrange("p (i t) -> p i t", t=512),
                        in_=fuse_sc[ic0:ic0 + 8, :, tc * 512:(tc + 1) * 512].rearrange("i p t -> p i t"),
                    ).then_inc(S["sFB"], 16)
            if LEVEL >= 7:
                sync.wait_ge(S["mBc"], 36 * NTC)
                sync.wait_ge(S["CC"], 1)
            for tt in ((0, 1) if (LEVEL >= 7 and COMBPART >= 2) else ()):
                sync.dma_start(out=xcomb[tt], in_=x_own[tt * 128:(tt + 1) * 128, :]).then_inc(S["sX"], 16)
                sync.dma_start(out=rcomb[tt], in_=rs_out[tt * 128:(tt + 1) * 128, :]).then_inc(S["sX"], 16)
            for tt in range(16 if (LEVEL >= 7 and COMBPART >= 3) else 0):
                sync.wait_ge(S["vC"], 5 * (tt + 1))
                sync.dma_start(out=y_own[tt * 128:(tt + 1) * 128, :],
                               in_=ycomb[tt % 2]).then_inc(S["ST"], 16)
                if tt + 2 < 16:
                    sync.dma_start(out=xcomb[tt % 2],
                                   in_=x_own[(tt + 2) * 128:(tt + 3) * 128, :]).then_inc(S["sX"], 16)
                    sync.dma_start(out=rcomb[tt % 2],
                                   in_=rs_out[(tt + 2) * 128:(tt + 3) * 128, :]).then_inc(S["sX"], 16)
            nst = 16 if (LEVEL >= 7 and COMBPART >= 3) else 0
            if LEVEL >= 7 and COMBPART == 0:
                sync.wait_ge(S["mC"], 64)
            elif LEVEL >= 7 and COMBPART == 1:
                sync.wait_ge(S["aC"], 64)
            elif LEVEL >= 7 and COMBPART == 2:
                sync.wait_ge(S["vC"], 80)
            if LEVEL < 7:
                if LEVEL == 1:
                    sync.wait_ge(S["sW"], 16 * NLA)
                elif LEVEL == 2:
                    sync.wait_ge(S["sA"], 16 * HPC * NQC)
                elif LEVEL == 3:
                    if WOPART >= 3:
                        sync.wait_ge(S["sR"], 16 * NWTC * 4)
                    elif WOPART == 2:
                        sync.wait_ge(S["mT"], 256)
                    else:
                        sync.wait_ge(S["aW"], 256)
                elif LEVEL == 4:
                    sync.wait_ge(S["CC"], 1)
                elif LEVEL == 5:
                    sync.wait_ge(S["sFu"], 16 * NIC * NTC)
                elif LEVEL == 6:
                    sync.wait_ge(S["aB"], 3 * NTC)
            if "xn" in DEBUG:
                sync.dma_start(out=dbg["d_xn"], in_=xn_sc).then_inc(S["ST"], 16)
                nst += 1
            if "od" in DEBUG:
                sync.dma_start(out=dbg["d_od"], in_=od_sc.rearrange("h p t -> (h p) t")).then_inc(S["ST"], 16)
                nst += 1
            if "rsin" in DEBUG:
                sync.dma_start(out=dbg["d_rsin"], in_=rs_in).then_inc(S["ST"], 16)
                nst += 1
            if "rsout" in DEBUG:
                sync.dma_start(out=dbg["d_rsout"], in_=rs_out).then_inc(S["ST"], 16)
                nst += 1
            if "fuse" in DEBUG:
                sync.dma_start(out=dbg["d_fuse"], in_=fuse_sc.rearrange("i p t -> (i p) t")).then_inc(S["ST"], 16)
                nst += 1
            sync.wait_ge(S["ST"], 16 * nst)

        @block.gpsimd
        def _(g):
            g.dma_start(out=ones_s[:], in_=onesd).then_inc(S["gC"], 16)
            g.dma_start(out=eyeb_s[:], in_=eyebd).then_inc(S["gC"], 16)
            g.dma_start(out=eyef_s[:], in_=eyefd).then_inc(S["gC"], 16)
            g.dma_start(out=swp_s[:], in_=swpd).then_inc(S["gC"], 16)
            g.dma_start(out=masks[:].rearrange("p (r q) -> p r q", q=512),
                        in_=maskd.rearrange("r p q -> p r q")).then_inc(S["gC"], 16)
            g.dma_start(out=gs_sb[:], in_=gsc).then_inc(S["gC"], 16)
            g.dma_start(out=us_sb[:], in_=usc).then_inc(S["gC"], 16)
            g.dma_start(out=ds_sb[:], in_=dsc).then_inc(S["gC"], 16)
            for h in range(HPC if LEVEL >= 2 else 0):
                if h >= 2:
                    g.wait_ge(S["mQ"], 6 * NSC * (h - 1))
                for wsrc, wdst in [(wq, wq_sb), (wk, wk_sb), (wv, wv_sb)]:
                    g.dma_start(out=wdst[h % 2].rearrange("p (ec d) -> p ec d", d=128),
                                in_=wsrc[h]).then_inc(S["gQ"], 16)
            if LEVEL >= 3:
                g.wait_ge(S["vF"], HPC * NQC)
                g.dma_start(out=wo_sb.rearrange("p (dh f) -> p dh f", f=DM),
                            in_=wo).then_inc(S["gWo"], 16)
            for tc in range(NWTC if LEVEL >= 3 else 0):
                if tc == 0:
                    g.wait_ge(S["sA"], 16 * HPC * NQC)
                if tc >= 2:
                    g.wait_ge(S["mW"], 16 * (tc - 1))
                g.dma_start(out=od_buf[tc % 2].rearrange("p (dh t) -> p dh t", t=512),
                            in_=od_sc[:, :, tc * 512:(tc + 1) * 512].rearrange("h p t -> p h t"),
                            ).then_inc(S["gO"], 16)
                for j in range(9 if LEVEL >= 5 else 0):
                    p = tc * 9 + j
                    ic = p // 3
                    srcw, dstw = ((dwt, dw8_sc), (gw, gw8_sc), (uw, uw8_sc))[p % 3]
                    if p >= 2:
                        g.wait_ge(S["gD"], 32 * (p - 1))
                    g.dma_start(out=pp8[p % 2][:],
                                in_=srcw[ic].rearrange("p a b -> p (a b)")).then_inc(S["gD"], 16)
                    g.wait_ge(S["gD"], 32 * p + 16)
                    g.dma_start(out=dstw[ic], in_=pp8[p % 2][:]).then_inc(S["gD"], 16)

            if LEVEL >= 4 and not SKIPRS:
                g.wait_ge(S["sR"], 16 * NWTC * 4)
                if LEVEL >= 5:
                    g.wait_ge(S["gD"], 32 * 9 * NWTC)
                g.collective_compute(
                    "ReduceScatter", ALU.add,
                    replica_groups=[[0, 1, 2, 3], [4, 5, 6, 7]],
                    ins=[rs_in_t.ap().opt()], outs=[rs_out_t.ap().opt()],
                ).then_inc(S["CC"], 1)
            if LEVEL >= 6:
                g.wait_ge(S["gD"], 32 * 9 * NWTC)
                g.wait_ge(S["CC"], 1)
            for tc in range(NTC if LEVEL >= 6 else 0):
                for pi, (h0, h1) in enumerate(FB_PASSES):
                    gidx = tc * 3 + pi
                    w = (h1 - h0) * 128
                    for cg in range(12):
                        CH = gidx * 12 + cg
                        if CH >= 2:
                            g.wait_ge(S["mBc"], CH - 1)
                        g.dma_start(
                            out=dw8b[CH % 2][:, 0:4 * w].rearrange("p (i t) -> p i t", t=w),
                            in_=dw8_sc[cg * 4:cg * 4 + 4, :, h0 * 128:h1 * 128].rearrange("i p t -> p i t"),
                        ).then_inc(S["gB"], 16)

        @block.tensor
        def _(te):
            te.wait_ge(S["gC"], 16 * 8)
            for h in range(HPC if LEVEL >= 2 else 0):
                hb = h % 2
                for sc in range(NSC):
                    Sg = h * NSC + sc
                    te.wait_ge(S["sT"], 16 * 18 * (Sg + 1))
                    te.wait_ge(S["gQ"], 16 * 3 * (h + 1))
                    if h >= 1 and sc == 0:
                        te.wait_ge(S["vF"], NQC * h)
                    if Sg >= 1:
                        te.wait_ge(S["aQ"], 4 * (Sg - 1) + 3)
                    for wsb, pb in ((wq_sb, P[0]), (wk_sb, P[1]), (wv_sb, P[2])):
                        for ec in range(NEC):
                            ins = te.matmul(pb[:], wsb[hb][:, ec * 128:(ec + 1) * 128],
                                            xnt[Sg % 2][ec], start=(ec == 0), stop=(ec == NEC - 1))
                        ins.then_inc(S["mQ"], 1)
                    te.wait_ge(S["aQ"], 4 * Sg + 1)
                    if Sg >= 1:
                        te.wait_ge(S["vR"], 2 * (Sg - 1) + 1)
                    te.matmul(P[3][:], swp_s[:], q_sb, start=True, stop=True).then_inc(S["mQ"], 1)
                    te.wait_ge(S["aQ"], 4 * Sg + 2)
                    if Sg >= 1:
                        te.wait_ge(S["vR"], 2 * Sg)
                    te.matmul(P[4][:], swp_s[:], k_sb, start=True, stop=True).then_inc(S["mQ"], 1)
                    te.wait_ge(S["aQ"], 4 * Sg + 3)
                    if Sg >= 1:
                        te.wait_ge(S["aQ"], 4 * (Sg - 1) + 4)
                    for j in range(4):
                        ins = te.transpose(PT[0][:, j * 128:(j + 1) * 128],
                                           v_tmp[Sg % 2][:, j * 128:(j + 1) * 128],
                                           eyeb_s[:])
                    ins.then_inc(S["mQ"], 1)
                T0 = NT_H * h
                def emit_st(t, h=h):
                    T = T0 + t
                    qc, jb = ATT_TILES[t]
                    if t == 0:
                        te.wait_ge(S["vR"], 2 * NSC * (h + 1))
                    if T >= 3:
                        te.wait_ge(S["aS"], T - 2)
                    te.wait_ge(S["vR"], 2 * NSC * h + 2 * (qc + 1))
                    te.matmul(P[4 + T % 3][:], k_bf[:, jb * 128:(jb + 1) * 128],
                              q_bf[:, qc * 512:(qc + 1) * 512],
                              start=True, stop=True).then_inc(S["mS"], 1)
                emit_st(0)
                emit_st(1)
                for t, (qc, jb) in enumerate(ATT_TILES):
                    T = T0 + t
                    buf = T % 2
                    if t + 2 < NT_H:
                        emit_st(t + 2)
                    te.wait_ge(S["vS"], T + 1)
                    first = (jb == 0)
                    last = (jb == 4 * (qc + 1) - 1)
                    if first:
                        gq = h * NQC + qc
                        if gq > 0:
                            te.wait_ge(S["vF"], gq)
                        if qc == 0:
                            te.wait_ge(S["aQ"], 64 * h + 63)
                    if jb % 4 == 0:
                        te.wait_ge(S["aQ"], 64 * h + 4 * (jb // 4 + 1))
                    te.matmul(P[2][:], v_tm[:, jb * 128:(jb + 1) * 128], s4[buf],
                              start=first, stop=last)
                    te.matmul(P[1][:], ones_s[:], s4[buf],
                              start=first, stop=last).then_inc(S["mV"], 1)
            if LEVEL >= 3:
                te.wait_ge(S["vF"], HPC * NQC)
                te.wait_ge(S["gWo"], 16)
            for tc in range(NWTC if LEVEL >= 3 else 0):
                te.wait_ge(S["gO"], 16 * (tc + 1))
                for fc in range(16):
                    idx = 16 * tc + fc
                    bank = P[fc % 2]
                    if idx >= 2:
                        te.wait_ge(S["aW"], idx - 1)
                    for dh in range(HPC):
                        ins = te.matmul(bank[:],
                                        wo_sb[:, dh * 2048 + fc * 128: dh * 2048 + fc * 128 + 128],
                                        od_buf[tc % 2][:, dh * 512:(dh + 1) * 512],
                                        start=(dh == 0), stop=(dh == HPC - 1))
                    ins.then_inc(S["mW"], 1)
                if WOPART < 2:
                    continue
                te.wait_ge(S["aW"], 16 * tc + 16)
                for tt in range(4):
                    for fcg in range(4):
                        gi = 16 * tc + 4 * tt + fcg
                        if gi >= 1 and WOPART >= 3:
                            te.wait_ge(S["aG"], gi)
                        for j in range(4):
                            fc = 4 * fcg + j
                            ins = te.transpose(P[2][:, j * 128:(j + 1) * 128],
                                               attn_T[fc][:, tt * 128:(tt + 1) * 128],
                                               eyef_s[:])
                        ins.then_inc(S["mT"], 1)
            if LEVEL >= 5:
                te.wait_ge(S["sTF"], 16 * NEC)
                if LEVEL >= 3:
                    te.wait_ge(S["aW"], 256)
                    te.wait_ge(S["aG"], 256)
            for ic in range(NIC if LEVEL >= 5 else 0):
                te.wait_ge(S["gF"], 32 * (ic + 1))
                for tc in range(NTC):
                    k = ic * NTC + tc
                    pg, pu = (P[0], P[1]) if k % 2 == 0 else (P[2], P[3])
                    if k >= 2:
                        te.wait_ge(S["aA"], k - 1)
                        te.wait_ge(S["vA"], k - 1)
                    for gj, pb in ((0, pg), (1, pu)):
                        for ec in range(NEC):
                            ins = te.matmul(pb[:], gu8[ic % 2][gj][:, ec * 128:(ec + 1) * 128],
                                            xnoT[ec][:, tc * 512:(tc + 1) * 512],
                                            start=(ec == 0), stop=(ec == NEC - 1))
                        ins.then_inc(S["mA"], 1)
            for tc in range(NTC if LEVEL >= 6 else 0):
                te.wait_ge(S["sFB"], 16 * 6 * (tc + 1))
                for pi, (h0, h1) in enumerate(FB_PASSES):
                    gidx = tc * 3 + pi
                    nh = h1 - h0
                    w = nh * 128
                    if gidx == 0:
                        te.wait_ge(S["aA"], NIC * NTC)
                        te.wait_ge(S["vA"], NIC * NTC)
                    else:
                        te.wait_ge(S["aB"], gidx)
                    for cg in range(12):
                        CH = gidx * 12 + cg
                        te.wait_ge(S["gB"], 16 * (CH + 1))
                        for j in range(4):
                            ic = cg * 4 + j
                            for hi in range(nh):
                                ins = te.matmul(P[hi][:],
                                                dw8b[CH % 2][:, j * w + hi * 128: j * w + hi * 128 + 128],
                                                fuse_tc[ic],
                                                start=(ic == 0), stop=(ic == NIC - 1))
                        ins.then_inc(S["mBc"], 1)
            if LEVEL >= 7:
                te.wait_ge(S["aB"], 3 * NTC)
            for tt in range(16 if LEVEL >= 7 else 0):
                for fcg in range(4):
                    gi = 4 * tt + fcg
                    pt = PT[0]
                    if gi >= 1 and COMBPART >= 1:
                        te.wait_ge(S["aC"], gi)
                    for j in range(4):
                        hc = fcg * 4 + j
                        ins = te.transpose(pt[:, j * 128:(j + 1) * 128],
                                           ffn_T[hc][:, tt * 128:(tt + 1) * 128],
                                           eyeb_s[:])
                    ins.then_inc(S["mC"], 1)

        @block.scalar
        def _(sc_):
            sc_.wait_ge(S["gC"], 16 * 8)
            for ti in range(NLA):
                sc_.wait_ge(S["D0"], ti + 1)
                if ti >= 2:
                    sc_.wait_ge(S["V0"], ti - 1)
                sc_.activation(stdv[ti % 2][:], veps[ti % 2][:], AF.Sqrt).then_inc(S["A0"], 1)
            for h in range(HPC if LEVEL >= 2 else 0):
                for sc in range(NSC):
                    Sg = h * NSC + sc
                    sc_.wait_ge(S["mQ"], 6 * Sg + 1)
                    if Sg >= 1:
                        sc_.wait_ge(S["vR"], 2 * Sg - 1)
                    sc_.activation(q_sb, P[0][:], AF.Copy).then_inc(S["aQ"], 1)
                    sc_.wait_ge(S["mQ"], 6 * Sg + 2)
                    if Sg >= 1:
                        sc_.wait_ge(S["vR"], 2 * Sg)
                    sc_.activation(k_sb, P[1][:], AF.Copy).then_inc(S["aQ"], 1)
                    sc_.wait_ge(S["mQ"], 6 * Sg + 3)
                    sc_.activation(v_tmp[Sg % 2], P[2][:], AF.Copy).then_inc(S["aQ"], 1)
                    sc_.wait_ge(S["mQ"], 6 * Sg + 6)
                    sc_.activation(v_tm[:, sc * 512:(sc + 1) * 512], PT[0], AF.Copy).then_inc(S["aQ"], 1)
                T0 = NT_H * h
                for t in range(NT_H):
                    T = T0 + t
                    buf = T % 2
                    sc_.wait_ge(S["mS"], T + 1)
                    if t == 0:
                        sc_.wait_ge(S["vR"], 2 * NSC * (h + 1))
                    if T >= 2:
                        sc_.wait_ge(S["vS"], T - 1)
                    sc_.activation(s2[buf], P[4 + T % 3][:], AF.Square).then_inc(S["aS"], 1)
            for tc in range(NWTC if LEVEL >= 3 else 0):
                for fc in range(16):
                    idx = 16 * tc + fc
                    sc_.wait_ge(S["mW"], idx + 1)
                    if tc >= 1 and fc == 0 and WOPART >= 2:
                        sc_.wait_ge(S["mT"], 16 * tc)
                    sc_.activation(attn_T[fc], P[fc % 2][:], AF.Copy).then_inc(S["aW"], 1)
                for tt in range(4 if WOPART >= 3 else 0):
                    TT = 4 * tc + tt
                    for fcg in range(4):
                        gi = 16 * tc + 4 * tt + fcg
                        sc_.wait_ge(S["mT"], gi + 1)
                        if fcg == 0 and TT >= 2:
                            sc_.wait_ge(S["sR"], 16 * (TT - 1))
                        sc_.activation(stage[TT % 2][:, fcg * 512:(fcg + 1) * 512],
                                       P[2][:], AF.Copy).then_inc(S["aG"], 1)
            for ic in range(NIC if LEVEL >= 5 else 0):
                for tc in range(NTC):
                    k = ic * NTC + tc
                    pg = P[0] if k % 2 == 0 else P[2]
                    sc_.wait_ge(S["mA"], 2 * k + 1)
                    if k >= 2:
                        sc_.wait_ge(S["vA"], k - 1)
                    sc_.activation(g_s[k % 2], pg[:], AF.Silu,
                                   scale=gs_sb[:, ic:ic + 1]).then_inc(S["aA"], 1)
            for tc in range(NTC if LEVEL >= 6 else 0):
                for pi, (h0, h1) in enumerate(FB_PASSES):
                    gidx = tc * 3 + pi
                    sc_.wait_ge(S["mBc"], 12 * (gidx + 1))
                    for hi, hc in enumerate(range(h0, h1)):
                        ins = sc_.activation(ffn_T[hc][:, tc * 512:(tc + 1) * 512], P[hi][:],
                                             AF.Identity, scale=ds_sb[:, hc:hc + 1])
                    ins.then_inc(S["aB"], 1)
            for tt in range(16 if (LEVEL >= 7 and COMBPART >= 1) else 0):
                for fcg in range(4):
                    gi = 4 * tt + fcg
                    sc_.wait_ge(S["mC"], gi + 1)
                    if gi >= 2 and COMBPART >= 2:
                        pg = gi - 2
                        sc_.wait_ge(S["vC"], 5 * (pg // 4) + (pg % 4) + 2)
                    sc_.activation(ftm[gi % 2], PT[0], AF.Copy).then_inc(S["aC"], 1)

        @block.vector
        def _(ve):
            ve.wait_ge(S["gC"], 16 * 8)
            for ti in range(NLA):
                ve.wait_ge(S["sL"], 16 * (ti + 1))
                p = ti % 2
                for kk in range(4):
                    ve.bn_stats(stat6[p][:, 6 * kk:6 * kk + 6],
                                x_sb[p][:, 512 * kk:512 * (kk + 1)])
                for kk in range(4):
                    ve.bn_aggr(mvc[p][:, 2 * kk:2 * kk + 2], stat6[p][:, 6 * kk:6 * kk + 6])
                ve.tensor_add(pq8[p][:, 0:2], mvc[p][:, 0:2], mvc[p][:, 2:4])
                ve.tensor_add(pq8[p][:, 2:4], mvc[p][:, 4:6], mvc[p][:, 6:8])
                ve.tensor_mul(sq8[p][:], mvc[p][:], mvc[p][:])
                ve.tensor_add(pq8[p][:, 4:6], pq8[p][:, 0:2], pq8[p][:, 2:4])
                ve.tensor_add(tmp4[p][:, 0:2], sq8[p][:, 0:2], sq8[p][:, 2:4])
                ve.tensor_add(tmp4[p][:, 2:4], sq8[p][:, 4:6], sq8[p][:, 6:8])
                ve.tensor_scalar_mul(mv[p][:, 0:1], pq8[p][:, 4:5], 0.25)
                ve.tensor_add(pq8[p][:, 6:8], tmp4[p][:, 0:2], tmp4[p][:, 2:4])
                ve.tensor_mul(tmp4[p][:, 0:1], mv[p][:, 0:1], mv[p][:, 0:1])
                ve.tensor_add(pq8[p][:, 0:1], pq8[p][:, 5:6], pq8[p][:, 6:7])
                ve.tensor_copy(padb[:], gs_sb[:, 0:1])
                ve.scalar_tensor_tensor(tmp4[p][:, 1:2], pq8[p][:, 0:1], 0.25,
                                        tmp4[p][:, 0:1], ALU.mult, ALU.subtract)
                ve.tensor_copy(padb[:], gs_sb[:, 0:1])
                ve.tensor_scalar_add(veps[p][:], tmp4[p][:, 1:2], EPS_LN).then_inc(S["D0"], 1)
                ve.wait_ge(S["A0"], ti + 1)
                ve.reciprocal(rstd[ti % 2][:], stdv[ti % 2][:])
                if ti >= 2:
                    ve.wait_ge(S["sW"], 16 * (ti - 1))
                ve.tensor_copy(padb[:], gs_sb[:, 0:1])
                ve.tensor_scalar(xn_sb[ti % 2], x_sb[ti % 2], mv[ti % 2][:, 0:1],
                                 rstd[ti % 2][:], ALU.subtract, ALU.mult).then_inc(S["V0"], 1)
            for h in range(HPC if LEVEL >= 2 else 0):
                for sc in range(NSC):
                    Sg = h * NSC + sc
                    ve.wait_ge(S["sT"], 16 * 18 * (Sg + 1))
                    if sc == 0 and h >= 1:
                        ve.wait_ge(S["mV"], NT_H * h)
                    ve.wait_ge(S["aQ"], 4 * Sg + 1)
                    ve.wait_ge(S["mQ"], 6 * Sg + 4)
                    ve.tensor_mul(t1, q_sb, cs_buf[Sg % 2])
                    ve.tensor_mul(t2, P[3][:], sn_buf[Sg % 2])
                    ve.tensor_add(q_bf[:, sc * 512:(sc + 1) * 512], t1, t2).then_inc(S["vR"], 1)
                    ve.wait_ge(S["aQ"], 4 * Sg + 2)
                    ve.wait_ge(S["mQ"], 6 * Sg + 5)
                    ve.tensor_mul(t1, k_sb, cs_buf[Sg % 2])
                    ve.tensor_mul(t2, P[4][:], sn_buf[Sg % 2])
                    ve.tensor_add(k_bf[:, sc * 512:(sc + 1) * 512], t1, t2).then_inc(S["vR"], 1)
                T0 = NT_H * h
                for t, (qc, jb) in enumerate(ATT_TILES):
                    T = T0 + t
                    buf = T % 2
                    ve.wait_ge(S["aS"], T + 1)
                    if T >= 2:
                        ve.wait_ge(S["mV"], T - 1)
                    if jb >= 4 * qc:
                        r = jb - 4 * qc
                        ve.tensor_mul(s2[buf], s2[buf], masks[:, r * 512:(r + 1) * 512])
                    ve.tensor_mul(s4[buf], s2[buf], s2[buf]).then_inc(S["vS"], 1)
                    if jb == 4 * (qc + 1) - 1:
                        gq = h * NQC + qc
                        ve.wait_ge(S["mV"], T + 1)
                        if gq >= 2:
                            ve.wait_ge(S["sA"], 16 * (gq - 1))
                        ve.tensor_scalar_max(recip, P[1][:], EPS_AT)
                        ve.reciprocal(recip, recip)
                        ve.tensor_mul(e_out[gq % 2], P[2][:], recip).then_inc(S["vF"], 1)
            for ic in range(NIC if LEVEL >= 5 else 0):
                for tc in range(NTC):
                    k = ic * NTC + tc
                    pu = P[1] if k % 2 == 0 else P[3]
                    ve.wait_ge(S["mA"], 2 * k + 2)
                    ve.wait_ge(S["aA"], k + 1)
                    if k >= 2:
                        ve.wait_ge(S["sFu"], 16 * (k - 1))
                    ve.scalar_tensor_tensor(fuse_sb[k % 2], pu[:], us_sb[:, ic:ic + 1],
                                            g_s[k % 2], ALU.mult, ALU.mult).then_inc(S["vA"], 1)
            for tt in range(16 if (LEVEL >= 7 and COMBPART >= 2) else 0):
                ve.wait_ge(S["sX"], 32 * (tt + 1))
                if tt >= 2 and COMBPART >= 3:
                    ve.wait_ge(S["ST"], 16 * (tt - 1))
                ve.tensor_add(ycomb[tt % 2], xcomb[tt % 2], rcomb[tt % 2]).then_inc(S["vC"], 1)
                for fcg in range(4):
                    gi = 4 * tt + fcg
                    ve.wait_ge(S["aC"], gi + 1)
                    ve.tensor_add(ycomb[tt % 2][:, fcg * 512:(fcg + 1) * 512],
                                  ycomb[tt % 2][:, fcg * 512:(fcg + 1) * 512],
                                  ftm[gi % 2]).then_inc(S["vC"], 1)

    return nc



def _layout_inputs(inputs):
    x = np.ascontiguousarray(np.asarray(inputs["L0_attn_x"], dtype=np.float32))
    Wq = np.asarray(inputs["L0_attn_Wq"], dtype=np.float32)
    Wk = np.asarray(inputs["L0_attn_Wk"], dtype=np.float32)
    Wv = np.asarray(inputs["L0_attn_Wv"], dtype=np.float32)
    Wo = np.asarray(inputs["L0_attn_Wo"], dtype=np.float32)
    gwf = np.asarray(inputs["L0_ffn_gate_w"], dtype=np.float32)
    uwf = np.asarray(inputs["L0_ffn_up_w"], dtype=np.float32)
    dwf = np.asarray(inputs["L0_ffn_down_w"], dtype=np.float32)
    gs = np.asarray(inputs["L0_ffn_gate_scale"], dtype=np.float32)
    us = np.asarray(inputs["L0_ffn_up_scale"], dtype=np.float32)
    ds = np.asarray(inputs["L0_ffn_down_scale"], dtype=np.float32)

    gw_t = np.ascontiguousarray(gwf.reshape(16, 128, 48, 128).transpose(2, 1, 0, 3))
    uw_t = np.ascontiguousarray(uwf.reshape(16, 128, 48, 128).transpose(2, 1, 0, 3))
    dw_t = np.ascontiguousarray(dwf.reshape(48, 128, 16, 128))
    gs_t = np.ascontiguousarray(gs.reshape(48, 128).T)
    us_t = np.ascontiguousarray(us.reshape(48, 128).T)
    ds_t = np.ascontiguousarray(ds.reshape(16, 128).T)

    half = 64
    theta = np.power(np.float32(10000.0),
                     -(np.arange(half, dtype=np.float32) * np.float32(2.0) / np.float32(128)))
    pos = np.arange(N, dtype=np.float32)
    ang = (theta[:, None] * pos[None, :]).astype(np.float32)
    cos = np.cos(ang).astype(np.float32)
    sin = np.sin(ang).astype(np.float32)
    cst = np.ascontiguousarray(np.vstack([cos, cos]))
    snt = np.ascontiguousarray(np.vstack([-sin, sin]))

    ones = np.ones((128, 128), dtype=ml_dtypes.bfloat16)
    eyeb = np.eye(128, dtype=ml_dtypes.bfloat16)
    eyef = np.eye(128, dtype=np.float32)
    swp = np.zeros((128, 128), dtype=np.float32)
    swp[(np.arange(128) + 64) % 128, np.arange(128)] = 1.0
    msk = np.zeros((4, 128, 512), dtype=ml_dtypes.bfloat16)
    jj = np.arange(128)[:, None]
    qq = np.arange(512)[None, :]
    for r in range(4):
        msk[r] = ((jj + 128 * r) <= qq).astype(ml_dtypes.bfloat16)

    in_maps = []
    for c in range(NCORE_G):
        b = c // 4
        l = c % 4
        wq_c = np.ascontiguousarray(
            Wq[512 * l:512 * (l + 1), :].T.reshape(16, 128, 4, 128).transpose(2, 1, 0, 3))
        wk_c = np.ascontiguousarray(
            Wk[512 * l:512 * (l + 1), :].T.reshape(16, 128, 4, 128).transpose(2, 1, 0, 3))
        wv_c = np.ascontiguousarray(
            Wv[512 * l:512 * (l + 1), :].T.reshape(16, 128, 4, 128).transpose(2, 1, 0, 3))
        wo_c = np.ascontiguousarray(
            Wo.T[512 * l:512 * (l + 1), :].reshape(4, 128, 2048).transpose(1, 0, 2))
        in_maps.append({
            "xb": x[b], "x_own": np.ascontiguousarray(x[b, 2048 * l:2048 * (l + 1)]),
            "wq": wq_c, "wk": wk_c, "wv": wv_c, "wo": wo_c,
            "gw": gw_t, "uw": uw_t, "dwt": dw_t,
            "gsc": gs_t, "usc": us_t, "dsc": ds_t,
            "cst": cst, "snt": snt,
            "onesd": ones, "eyebd": eyeb, "eyefd": eyef, "swpd": swp,
            "maskd": msk,
        })
    return in_maps


NCORE_G = 8


def kernel(**inputs):
    in_maps = _layout_inputs(inputs)
    nc = build()
    res = run_bass_kernel_spmd(nc, in_maps, core_ids=list(range(NCORE_G)))
    y = np.empty((B, N, DM), dtype=np.float32)
    for c in range(NCORE_G):
        b = c // 4
        l = c % 4
        y[b, 2048 * l:2048 * (l + 1), :] = res.results[c]["y_own"]
    return y


# version 07
# speedup vs baseline: 1.5814x; 1.0076x over previous
import numpy as np
import ml_dtypes
import concourse.bass as bass
import concourse.mybir as mybir
from concourse.bass_utils import run_bass_kernel_spmd

F32 = mybir.dt.float32
BF16 = mybir.dt.bfloat16
FP8 = mybir.dt.float8e4
AF = mybir.ActivationFunctionType
ALU = mybir.AluOpType
AX = mybir.AxisListType

B, N, DM = 2, 8192, 2048
DI = 6144
HPC = 4
OWN = 2048
EPS_LN = 1e-5
EPS_AT = 1e-6

NLT = N // 128
NLA = NLT + OWN // 128
NSC = N // 512
NQC = N // 512
NIC = DI // 128
NTC = OWN // 512
NWTC = N // 512
NEC = DM // 128

ATT_TILES = [(qc, jb) for qc in range(NQC) for jb in range(4 * (qc + 1))]
NT_H = len(ATT_TILES)
FB_PASSES = [(0, 6), (6, 12), (12, 16)]

DEBUG = ()
LEVEL = 7
WOPART = 3
SKIPRS = False
COMBPART = 3


def build():
    nc = bass.Bass("TRN2", target_bir_lowering=False, debug=False)
    xb = nc.dram_tensor("xb", [N, DM], F32, kind="ExternalInput").ap()
    x_own = nc.dram_tensor("x_own", [OWN, DM], F32, kind="ExternalInput").ap()
    wq = nc.dram_tensor("wq", [HPC, 128, NEC, 128], F32, kind="ExternalInput").ap()
    wk = nc.dram_tensor("wk", [HPC, 128, NEC, 128], F32, kind="ExternalInput").ap()
    wv = nc.dram_tensor("wv", [HPC, 128, NEC, 128], F32, kind="ExternalInput").ap()
    wo = nc.dram_tensor("wo", [128, HPC, DM], F32, kind="ExternalInput").ap()
    gw = nc.dram_tensor("gw", [NIC, 128, NEC, 128], F32, kind="ExternalInput").ap()
    uw = nc.dram_tensor("uw", [NIC, 128, NEC, 128], F32, kind="ExternalInput").ap()
    dwt = nc.dram_tensor("dwt", [NIC, 128, NEC, 128], F32, kind="ExternalInput").ap()
    gsc = nc.dram_tensor("gsc", [128, NIC], F32, kind="ExternalInput").ap()
    usc = nc.dram_tensor("usc", [128, NIC], F32, kind="ExternalInput").ap()
    dsc = nc.dram_tensor("dsc", [128, NEC], F32, kind="ExternalInput").ap()
    cst = nc.dram_tensor("cst", [128, N], F32, kind="ExternalInput").ap()
    snt = nc.dram_tensor("snt", [128, N], F32, kind="ExternalInput").ap()
    onesd = nc.dram_tensor("onesd", [128, 128], BF16, kind="ExternalInput").ap()
    eyebd = nc.dram_tensor("eyebd", [128, 128], BF16, kind="ExternalInput").ap()
    eyefd = nc.dram_tensor("eyefd", [128, 128], F32, kind="ExternalInput").ap()
    swpd = nc.dram_tensor("swpd", [128, 128], F32, kind="ExternalInput").ap()
    maskd = nc.dram_tensor("maskd", [4, 128, 512], BF16, kind="ExternalInput").ap()

    y_own = nc.dram_tensor("y_own", [OWN, DM], F32, kind="ExternalOutput").ap()

    xn_sc = nc.dram_tensor("xn_sc", [N, DM], BF16, kind="Internal").ap()
    xno_sc = nc.dram_tensor("xno_sc", [OWN, DM], BF16, kind="Internal").ap()
    od_sc = nc.dram_tensor("od_sc", [HPC, 128, N], BF16, kind="Internal").ap()
    fuse_sc = nc.dram_tensor("fuse_sc", [NIC, 128, OWN], BF16, kind="Internal").ap()
    dw8_sc = nc.dram_tensor("dw8_sc", [NIC, 128, DM], FP8, kind="Internal").ap()
    gw8_sc = nc.dram_tensor("gw8_sc", [NIC, 128, DM], FP8, kind="Internal").ap()
    uw8_sc = nc.dram_tensor("uw8_sc", [NIC, 128, DM], FP8, kind="Internal").ap()
    rs_in_t = nc.dram_tensor("rs_in", [N, DM], BF16, kind="Internal")
    rs_out_t = nc.dram_tensor("rs_out", [OWN, DM], BF16, kind="Internal")
    rs_in = rs_in_t.ap()
    rs_out = rs_out_t.ap()

    dbg = {}
    if "xn" in DEBUG:
        dbg["d_xn"] = nc.dram_tensor("d_xn", [N, DM], BF16, kind="ExternalOutput").ap()
    if "od" in DEBUG:
        dbg["d_od"] = nc.dram_tensor("d_od", [HPC * 128, N], BF16, kind="ExternalOutput").ap()
    if "rsin" in DEBUG:
        dbg["d_rsin"] = nc.dram_tensor("d_rsin", [N, DM], BF16, kind="ExternalOutput").ap()
    if "rsout" in DEBUG:
        dbg["d_rsout"] = nc.dram_tensor("d_rsout", [OWN, DM], BF16, kind="ExternalOutput").ap()
    if "fuse" in DEBUG:
        dbg["d_fuse"] = nc.dram_tensor("d_fuse", [NIC * 128, OWN], BF16, kind="ExternalOutput").ap()
    NDBG = len(dbg)

    from contextlib import ExitStack
    with ExitStack() as stack:
        en = stack.enter_context

        A = en(nc.sbuf_tensor("A", [128, 32768], BF16))
        Af = A.bitcast(F32)
        q_bf = A[:, 0:8192]
        k_bf = A[:, 8192:16384]
        v_tm = A[:, 16384:24576]
        xnoT = [A[:, ec * 2048:(ec + 1) * 2048] for ec in range(NEC)]
        q_sb = Af[:, 12288:12800]
        k_sb = Af[:, 12800:13312]
        t1 = Af[:, 13312:13824]
        t2 = Af[:, 13824:14336]
        v_tmp = [A[:, 28672 + i * 512:28672 + (i + 1) * 512] for i in range(2)]
        s2 = [Af[:, 12288:12800], Af[:, 12800:13312]]
        s4 = [A[:, 26624 + i * 512:26624 + (i + 1) * 512] for i in range(2)]
        recip = Af[:, 14848:15360]
        e_out = [A[:, 30720 + i * 512:30720 + (i + 1) * 512] for i in range(2)]

        Bx = en(nc.sbuf_tensor("Bx", [128, 32768], BF16))
        Bf = Bx.bitcast(F32)
        xnt = [[Bx[:, bi * 8192 + ec * 512: bi * 8192 + (ec + 1) * 512]
                for ec in range(NEC)] for bi in range(2)]
        cs_buf = [Bf[:, 8192 + i * 512:8192 + (i + 1) * 512] for i in range(2)]
        sn_buf = [Bf[:, 9216 + i * 512:9216 + (i + 1) * 512] for i in range(2)]
        wq_sb = [Bx[:, 20480 + bi * 6144:20480 + bi * 6144 + 2048] for bi in range(2)]
        wk_sb = [Bx[:, 22528 + bi * 6144:22528 + bi * 6144 + 2048] for bi in range(2)]
        wv_sb = [Bx[:, 24576 + bi * 6144:24576 + bi * 6144 + 2048] for bi in range(2)]
        attn_T = [Bf[:, fc * 512:(fc + 1) * 512] for fc in range(16)]
        stage = [Bx[:, 16384 + i * 2048:16384 + (i + 1) * 2048] for i in range(2)]
        wo_sb = Bx[:, 24576:32768]
        ffn_T = [Bx[:, hc * 2048:(hc + 1) * 2048] for hc in range(NEC)]

        C = en(nc.sbuf_tensor("C", [128, 24576], BF16))
        Cf = C.bitcast(F32)
        x_sb = [Cf[:, i * 2048:(i + 1) * 2048] for i in range(2)]
        xn_sb = [C[:, 8192 + i * 2048:8192 + (i + 1) * 2048] for i in range(2)]
        od_buf = [C[:, i * 2048:(i + 1) * 2048] for i in range(2)]
        g_s = [Cf[:, 1024 + i * 512:1024 + (i + 1) * 512] for i in range(2)]
        fuse_sb = [C[:, 4096 + i * 512:4096 + (i + 1) * 512] for i in range(2)]
        fuse_tc = [C[:, ic * 512:(ic + 1) * 512] for ic in range(NIC)]
        xcomb = [Af[:, i * 2048:(i + 1) * 2048] for i in range(2)]
        rcomb = [A[:, 8192 + i * 2048:8192 + (i + 1) * 2048] for i in range(2)]
        ycomb = [Af[:, 6144 + i * 2048:6144 + (i + 1) * 2048] for i in range(2)]
        ftm = [A[:, 28672 + i * 512:28672 + (i + 1) * 512] for i in range(2)]

        masks = en(nc.sbuf_tensor("masks", [128, 2048], BF16))
        gu8t = en(nc.sbuf_tensor("gu8t", [128, 8192], FP8))
        gu8 = [[gu8t[:, (bi * 2 + j) * 2048:(bi * 2 + j + 1) * 2048] for j in range(2)]
               for bi in range(2)]
        dw8t = en(nc.sbuf_tensor("dw8t", [128, 6144], FP8))
        dw8b = [dw8t[:, i * 3072:(i + 1) * 3072] for i in range(2)]
        dw8p = en(nc.sbuf_tensor("dw8p", [128, 2048], FP8))
        pp8 = [en(nc.sbuf_tensor(f"pp8_{i}", [128, 2048], FP8)) for i in range(2)]
        ones_s = en(nc.sbuf_tensor("ones_s", [128, 128], BF16))
        eyeb_s = en(nc.sbuf_tensor("eyeb_s", [128, 128], BF16))
        eyef_s = en(nc.sbuf_tensor("eyef_s", [128, 128], F32))
        swp_s = en(nc.sbuf_tensor("swp_s", [128, 128], F32))
        gs_sb = en(nc.sbuf_tensor("gs_sb", [128, NIC], F32))
        us_sb = en(nc.sbuf_tensor("us_sb", [128, NIC], F32))
        ds_sb = en(nc.sbuf_tensor("ds_sb", [128, NEC], F32))
        stat6 = [en(nc.sbuf_tensor(f"stat6_{i}", [128, 24], F32)) for i in range(2)]
        mvc = [en(nc.sbuf_tensor(f"mvc_{i}", [128, 8], F32)) for i in range(2)]
        tmp4 = [en(nc.sbuf_tensor(f"tmp4_{i}", [128, 4], F32)) for i in range(2)]
        pq8 = [en(nc.sbuf_tensor(f"pq8_{i}", [128, 8], F32)) for i in range(2)]
        sq8 = [en(nc.sbuf_tensor(f"sq8_{i}", [128, 8], F32)) for i in range(2)]
        padb = en(nc.sbuf_tensor("padb", [128, 1], F32))
        onesf = en(nc.sbuf_tensor("onesf", [1, 128], F32))
        mv = [en(nc.sbuf_tensor(f"mv_{i}", [128, 2], F32)) for i in range(2)]
        veps = [en(nc.sbuf_tensor(f"veps_{i}", [128, 1], F32)) for i in range(2)]
        stdv = [en(nc.sbuf_tensor(f"stdv_{i}", [128, 1], F32)) for i in range(2)]
        rstd = [en(nc.sbuf_tensor(f"rstd_{i}", [128, 1], F32)) for i in range(2)]

        P = [en(nc.psum_tensor(f"P{i}", [128, 512], F32)) for i in range(7)]
        PTt = en(nc.psum_tensor("PTt", [128, 1024], BF16))
        PT = [PTt[:, 0:512], PTt[:, 512:1024]]

        names = ["sL", "sW", "V0", "D0", "A0", "sT", "gQ", "mQ", "aQ", "vR",
                 "mS", "aS", "vS", "mV", "vF", "sA", "gWo", "gO", "mW", "aW",
                 "mT", "aG", "sR", "CC", "gD", "gF", "sTF", "mA", "aA", "vA",
                 "sFu", "sFB", "gB", "mBc", "aB", "sX", "mC", "vC",
                 "ST", "gC", "aC", "vF1", "mB1", "aB1"]
        S = {nm: en(nc.semaphore(nm)) for nm in names}
        block = en(nc.Block())

        def ln_src(ti):
            if ti < NLT:
                return xb[ti * 128:(ti + 1) * 128, :]
            t = ti - NLT
            return x_own[t * 128:(t + 1) * 128, :]

        def ln_dst(ti):
            if ti < NLT:
                return xn_sc[ti * 128:(ti + 1) * 128, :]
            t = ti - NLT
            return xno_sc[t * 128:(t + 1) * 128, :]

        @block.sync
        def _(sync):
            sync.dma_start(out=x_sb[0], in_=ln_src(0)).then_inc(S["sL"], 16)
            sync.dma_start(out=x_sb[1], in_=ln_src(1)).then_inc(S["sL"], 16)
            for ti in range(NLA):
                sync.wait_ge(S["V0"], ti + 1)
                sync.dma_start(out=ln_dst(ti), in_=xn_sb[ti % 2]).then_inc(S["sW"], 16)
                if ti + 2 < NLA:
                    sync.dma_start(out=x_sb[ti % 2], in_=ln_src(ti + 2)).then_inc(S["sL"], 16)
            for h in range(HPC if LEVEL >= 2 else 0):
                for sc in range(NSC):
                    Sg = h * NSC + sc
                    sync.wait_ge(S["sW"], 16 * min(4 * sc + 4, NLT))
                    if Sg >= 2:
                        sync.wait_ge(S["mQ"], 6 * (Sg - 2) + 3)
                        sync.wait_ge(S["vR"], 2 * (Sg - 1))
                    for ec in range(NEC):
                        sync.dma_start(
                            out=xnt[Sg % 2][ec],
                            in_=xn_sc[sc * 512:(sc + 1) * 512, ec * 128:(ec + 1) * 128],
                            transpose=True).then_inc(S["sT"], 16)
                    sync.dma_start(out=cs_buf[Sg % 2],
                                   in_=cst[:, sc * 512:(sc + 1) * 512]).then_inc(S["sT"], 16)
                    sync.dma_start(out=sn_buf[Sg % 2],
                                   in_=snt[:, sc * 512:(sc + 1) * 512]).then_inc(S["sT"], 16)
                for qc in range(NQC):
                    g = h * NQC + qc
                    sync.wait_ge(S["vF"], g + 1)
                    sync.dma_start(out=od_sc[h, :, qc * 512:(qc + 1) * 512],
                                   in_=e_out[g % 2]).then_inc(S["sA"], 16)
            if LEVEL < 2:
                sync.wait_ge(S["sW"], 16 * NLA)
            if LEVEL >= 5:
                sync.wait_ge(S["vF"], HPC * NQC)
            if LEVEL >= 5:
                sync.wait_ge(S["sA"], 16 * HPC * NQC)
                sync.wait_ge(S["sW"], 16 * NLA)
                for ec in range(NEC):
                    sync.dma_start(out=xnoT[ec], in_=xno_sc[:, ec * 128:(ec + 1) * 128],
                                   transpose=True).then_inc(S["sTF"], 16)
            for tc in range(NWTC if (LEVEL >= 3 and WOPART >= 3) else 0):
                for tt in range(4):
                    TT = 4 * tc + tt
                    sync.wait_ge(S["aG"], 4 * TT + 4)
                    sync.dma_start(out=rs_in[TT * 128:(TT + 1) * 128, :],
                                   in_=stage[TT % 2]).then_inc(S["sR"], 16)
            if LEVEL >= 5:
                sync.wait_ge(S["gD"], 32 * 9 * NWTC)
                if LEVEL >= 4 and not SKIPRS:
                    sync.wait_ge(S["CC"], 1)
                for ic0 in (0, 1):
                    sync.dma_start(out=gu8[ic0 % 2][0], in_=gw8_sc[ic0]).then_inc(S["gF"], 16)
                    sync.dma_start(out=gu8[ic0 % 2][1], in_=uw8_sc[ic0]).then_inc(S["gF"], 16)
            for ic in range(NIC if LEVEL >= 5 else 0):
                for tc in range(NTC):
                    k = ic * NTC + tc
                    sync.wait_ge(S["vA"], k + 1)
                    sync.dma_start(out=fuse_sc[ic, :, tc * 512:(tc + 1) * 512],
                                   in_=fuse_sb[k % 2]).then_inc(S["sFu"], 16)
                if ic + 2 < NIC:
                    sync.dma_start(out=gu8[ic % 2][0], in_=gw8_sc[ic + 2]).then_inc(S["gF"], 16)
                    sync.dma_start(out=gu8[ic % 2][1], in_=uw8_sc[ic + 2]).then_inc(S["gF"], 16)
            if LEVEL >= 6:
                sync.wait_ge(S["sFu"], 16 * NIC * NTC)
            for tc in range(NTC if LEVEL >= 6 else 0):
                if tc > 0:
                    sync.wait_ge(S["mBc"], 36 * tc)
                for cg in range(6):
                    ic0 = cg * 8
                    sync.dma_start(
                        out=C[:, ic0 * 512:(ic0 + 8) * 512].rearrange("p (i t) -> p i t", t=512),
                        in_=fuse_sc[ic0:ic0 + 8, :, tc * 512:(tc + 1) * 512].rearrange("i p t -> p i t"),
                    ).then_inc(S["sFB"], 16)
            if LEVEL >= 7:
                sync.wait_ge(S["mA"], 2 * NIC * NTC)
                sync.wait_ge(S["CC"], 1)
            for tt in ((0, 1) if (LEVEL >= 7 and COMBPART >= 2) else ()):
                sync.dma_start(out=xcomb[tt], in_=x_own[tt * 128:(tt + 1) * 128, :]).then_inc(S["sX"], 16)
                sync.dma_start(out=rcomb[tt], in_=rs_out[tt * 128:(tt + 1) * 128, :]).then_inc(S["sX"], 16)
            for tt in range(16 if (LEVEL >= 7 and COMBPART >= 3) else 0):
                sync.wait_ge(S["vC"], 5 * (tt + 1))
                sync.dma_start(out=y_own[tt * 128:(tt + 1) * 128, :],
                               in_=ycomb[tt % 2]).then_inc(S["ST"], 16)
                if tt + 2 < 16:
                    sync.dma_start(out=xcomb[tt % 2],
                                   in_=x_own[(tt + 2) * 128:(tt + 3) * 128, :]).then_inc(S["sX"], 16)
                    sync.dma_start(out=rcomb[tt % 2],
                                   in_=rs_out[(tt + 2) * 128:(tt + 3) * 128, :]).then_inc(S["sX"], 16)
            nst = 16 if (LEVEL >= 7 and COMBPART >= 3) else 0
            if LEVEL >= 7 and COMBPART == 0:
                sync.wait_ge(S["mC"], 64)
            elif LEVEL >= 7 and COMBPART == 1:
                sync.wait_ge(S["aC"], 64)
            elif LEVEL >= 7 and COMBPART == 2:
                sync.wait_ge(S["vC"], 80)
            if LEVEL < 7:
                if LEVEL == 1:
                    sync.wait_ge(S["sW"], 16 * NLA)
                elif LEVEL == 2:
                    sync.wait_ge(S["sA"], 16 * HPC * NQC)
                elif LEVEL == 3:
                    if WOPART >= 3:
                        sync.wait_ge(S["sR"], 16 * NWTC * 4)
                    elif WOPART == 2:
                        sync.wait_ge(S["mT"], 256)
                    else:
                        sync.wait_ge(S["aW"], 256)
                elif LEVEL == 4:
                    sync.wait_ge(S["CC"], 1)
                elif LEVEL == 5:
                    sync.wait_ge(S["sFu"], 16 * NIC * NTC)
                elif LEVEL == 6:
                    sync.wait_ge(S["aB"], 3 * NTC)
            if "xn" in DEBUG:
                sync.dma_start(out=dbg["d_xn"], in_=xn_sc).then_inc(S["ST"], 16)
                nst += 1
            if "od" in DEBUG:
                sync.dma_start(out=dbg["d_od"], in_=od_sc.rearrange("h p t -> (h p) t")).then_inc(S["ST"], 16)
                nst += 1
            if "rsin" in DEBUG:
                sync.dma_start(out=dbg["d_rsin"], in_=rs_in).then_inc(S["ST"], 16)
                nst += 1
            if "rsout" in DEBUG:
                sync.dma_start(out=dbg["d_rsout"], in_=rs_out).then_inc(S["ST"], 16)
                nst += 1
            if "fuse" in DEBUG:
                sync.dma_start(out=dbg["d_fuse"], in_=fuse_sc.rearrange("i p t -> (i p) t")).then_inc(S["ST"], 16)
                nst += 1
            sync.wait_ge(S["ST"], 16 * nst)

        @block.gpsimd
        def _(g):
            g.dma_start(out=ones_s[:], in_=onesd).then_inc(S["gC"], 16)
            g.dma_start(out=eyeb_s[:], in_=eyebd).then_inc(S["gC"], 16)
            g.dma_start(out=eyef_s[:], in_=eyefd).then_inc(S["gC"], 16)
            g.dma_start(out=swp_s[:], in_=swpd).then_inc(S["gC"], 16)
            g.dma_start(out=masks[:].rearrange("p (r q) -> p r q", q=512),
                        in_=maskd.rearrange("r p q -> p r q")).then_inc(S["gC"], 16)
            g.dma_start(out=gs_sb[:], in_=gsc).then_inc(S["gC"], 16)
            g.dma_start(out=us_sb[:], in_=usc).then_inc(S["gC"], 16)
            g.dma_start(out=ds_sb[:], in_=dsc).then_inc(S["gC"], 16)
            for h in range(HPC if LEVEL >= 2 else 0):
                if h >= 2:
                    g.wait_ge(S["mQ"], 6 * NSC * (h - 1))
                for wsrc, wdst in [(wq, wq_sb), (wk, wk_sb), (wv, wv_sb)]:
                    g.dma_start(out=wdst[h % 2].rearrange("p (ec d) -> p ec d", d=128),
                                in_=wsrc[h]).then_inc(S["gQ"], 16)
            if LEVEL >= 3:
                g.wait_ge(S["vF"], HPC * NQC)
                g.dma_start(out=wo_sb.rearrange("p (dh f) -> p dh f", f=DM),
                            in_=wo).then_inc(S["gWo"], 16)
            for tc in range(NWTC if LEVEL >= 3 else 0):
                if tc == 0:
                    g.wait_ge(S["sA"], 16 * HPC * NQC)
                if tc >= 2:
                    g.wait_ge(S["mW"], 16 * (tc - 1))
                g.dma_start(out=od_buf[tc % 2].rearrange("p (dh t) -> p dh t", t=512),
                            in_=od_sc[:, :, tc * 512:(tc + 1) * 512].rearrange("h p t -> p h t"),
                            ).then_inc(S["gO"], 16)
                for j in range(9 if LEVEL >= 5 else 0):
                    p = tc * 9 + j
                    ic = p // 3
                    srcw, dstw = ((dwt, dw8_sc), (gw, gw8_sc), (uw, uw8_sc))[p % 3]
                    if p >= 2:
                        g.wait_ge(S["gD"], 32 * (p - 1))
                    g.dma_start(out=pp8[p % 2][:],
                                in_=srcw[ic].rearrange("p a b -> p (a b)")).then_inc(S["gD"], 16)
                    g.wait_ge(S["gD"], 32 * p + 16)
                    g.dma_start(out=dstw[ic], in_=pp8[p % 2][:]).then_inc(S["gD"], 16)

            if LEVEL >= 4 and not SKIPRS:
                g.wait_ge(S["sR"], 16 * NWTC * 4)
                if LEVEL >= 5:
                    g.wait_ge(S["gD"], 32 * 9 * NWTC)
                g.collective_compute(
                    "ReduceScatter", ALU.add,
                    replica_groups=[[0, 1, 2, 3], [4, 5, 6, 7]],
                    ins=[rs_in_t.ap().opt()], outs=[rs_out_t.ap().opt()],
                ).then_inc(S["CC"], 1)
            if LEVEL >= 6:
                g.wait_ge(S["gD"], 32 * 9 * NWTC)
                g.wait_ge(S["CC"], 1)
            for tc in range(NTC if LEVEL >= 6 else 0):
                for pi, (h0, h1) in enumerate(FB_PASSES):
                    gidx = tc * 3 + pi
                    w = (h1 - h0) * 128
                    for cg in range(12):
                        CH = gidx * 12 + cg
                        if CH >= 2:
                            g.wait_ge(S["mBc"], CH - 1)
                        g.dma_start(
                            out=dw8b[CH % 2][:, 0:4 * w].rearrange("p (i t) -> p i t", t=w),
                            in_=dw8_sc[cg * 4:cg * 4 + 4, :, h0 * 128:h1 * 128].rearrange("i p t -> p i t"),
                        ).then_inc(S["gB"], 16)

        @block.tensor
        def _(te):
            te.wait_ge(S["gC"], 16 * 8)
            for h in range(HPC if LEVEL >= 2 else 0):
                hb = h % 2
                for sc in range(NSC):
                    Sg = h * NSC + sc
                    te.wait_ge(S["sT"], 16 * 18 * (Sg + 1))
                    te.wait_ge(S["gQ"], 16 * 3 * (h + 1))
                    if h >= 1 and sc == 0:
                        te.wait_ge(S["vF"], NQC * h)
                    if Sg >= 1:
                        te.wait_ge(S["aQ"], 4 * (Sg - 1) + 3)
                    for wsb, pb in ((wq_sb, P[0]), (wk_sb, P[1]), (wv_sb, P[2])):
                        for ec in range(NEC):
                            ins = te.matmul(pb[:], wsb[hb][:, ec * 128:(ec + 1) * 128],
                                            xnt[Sg % 2][ec], start=(ec == 0), stop=(ec == NEC - 1))
                        ins.then_inc(S["mQ"], 1)
                    te.wait_ge(S["aQ"], 4 * Sg + 1)
                    if Sg >= 1:
                        te.wait_ge(S["vR"], 2 * (Sg - 1) + 1)
                    te.matmul(P[3][:], swp_s[:], q_sb, start=True, stop=True).then_inc(S["mQ"], 1)
                    te.wait_ge(S["aQ"], 4 * Sg + 2)
                    if Sg >= 1:
                        te.wait_ge(S["vR"], 2 * Sg)
                    te.matmul(P[4][:], swp_s[:], k_sb, start=True, stop=True).then_inc(S["mQ"], 1)
                    te.wait_ge(S["aQ"], 4 * Sg + 3)
                    if Sg >= 1:
                        te.wait_ge(S["aQ"], 4 * (Sg - 1) + 4)
                    for j in range(4):
                        ins = te.transpose(PT[0][:, j * 128:(j + 1) * 128],
                                           v_tmp[Sg % 2][:, j * 128:(j + 1) * 128],
                                           eyeb_s[:])
                    ins.then_inc(S["mQ"], 1)
                T0 = NT_H * h
                def emit_st(t, h=h):
                    T = T0 + t
                    qc, jb = ATT_TILES[t]
                    if t == 0:
                        te.wait_ge(S["vR"], 2 * NSC * (h + 1))
                    if T >= 3:
                        te.wait_ge(S["aS"], T - 2)
                    te.wait_ge(S["vR"], 2 * NSC * h + 2 * (qc + 1))
                    te.matmul(P[4 + T % 3][:], k_bf[:, jb * 128:(jb + 1) * 128],
                              q_bf[:, qc * 512:(qc + 1) * 512],
                              start=True, stop=True).then_inc(S["mS"], 1)
                emit_st(0)
                emit_st(1)
                for t, (qc, jb) in enumerate(ATT_TILES):
                    T = T0 + t
                    buf = T % 2
                    if t + 2 < NT_H:
                        emit_st(t + 2)
                    te.wait_ge(S["vS"], T + 1)
                    first = (jb == 0)
                    last = (jb == 4 * (qc + 1) - 1)
                    if first:
                        if h * NQC + qc >= 2:
                            te.wait_ge(S["vF"], h * NQC + qc - 1)
                        if qc == 0:
                            te.wait_ge(S["aQ"], 64 * h + 63)
                    if jb % 4 == 0:
                        te.wait_ge(S["aQ"], 64 * h + 4 * (jb // 4 + 1))
                    gq = h * NQC + qc
                    psv = P[2] if gq % 2 == 0 else P[0]
                    pdn = P[1] if gq % 2 == 0 else P[3]
                    te.matmul(psv[:], v_tm[:, jb * 128:(jb + 1) * 128], s4[buf],
                              start=first, stop=last)
                    te.matmul(pdn[:], ones_s[:], s4[buf],
                              start=first, stop=last).then_inc(S["mV"], 1)
            if LEVEL >= 3:
                te.wait_ge(S["vF"], HPC * NQC)
                te.wait_ge(S["gWo"], 16)
            for tc in range(NWTC if LEVEL >= 3 else 0):
                te.wait_ge(S["gO"], 16 * (tc + 1))
                for fc in range(16):
                    idx = 16 * tc + fc
                    bank = P[fc % 2]
                    if idx >= 2:
                        te.wait_ge(S["aW"], idx - 1)
                    for dh in range(HPC):
                        ins = te.matmul(bank[:],
                                        wo_sb[:, dh * 2048 + fc * 128: dh * 2048 + fc * 128 + 128],
                                        od_buf[tc % 2][:, dh * 512:(dh + 1) * 512],
                                        start=(dh == 0), stop=(dh == HPC - 1))
                    ins.then_inc(S["mW"], 1)
                if WOPART < 2:
                    continue
                te.wait_ge(S["aW"], 16 * tc + 16)
                for tt in range(4):
                    for fcg in range(4):
                        gi = 16 * tc + 4 * tt + fcg
                        if gi >= 1 and WOPART >= 3:
                            te.wait_ge(S["aG"], gi)
                        for j in range(4):
                            fc = 4 * fcg + j
                            ins = te.transpose(P[2][:, j * 128:(j + 1) * 128],
                                               attn_T[fc][:, tt * 128:(tt + 1) * 128],
                                               eyef_s[:])
                        ins.then_inc(S["mT"], 1)
            if LEVEL >= 5:
                te.wait_ge(S["sTF"], 16 * NEC)
                if LEVEL >= 3:
                    te.wait_ge(S["aW"], 256)
                    te.wait_ge(S["aG"], 256)
            for ic in range(NIC if LEVEL >= 5 else 0):
                te.wait_ge(S["gF"], 32 * (ic + 1))
                for tc in range(NTC):
                    k = ic * NTC + tc
                    pg, pu = (P[0], P[1]) if k % 2 == 0 else (P[2], P[3])
                    if k >= 2:
                        te.wait_ge(S["aA"], k - 1)
                        te.wait_ge(S["vA"], k - 1)
                    for gj, pb in ((0, pg), (1, pu)):
                        for ec in range(NEC):
                            ins = te.matmul(pb[:], gu8[ic % 2][gj][:, ec * 128:(ec + 1) * 128],
                                            xnoT[ec][:, tc * 512:(tc + 1) * 512],
                                            start=(ec == 0), stop=(ec == NEC - 1))
                        ins.then_inc(S["mA"], 1)
            for tc in range(NTC if LEVEL >= 6 else 0):
                te.wait_ge(S["sFB"], 16 * 6 * (tc + 1))
                for pi, (h0, h1) in enumerate(FB_PASSES):
                    gidx = tc * 3 + pi
                    nh = h1 - h0
                    w = nh * 128
                    if gidx == 0:
                        te.wait_ge(S["aA"], NIC * NTC)
                        te.wait_ge(S["vA"], NIC * NTC)
                    else:
                        te.wait_ge(S["aB"], gidx)
                    for cg in range(12):
                        CH = gidx * 12 + cg
                        te.wait_ge(S["gB"], 16 * (CH + 1))
                        for j in range(4):
                            ic = cg * 4 + j
                            for hi in range(nh):
                                ins = te.matmul(P[hi][:],
                                                dw8b[CH % 2][:, j * w + hi * 128: j * w + hi * 128 + 128],
                                                fuse_tc[ic],
                                                start=(ic == 0), stop=(ic == NIC - 1))
                        ins.then_inc(S["mBc"], 1)
            if LEVEL >= 7:
                te.wait_ge(S["aB"], 3 * NTC)
            for tt in range(16 if LEVEL >= 7 else 0):
                for fcg in range(4):
                    gi = 4 * tt + fcg
                    pt = PT[0]
                    if gi >= 1 and COMBPART >= 1:
                        te.wait_ge(S["aC"], gi)
                    for j in range(4):
                        hc = fcg * 4 + j
                        ins = te.transpose(pt[:, j * 128:(j + 1) * 128],
                                           ffn_T[hc][:, tt * 128:(tt + 1) * 128],
                                           eyeb_s[:])
                    ins.then_inc(S["mC"], 1)

        @block.scalar
        def _(sc_):
            sc_.wait_ge(S["gC"], 16 * 8)
            for ti in range(NLA):
                sc_.wait_ge(S["D0"], ti + 1)
                if ti >= 2:
                    sc_.wait_ge(S["V0"], ti - 1)
                sc_.activation(stdv[ti % 2][:], veps[ti % 2][:], AF.Sqrt).then_inc(S["A0"], 1)
            for h in range(HPC if LEVEL >= 2 else 0):
                for sc in range(NSC):
                    Sg = h * NSC + sc
                    sc_.wait_ge(S["mQ"], 6 * Sg + 1)
                    if Sg >= 1:
                        sc_.wait_ge(S["vR"], 2 * Sg - 1)
                    sc_.activation(q_sb, P[0][:], AF.Copy).then_inc(S["aQ"], 1)
                    sc_.wait_ge(S["mQ"], 6 * Sg + 2)
                    if Sg >= 1:
                        sc_.wait_ge(S["vR"], 2 * Sg)
                    sc_.activation(k_sb, P[1][:], AF.Copy).then_inc(S["aQ"], 1)
                    sc_.wait_ge(S["mQ"], 6 * Sg + 3)
                    sc_.activation(v_tmp[Sg % 2], P[2][:], AF.Copy).then_inc(S["aQ"], 1)
                    sc_.wait_ge(S["mQ"], 6 * Sg + 6)
                    sc_.activation(v_tm[:, sc * 512:(sc + 1) * 512], PT[0], AF.Copy).then_inc(S["aQ"], 1)
                T0 = NT_H * h
                for t in range(NT_H):
                    T = T0 + t
                    buf = T % 2
                    qc, jb = ATT_TILES[t]
                    sc_.wait_ge(S["mS"], T + 1)
                    if t == 0:
                        sc_.wait_ge(S["vR"], 2 * NSC * (h + 1))
                    if T >= 2:
                        sc_.wait_ge(S["vS"], T - 1)
                    sc_.activation(s2[buf], P[4 + T % 3][:], AF.Square).then_inc(S["aS"], 1)

            for tc in range(NWTC if LEVEL >= 3 else 0):
                for fc in range(16):
                    idx = 16 * tc + fc
                    sc_.wait_ge(S["mW"], idx + 1)
                    if tc >= 1 and fc == 0 and WOPART >= 2:
                        sc_.wait_ge(S["mT"], 16 * tc)
                    sc_.activation(attn_T[fc], P[fc % 2][:], AF.Copy).then_inc(S["aW"], 1)
                for tt in range(4 if WOPART >= 3 else 0):
                    TT = 4 * tc + tt
                    for fcg in range(4):
                        gi = 16 * tc + 4 * tt + fcg
                        sc_.wait_ge(S["mT"], gi + 1)
                        if fcg == 0 and TT >= 2:
                            sc_.wait_ge(S["sR"], 16 * (TT - 1))
                        sc_.activation(stage[TT % 2][:, fcg * 512:(fcg + 1) * 512],
                                       P[2][:], AF.Copy).then_inc(S["aG"], 1)
            for ic in range(NIC if LEVEL >= 5 else 0):
                for tc in range(NTC):
                    k = ic * NTC + tc
                    pg = P[0] if k % 2 == 0 else P[2]
                    sc_.wait_ge(S["mA"], 2 * k + 1)
                    if k >= 2:
                        sc_.wait_ge(S["vA"], k - 1)
                    sc_.activation(g_s[k % 2], pg[:], AF.Silu,
                                   scale=gs_sb[:, ic:ic + 1]).then_inc(S["aA"], 1)
            for tc in range(NTC if LEVEL >= 6 else 0):
                for pi, (h0, h1) in enumerate(FB_PASSES):
                    gidx = tc * 3 + pi
                    sc_.wait_ge(S["mBc"], 12 * (gidx + 1))
                    for hi, hc in enumerate(range(h0, h1)):
                        ins = sc_.activation(ffn_T[hc][:, tc * 512:(tc + 1) * 512], P[hi][:],
                                             AF.Identity, scale=ds_sb[:, hc:hc + 1])
                    ins.then_inc(S["aB"], 1)
            for tt in range(16 if (LEVEL >= 7 and COMBPART >= 1) else 0):
                for fcg in range(4):
                    gi = 4 * tt + fcg
                    sc_.wait_ge(S["mC"], gi + 1)
                    if gi >= 2 and COMBPART >= 2:
                        pg = gi - 2
                        sc_.wait_ge(S["vC"], 5 * (pg // 4) + (pg % 4) + 2)
                    sc_.activation(ftm[gi % 2], PT[0], AF.Copy).then_inc(S["aC"], 1)

        @block.vector
        def _(ve):
            ve.wait_ge(S["gC"], 16 * 8)
            ve.memset(onesf[:], 1.0)
            for ti in range(NLA):
                ve.wait_ge(S["sL"], 16 * (ti + 1))
                p = ti % 2
                for kk in range(4):
                    ve.bn_stats(stat6[p][:, 6 * kk:6 * kk + 6],
                                x_sb[p][:, 512 * kk:512 * (kk + 1)])
                for kk in range(4):
                    ve.bn_aggr(mvc[p][:, 2 * kk:2 * kk + 2], stat6[p][:, 6 * kk:6 * kk + 6])
                ve.tensor_add(pq8[p][:, 0:2], mvc[p][:, 0:2], mvc[p][:, 2:4])
                ve.tensor_add(pq8[p][:, 2:4], mvc[p][:, 4:6], mvc[p][:, 6:8])
                ve.tensor_mul(sq8[p][:], mvc[p][:], mvc[p][:])
                ve.tensor_add(pq8[p][:, 4:6], pq8[p][:, 0:2], pq8[p][:, 2:4])
                ve.tensor_add(tmp4[p][:, 0:2], sq8[p][:, 0:2], sq8[p][:, 2:4])
                ve.tensor_add(tmp4[p][:, 2:4], sq8[p][:, 4:6], sq8[p][:, 6:8])
                ve.tensor_scalar_mul(mv[p][:, 0:1], pq8[p][:, 4:5], 0.25)
                ve.tensor_add(pq8[p][:, 6:8], tmp4[p][:, 0:2], tmp4[p][:, 2:4])
                ve.tensor_mul(tmp4[p][:, 0:1], mv[p][:, 0:1], mv[p][:, 0:1])
                ve.tensor_add(pq8[p][:, 0:1], pq8[p][:, 5:6], pq8[p][:, 6:7])
                ve.tensor_copy(padb[:], gs_sb[:, 0:1])
                ve.scalar_tensor_tensor(tmp4[p][:, 1:2], pq8[p][:, 0:1], 0.25,
                                        tmp4[p][:, 0:1], ALU.mult, ALU.subtract)
                ve.tensor_copy(padb[:], gs_sb[:, 0:1])
                ve.tensor_scalar_add(veps[p][:], tmp4[p][:, 1:2], EPS_LN).then_inc(S["D0"], 1)
                ve.wait_ge(S["A0"], ti + 1)
                ve.reciprocal(rstd[ti % 2][:], stdv[ti % 2][:])
                if ti >= 2:
                    ve.wait_ge(S["sW"], 16 * (ti - 1))
                ve.tensor_copy(padb[:], gs_sb[:, 0:1])
                ve.tensor_scalar(xn_sb[ti % 2], x_sb[ti % 2], mv[ti % 2][:, 0:1],
                                 rstd[ti % 2][:], ALU.subtract, ALU.mult).then_inc(S["V0"], 1)
            for h in range(HPC if LEVEL >= 2 else 0):
                for sc in range(NSC):
                    Sg = h * NSC + sc
                    ve.wait_ge(S["sT"], 16 * 18 * (Sg + 1))
                    if sc == 0 and h >= 1:
                        ve.wait_ge(S["mV"], NT_H * h)
                    ve.wait_ge(S["aQ"], 4 * Sg + 1)
                    ve.wait_ge(S["mQ"], 6 * Sg + 4)
                    ve.tensor_mul(t1, q_sb, cs_buf[Sg % 2])
                    ve.tensor_mul(t2, P[3][:], sn_buf[Sg % 2])
                    ve.tensor_add(q_bf[:, sc * 512:(sc + 1) * 512], t1, t2).then_inc(S["vR"], 1)
                    ve.wait_ge(S["aQ"], 4 * Sg + 2)
                    ve.wait_ge(S["mQ"], 6 * Sg + 5)
                    ve.tensor_mul(t1, k_sb, cs_buf[Sg % 2])
                    ve.tensor_mul(t2, P[4][:], sn_buf[Sg % 2])
                    ve.tensor_add(k_bf[:, sc * 512:(sc + 1) * 512], t1, t2).then_inc(S["vR"], 1)
                T0 = NT_H * h
                for t, (qc, jb) in enumerate(ATT_TILES):
                    T = T0 + t
                    buf = T % 2
                    ve.wait_ge(S["aS"], T + 1)
                    if T >= 2:
                        ve.wait_ge(S["mV"], T - 1)
                    if jb >= 4 * qc:
                        r = jb - 4 * qc
                        ve.tensor_mul(s2[buf], s2[buf], masks[:, r * 512:(r + 1) * 512])
                    ve.tensor_mul(s4[buf], s2[buf], s2[buf]).then_inc(S["vS"], 1)
                    if jb == 4 * (qc + 1) - 1:
                        gq = h * NQC + qc
                        psv = P[2] if gq % 2 == 0 else P[0]
                        pdn = P[1] if gq % 2 == 0 else P[3]
                        ve.wait_ge(S["mV"], T + 1)
                        ve.tensor_scalar_max(recip, pdn[:], EPS_AT)
                        ve.tensor_copy(padb[:], gs_sb[:, 0:1])
                        ve.reciprocal(recip, recip)
                        if gq >= 2:
                            ve.wait_ge(S["sA"], 16 * (gq - 1))
                        ve.tensor_copy(padb[:], gs_sb[:, 0:1])
                        ve.tensor_mul(e_out[gq % 2], psv[:], recip).then_inc(S["vF"], 1)
            for ic in range(NIC if LEVEL >= 5 else 0):
                for tc in range(NTC):
                    k = ic * NTC + tc
                    pu = P[1] if k % 2 == 0 else P[3]
                    ve.wait_ge(S["mA"], 2 * k + 2)
                    ve.wait_ge(S["aA"], k + 1)
                    if k >= 2:
                        ve.wait_ge(S["sFu"], 16 * (k - 1))
                    ve.scalar_tensor_tensor(fuse_sb[k % 2], pu[:], us_sb[:, ic:ic + 1],
                                            g_s[k % 2], ALU.mult, ALU.mult).then_inc(S["vA"], 1)
            for tt in range(16 if (LEVEL >= 7 and COMBPART >= 2) else 0):
                ve.wait_ge(S["sX"], 32 * (tt + 1))
                if tt >= 2 and COMBPART >= 3:
                    ve.wait_ge(S["ST"], 16 * (tt - 1))
                ve.tensor_add(ycomb[tt % 2], xcomb[tt % 2], rcomb[tt % 2]).then_inc(S["vC"], 1)
                for fcg in range(4):
                    gi = 4 * tt + fcg
                    ve.wait_ge(S["aC"], gi + 1)
                    ve.tensor_add(ycomb[tt % 2][:, fcg * 512:(fcg + 1) * 512],
                                  ycomb[tt % 2][:, fcg * 512:(fcg + 1) * 512],
                                  ftm[gi % 2]).then_inc(S["vC"], 1)

    return nc



def _layout_inputs(inputs):
    x = np.ascontiguousarray(np.asarray(inputs["L0_attn_x"], dtype=np.float32))
    Wq = np.asarray(inputs["L0_attn_Wq"], dtype=np.float32)
    Wk = np.asarray(inputs["L0_attn_Wk"], dtype=np.float32)
    Wv = np.asarray(inputs["L0_attn_Wv"], dtype=np.float32)
    Wo = np.asarray(inputs["L0_attn_Wo"], dtype=np.float32)
    gwf = np.asarray(inputs["L0_ffn_gate_w"], dtype=np.float32)
    uwf = np.asarray(inputs["L0_ffn_up_w"], dtype=np.float32)
    dwf = np.asarray(inputs["L0_ffn_down_w"], dtype=np.float32)
    gs = np.asarray(inputs["L0_ffn_gate_scale"], dtype=np.float32)
    us = np.asarray(inputs["L0_ffn_up_scale"], dtype=np.float32)
    ds = np.asarray(inputs["L0_ffn_down_scale"], dtype=np.float32)

    gw_t = np.ascontiguousarray(gwf.reshape(16, 128, 48, 128).transpose(2, 1, 0, 3))
    uw_t = np.ascontiguousarray(uwf.reshape(16, 128, 48, 128).transpose(2, 1, 0, 3))
    dw_t = np.ascontiguousarray(dwf.reshape(48, 128, 16, 128))
    gs_t = np.ascontiguousarray(gs.reshape(48, 128).T)
    us_t = np.ascontiguousarray(us.reshape(48, 128).T)
    ds_t = np.ascontiguousarray(ds.reshape(16, 128).T)

    half = 64
    theta = np.power(np.float32(10000.0),
                     -(np.arange(half, dtype=np.float32) * np.float32(2.0) / np.float32(128)))
    pos = np.arange(N, dtype=np.float32)
    ang = (theta[:, None] * pos[None, :]).astype(np.float32)
    cos = np.cos(ang).astype(np.float32)
    sin = np.sin(ang).astype(np.float32)
    cst = np.ascontiguousarray(np.vstack([cos, cos]))
    snt = np.ascontiguousarray(np.vstack([-sin, sin]))

    ones = np.ones((128, 128), dtype=ml_dtypes.bfloat16)
    eyeb = np.eye(128, dtype=ml_dtypes.bfloat16)
    eyef = np.eye(128, dtype=np.float32)
    swp = np.zeros((128, 128), dtype=np.float32)
    swp[(np.arange(128) + 64) % 128, np.arange(128)] = 1.0
    msk = np.zeros((4, 128, 512), dtype=ml_dtypes.bfloat16)
    jj = np.arange(128)[:, None]
    qq = np.arange(512)[None, :]
    for r in range(4):
        msk[r] = ((jj + 128 * r) <= qq).astype(ml_dtypes.bfloat16)

    in_maps = []
    for c in range(NCORE_G):
        b = c // 4
        l = c % 4
        wq_c = np.ascontiguousarray(
            Wq[512 * l:512 * (l + 1), :].T.reshape(16, 128, 4, 128).transpose(2, 1, 0, 3))
        wk_c = np.ascontiguousarray(
            Wk[512 * l:512 * (l + 1), :].T.reshape(16, 128, 4, 128).transpose(2, 1, 0, 3))
        wv_c = np.ascontiguousarray(
            Wv[512 * l:512 * (l + 1), :].T.reshape(16, 128, 4, 128).transpose(2, 1, 0, 3))
        wo_c = np.ascontiguousarray(
            Wo.T[512 * l:512 * (l + 1), :].reshape(4, 128, 2048).transpose(1, 0, 2))
        in_maps.append({
            "xb": x[b], "x_own": np.ascontiguousarray(x[b, 2048 * l:2048 * (l + 1)]),
            "wq": wq_c, "wk": wk_c, "wv": wv_c, "wo": wo_c,
            "gw": gw_t, "uw": uw_t, "dwt": dw_t,
            "gsc": gs_t, "usc": us_t, "dsc": ds_t,
            "cst": cst, "snt": snt,
            "onesd": ones, "eyebd": eyeb, "eyefd": eyef, "swpd": swp,
            "maskd": msk,
        })
    return in_maps


NCORE_G = 8


def kernel(**inputs):
    in_maps = _layout_inputs(inputs)
    nc = build()
    res = run_bass_kernel_spmd(nc, in_maps, core_ids=list(range(NCORE_G)))
    y = np.empty((B, N, DM), dtype=np.float32)
    for c in range(NCORE_G):
        b = c // 4
        l = c % 4
        y[b, 2048 * l:2048 * (l + 1), :] = res.results[c]["y_own"]
    return y


# version 08
# speedup vs baseline: 1.5839x; 1.0016x over previous
import numpy as np
import ml_dtypes
import concourse.bass as bass
import concourse.mybir as mybir
from concourse.bass_utils import run_bass_kernel_spmd

F32 = mybir.dt.float32
BF16 = mybir.dt.bfloat16
FP8 = mybir.dt.float8e4
AF = mybir.ActivationFunctionType
ALU = mybir.AluOpType
AX = mybir.AxisListType

B, N, DM = 2, 8192, 2048
DI = 6144
HPC = 4
OWN = 2048
EPS_LN = 1e-5
EPS_AT = 1e-6

NLT = N // 128
NLA = NLT + OWN // 128
NSC = N // 512
NQC = N // 512
NIC = DI // 128
NTC = OWN // 512
NWTC = N // 512
NEC = DM // 128

ATT_TILES = [(qc, jb) for qc in range(NQC) for jb in range(4 * (qc + 1))]
NT_H = len(ATT_TILES)
FB_PASSES = [(0, 6), (6, 12), (12, 16)]

DEBUG = ()
LEVEL = 7
WOPART = 3
SKIPRS = False
COMBPART = 3


def build():
    nc = bass.Bass("TRN2", target_bir_lowering=False, debug=False)
    xb = nc.dram_tensor("xb", [N, DM], F32, kind="ExternalInput").ap()
    x_own = nc.dram_tensor("x_own", [OWN, DM], F32, kind="ExternalInput").ap()
    wq = nc.dram_tensor("wq", [HPC, 128, NEC, 128], F32, kind="ExternalInput").ap()
    wk = nc.dram_tensor("wk", [HPC, 128, NEC, 128], F32, kind="ExternalInput").ap()
    wv = nc.dram_tensor("wv", [HPC, 128, NEC, 128], F32, kind="ExternalInput").ap()
    wo = nc.dram_tensor("wo", [128, HPC, DM], F32, kind="ExternalInput").ap()
    gw = nc.dram_tensor("gw", [NIC, 128, NEC, 128], F32, kind="ExternalInput").ap()
    uw = nc.dram_tensor("uw", [NIC, 128, NEC, 128], F32, kind="ExternalInput").ap()
    dwt = nc.dram_tensor("dwt", [NIC, 128, NEC, 128], F32, kind="ExternalInput").ap()
    gsc = nc.dram_tensor("gsc", [128, NIC], F32, kind="ExternalInput").ap()
    usc = nc.dram_tensor("usc", [128, NIC], F32, kind="ExternalInput").ap()
    dsc = nc.dram_tensor("dsc", [128, NEC], F32, kind="ExternalInput").ap()
    cst = nc.dram_tensor("cst", [128, N], F32, kind="ExternalInput").ap()
    snt = nc.dram_tensor("snt", [128, N], F32, kind="ExternalInput").ap()
    onesd = nc.dram_tensor("onesd", [128, 128], BF16, kind="ExternalInput").ap()
    eyebd = nc.dram_tensor("eyebd", [128, 128], BF16, kind="ExternalInput").ap()
    eyefd = nc.dram_tensor("eyefd", [128, 128], F32, kind="ExternalInput").ap()
    swpd = nc.dram_tensor("swpd", [128, 128], F32, kind="ExternalInput").ap()
    maskd = nc.dram_tensor("maskd", [4, 128, 512], BF16, kind="ExternalInput").ap()

    y_own = nc.dram_tensor("y_own", [OWN, DM], F32, kind="ExternalOutput").ap()

    xn_sc = nc.dram_tensor("xn_sc", [N, DM], BF16, kind="Internal").ap()
    xno_sc = nc.dram_tensor("xno_sc", [OWN, DM], BF16, kind="Internal").ap()
    od_sc = nc.dram_tensor("od_sc", [HPC, 128, N], BF16, kind="Internal").ap()
    fuse_sc = nc.dram_tensor("fuse_sc", [NIC, 128, OWN], BF16, kind="Internal").ap()
    dw8_sc = nc.dram_tensor("dw8_sc", [NIC, 128, DM], FP8, kind="Internal").ap()
    gw8_sc = nc.dram_tensor("gw8_sc", [NIC, 128, DM], FP8, kind="Internal").ap()
    uw8_sc = nc.dram_tensor("uw8_sc", [NIC, 128, DM], FP8, kind="Internal").ap()
    rs_in_t = nc.dram_tensor("rs_in", [N, DM], BF16, kind="Internal")
    rs_out_t = nc.dram_tensor("rs_out", [OWN, DM], BF16, kind="Internal")
    rs_in = rs_in_t.ap()
    rs_out = rs_out_t.ap()

    dbg = {}
    if "xn" in DEBUG:
        dbg["d_xn"] = nc.dram_tensor("d_xn", [N, DM], BF16, kind="ExternalOutput").ap()
    if "od" in DEBUG:
        dbg["d_od"] = nc.dram_tensor("d_od", [HPC * 128, N], BF16, kind="ExternalOutput").ap()
    if "rsin" in DEBUG:
        dbg["d_rsin"] = nc.dram_tensor("d_rsin", [N, DM], BF16, kind="ExternalOutput").ap()
    if "rsout" in DEBUG:
        dbg["d_rsout"] = nc.dram_tensor("d_rsout", [OWN, DM], BF16, kind="ExternalOutput").ap()
    if "fuse" in DEBUG:
        dbg["d_fuse"] = nc.dram_tensor("d_fuse", [NIC * 128, OWN], BF16, kind="ExternalOutput").ap()
    NDBG = len(dbg)

    from contextlib import ExitStack
    with ExitStack() as stack:
        en = stack.enter_context

        A = en(nc.sbuf_tensor("A", [128, 32768], BF16))
        Af = A.bitcast(F32)
        q_bf = A[:, 0:8192]
        k_bf = A[:, 8192:16384]
        v_tm = A[:, 16384:24576]
        xnoT = [A[:, ec * 2048:(ec + 1) * 2048] for ec in range(NEC)]
        q_sb = Af[:, 12288:12800]
        k_sb = Af[:, 12800:13312]
        t1 = Af[:, 13312:13824]
        t2 = Af[:, 13824:14336]
        v_tmp = [A[:, 28672 + i * 512:28672 + (i + 1) * 512] for i in range(2)]
        s2 = [Af[:, 12288:12800], Af[:, 12800:13312]]
        s4 = [A[:, 26624 + i * 512:26624 + (i + 1) * 512] for i in range(2)]
        recip = Af[:, 14848:15360]
        e_out = [A[:, 30720 + i * 512:30720 + (i + 1) * 512] for i in range(2)]

        Bx = en(nc.sbuf_tensor("Bx", [128, 32768], BF16))
        Bf = Bx.bitcast(F32)
        xnt = [[Bx[:, bi * 8192 + ec * 512: bi * 8192 + (ec + 1) * 512]
                for ec in range(NEC)] for bi in range(2)]
        cs_buf = [Bf[:, 8192 + i * 512:8192 + (i + 1) * 512] for i in range(2)]
        sn_buf = [Bf[:, 9216 + i * 512:9216 + (i + 1) * 512] for i in range(2)]
        wq_sb = [Bx[:, 20480 + bi * 6144:20480 + bi * 6144 + 2048] for bi in range(2)]
        wk_sb = [Bx[:, 22528 + bi * 6144:22528 + bi * 6144 + 2048] for bi in range(2)]
        wv_sb = [Bx[:, 24576 + bi * 6144:24576 + bi * 6144 + 2048] for bi in range(2)]
        attn_T = [Bf[:, fc * 512:(fc + 1) * 512] for fc in range(16)]
        stage = [Bx[:, 16384 + i * 2048:16384 + (i + 1) * 2048] for i in range(2)]
        wo_sb = Bx[:, 24576:32768]
        ffn_T = [Bx[:, hc * 2048:(hc + 1) * 2048] for hc in range(NEC)]

        C = en(nc.sbuf_tensor("C", [128, 24576], BF16))
        Cf = C.bitcast(F32)
        x_sb = [Cf[:, i * 2048:(i + 1) * 2048] for i in range(2)]
        xn_sb = [C[:, 8192 + i * 2048:8192 + (i + 1) * 2048] for i in range(2)]
        od_buf = [C[:, i * 2048:(i + 1) * 2048] for i in range(2)]
        g_s = [Cf[:, 1024 + i * 512:1024 + (i + 1) * 512] for i in range(2)]
        fuse_sb = [C[:, 4096 + i * 512:4096 + (i + 1) * 512] for i in range(2)]
        fuse_tc = [C[:, ic * 512:(ic + 1) * 512] for ic in range(NIC)]
        xcomb = [Af[:, i * 2048:(i + 1) * 2048] for i in range(2)]
        rcomb = [A[:, 8192 + i * 2048:8192 + (i + 1) * 2048] for i in range(2)]
        ycomb = [Af[:, 6144 + i * 2048:6144 + (i + 1) * 2048] for i in range(2)]
        ftm = [A[:, 28672 + i * 512:28672 + (i + 1) * 512] for i in range(2)]

        masks = en(nc.sbuf_tensor("masks", [128, 2048], BF16))
        gu8t = en(nc.sbuf_tensor("gu8t", [128, 8192], FP8))
        gu8 = [[gu8t[:, (bi * 2 + j) * 2048:(bi * 2 + j + 1) * 2048] for j in range(2)]
               for bi in range(2)]
        dw8t = en(nc.sbuf_tensor("dw8t", [128, 6144], FP8))
        dw8b = [dw8t[:, i * 3072:(i + 1) * 3072] for i in range(2)]
        dw8p = en(nc.sbuf_tensor("dw8p", [128, 2048], FP8))
        pp8 = [en(nc.sbuf_tensor(f"pp8_{i}", [128, 2048], FP8)) for i in range(2)]
        ones_s = en(nc.sbuf_tensor("ones_s", [128, 128], BF16))
        eyeb_s = en(nc.sbuf_tensor("eyeb_s", [128, 128], BF16))
        eyef_s = en(nc.sbuf_tensor("eyef_s", [128, 128], F32))
        swp_s = en(nc.sbuf_tensor("swp_s", [128, 128], F32))
        gs_sb = en(nc.sbuf_tensor("gs_sb", [128, NIC], F32))
        us_sb = en(nc.sbuf_tensor("us_sb", [128, NIC], F32))
        ds_sb = en(nc.sbuf_tensor("ds_sb", [128, NEC], F32))
        stat6 = [en(nc.sbuf_tensor(f"stat6_{i}", [128, 24], F32)) for i in range(2)]
        mvc = [en(nc.sbuf_tensor(f"mvc_{i}", [128, 8], F32)) for i in range(2)]
        tmp4 = [en(nc.sbuf_tensor(f"tmp4_{i}", [128, 4], F32)) for i in range(2)]
        pq8 = [en(nc.sbuf_tensor(f"pq8_{i}", [128, 8], F32)) for i in range(2)]
        sq8 = [en(nc.sbuf_tensor(f"sq8_{i}", [128, 8], F32)) for i in range(2)]
        padb = en(nc.sbuf_tensor("padb", [128, 1], F32))
        onesf = en(nc.sbuf_tensor("onesf", [1, 128], F32))
        mv = [en(nc.sbuf_tensor(f"mv_{i}", [128, 2], F32)) for i in range(2)]
        veps = [en(nc.sbuf_tensor(f"veps_{i}", [128, 1], F32)) for i in range(2)]
        stdv = [en(nc.sbuf_tensor(f"stdv_{i}", [128, 1], F32)) for i in range(2)]
        rstd = [en(nc.sbuf_tensor(f"rstd_{i}", [128, 1], F32)) for i in range(2)]

        P = [en(nc.psum_tensor(f"P{i}", [128, 512], F32)) for i in range(7)]
        PTt = en(nc.psum_tensor("PTt", [128, 1024], BF16))
        PT = [PTt[:, 0:512], PTt[:, 512:1024]]

        names = ["sL", "sW", "V0", "D0", "A0", "sT", "gQ", "mQ", "aQ", "vR",
                 "mS", "aS", "vS", "mV", "vF", "sA", "gWo", "gO", "mW", "aW",
                 "mT", "aG", "sR", "CC", "gD", "gF", "sTF", "mA", "aA", "vA",
                 "sFu", "sFB", "gB", "mBc", "aB", "sX", "mC", "vC",
                 "ST", "gC", "aC", "vF1", "mB1", "aB1"]
        S = {nm: en(nc.semaphore(nm)) for nm in names}
        block = en(nc.Block())

        def ln_src(ti):
            if ti < NLT:
                return xb[ti * 128:(ti + 1) * 128, :]
            t = ti - NLT
            return x_own[t * 128:(t + 1) * 128, :]

        def ln_dst(ti):
            if ti < NLT:
                return xn_sc[ti * 128:(ti + 1) * 128, :]
            t = ti - NLT
            return xno_sc[t * 128:(t + 1) * 128, :]

        @block.sync
        def _(sync):
            sync.dma_start(out=x_sb[0], in_=ln_src(0)).then_inc(S["sL"], 16)
            sync.dma_start(out=x_sb[1], in_=ln_src(1)).then_inc(S["sL"], 16)
            for ti in range(NLA):
                sync.wait_ge(S["V0"], ti + 1)
                sync.dma_start(out=ln_dst(ti), in_=xn_sb[ti % 2]).then_inc(S["sW"], 16)
                if ti + 2 < NLA:
                    sync.dma_start(out=x_sb[ti % 2], in_=ln_src(ti + 2)).then_inc(S["sL"], 16)
            for h in range(HPC if LEVEL >= 2 else 0):
                for sc in range(NSC):
                    Sg = h * NSC + sc
                    sync.wait_ge(S["sW"], 16 * min(4 * sc + 4, NLT))
                    if Sg >= 2:
                        sync.wait_ge(S["mQ"], 6 * (Sg - 2) + 3)
                        sync.wait_ge(S["vR"], 2 * (Sg - 1))
                    for ec in range(NEC):
                        sync.dma_start(
                            out=xnt[Sg % 2][ec],
                            in_=xn_sc[sc * 512:(sc + 1) * 512, ec * 128:(ec + 1) * 128],
                            transpose=True).then_inc(S["sT"], 16)
                    sync.dma_start(out=cs_buf[Sg % 2],
                                   in_=cst[:, sc * 512:(sc + 1) * 512]).then_inc(S["sT"], 16)
                    sync.dma_start(out=sn_buf[Sg % 2],
                                   in_=snt[:, sc * 512:(sc + 1) * 512]).then_inc(S["sT"], 16)
                for qc in range(NQC):
                    g = h * NQC + qc
                    sync.wait_ge(S["vF"], g + 1)
                    sync.dma_start(out=od_sc[h, :, qc * 512:(qc + 1) * 512],
                                   in_=e_out[g % 2]).then_inc(S["sA"], 16)
            if LEVEL < 2:
                sync.wait_ge(S["sW"], 16 * NLA)
            if LEVEL >= 5:
                sync.wait_ge(S["vF"], HPC * NQC)
            if LEVEL >= 5:
                sync.wait_ge(S["sA"], 16 * HPC * NQC)
                sync.wait_ge(S["sW"], 16 * NLA)
                for ec in range(NEC):
                    sync.dma_start(out=xnoT[ec], in_=xno_sc[:, ec * 128:(ec + 1) * 128],
                                   transpose=True).then_inc(S["sTF"], 16)
            for tc in range(NWTC if (LEVEL >= 3 and WOPART >= 3) else 0):
                for tt in range(4):
                    TT = 4 * tc + tt
                    sync.wait_ge(S["aG"], 4 * TT + 4)
                    sync.dma_start(out=rs_in[TT * 128:(TT + 1) * 128, :],
                                   in_=stage[TT % 2]).then_inc(S["sR"], 16)
            if LEVEL >= 5:
                sync.wait_ge(S["gD"], 32 * 9 * NWTC)
                if LEVEL >= 4 and not SKIPRS:
                    sync.wait_ge(S["CC"], 1)
                for ic0 in (0, 1):
                    sync.dma_start(out=gu8[ic0 % 2][0], in_=gw8_sc[ic0]).then_inc(S["gF"], 16)
                    sync.dma_start(out=gu8[ic0 % 2][1], in_=uw8_sc[ic0]).then_inc(S["gF"], 16)
            for ic in range(NIC if LEVEL >= 5 else 0):
                for tc in range(NTC):
                    k = ic * NTC + tc
                    sync.wait_ge(S["vA"], k + 1)
                    sync.dma_start(out=fuse_sc[ic, :, tc * 512:(tc + 1) * 512],
                                   in_=fuse_sb[k % 2]).then_inc(S["sFu"], 16)
                if ic + 2 < NIC:
                    sync.dma_start(out=gu8[ic % 2][0], in_=gw8_sc[ic + 2]).then_inc(S["gF"], 16)
                    sync.dma_start(out=gu8[ic % 2][1], in_=uw8_sc[ic + 2]).then_inc(S["gF"], 16)
            if LEVEL >= 6:
                sync.wait_ge(S["sFu"], 16 * NIC * NTC)
            for tc in range(NTC if LEVEL >= 6 else 0):
                if tc > 0:
                    sync.wait_ge(S["mBc"], 36 * tc)
                for cg in range(6):
                    ic0 = cg * 8
                    sync.dma_start(
                        out=C[:, ic0 * 512:(ic0 + 8) * 512].rearrange("p (i t) -> p i t", t=512),
                        in_=fuse_sc[ic0:ic0 + 8, :, tc * 512:(tc + 1) * 512].rearrange("i p t -> p i t"),
                    ).then_inc(S["sFB"], 16)
            if LEVEL >= 7:
                sync.wait_ge(S["mA"], 2 * NIC * NTC)
                sync.wait_ge(S["CC"], 1)
            for tt in ((0, 1) if (LEVEL >= 7 and COMBPART >= 2) else ()):
                sync.dma_start(out=xcomb[tt], in_=x_own[tt * 128:(tt + 1) * 128, :]).then_inc(S["sX"], 16)
                sync.dma_start(out=rcomb[tt], in_=rs_out[tt * 128:(tt + 1) * 128, :]).then_inc(S["sX"], 16)
            for tt in range(16 if (LEVEL >= 7 and COMBPART >= 3) else 0):
                sync.wait_ge(S["vC"], 5 * (tt + 1))
                sync.dma_start(out=y_own[tt * 128:(tt + 1) * 128, :],
                               in_=ycomb[tt % 2]).then_inc(S["ST"], 16)
                if tt + 2 < 16:
                    sync.dma_start(out=xcomb[tt % 2],
                                   in_=x_own[(tt + 2) * 128:(tt + 3) * 128, :]).then_inc(S["sX"], 16)
                    sync.dma_start(out=rcomb[tt % 2],
                                   in_=rs_out[(tt + 2) * 128:(tt + 3) * 128, :]).then_inc(S["sX"], 16)
            nst = 16 if (LEVEL >= 7 and COMBPART >= 3) else 0
            if LEVEL >= 7 and COMBPART == 0:
                sync.wait_ge(S["mC"], 64)
            elif LEVEL >= 7 and COMBPART == 1:
                sync.wait_ge(S["aC"], 64)
            elif LEVEL >= 7 and COMBPART == 2:
                sync.wait_ge(S["vC"], 80)
            if LEVEL < 7:
                if LEVEL == 1:
                    sync.wait_ge(S["sW"], 16 * NLA)
                elif LEVEL == 2:
                    sync.wait_ge(S["sA"], 16 * HPC * NQC)
                elif LEVEL == 3:
                    if WOPART >= 3:
                        sync.wait_ge(S["sR"], 16 * NWTC * 4)
                    elif WOPART == 2:
                        sync.wait_ge(S["mT"], 256)
                    else:
                        sync.wait_ge(S["aW"], 256)
                elif LEVEL == 4:
                    sync.wait_ge(S["CC"], 1)
                elif LEVEL == 5:
                    sync.wait_ge(S["sFu"], 16 * NIC * NTC)
                elif LEVEL == 6:
                    sync.wait_ge(S["aB"], 3 * NTC)
            if "xn" in DEBUG:
                sync.dma_start(out=dbg["d_xn"], in_=xn_sc).then_inc(S["ST"], 16)
                nst += 1
            if "od" in DEBUG:
                sync.dma_start(out=dbg["d_od"], in_=od_sc.rearrange("h p t -> (h p) t")).then_inc(S["ST"], 16)
                nst += 1
            if "rsin" in DEBUG:
                sync.dma_start(out=dbg["d_rsin"], in_=rs_in).then_inc(S["ST"], 16)
                nst += 1
            if "rsout" in DEBUG:
                sync.dma_start(out=dbg["d_rsout"], in_=rs_out).then_inc(S["ST"], 16)
                nst += 1
            if "fuse" in DEBUG:
                sync.dma_start(out=dbg["d_fuse"], in_=fuse_sc.rearrange("i p t -> (i p) t")).then_inc(S["ST"], 16)
                nst += 1
            sync.wait_ge(S["ST"], 16 * nst)

        @block.gpsimd
        def _(g):
            g.dma_start(out=ones_s[:], in_=onesd).then_inc(S["gC"], 16)
            g.dma_start(out=eyeb_s[:], in_=eyebd).then_inc(S["gC"], 16)
            g.dma_start(out=eyef_s[:], in_=eyefd).then_inc(S["gC"], 16)
            g.dma_start(out=swp_s[:], in_=swpd).then_inc(S["gC"], 16)
            g.dma_start(out=masks[:].rearrange("p (r q) -> p r q", q=512),
                        in_=maskd.rearrange("r p q -> p r q")).then_inc(S["gC"], 16)
            g.dma_start(out=gs_sb[:], in_=gsc).then_inc(S["gC"], 16)
            g.dma_start(out=us_sb[:], in_=usc).then_inc(S["gC"], 16)
            g.dma_start(out=ds_sb[:], in_=dsc).then_inc(S["gC"], 16)
            for h in range(HPC if LEVEL >= 2 else 0):
                if h >= 2:
                    g.wait_ge(S["mQ"], 6 * NSC * (h - 1))
                for wsrc, wdst in [(wq, wq_sb), (wk, wk_sb), (wv, wv_sb)]:
                    g.dma_start(out=wdst[h % 2].rearrange("p (ec d) -> p ec d", d=128),
                                in_=wsrc[h]).then_inc(S["gQ"], 16)
            if LEVEL >= 3:
                g.wait_ge(S["vF"], HPC * NQC)
                g.dma_start(out=wo_sb.rearrange("p (dh f) -> p dh f", f=DM),
                            in_=wo).then_inc(S["gWo"], 16)
            for tc in range(NWTC if LEVEL >= 3 else 0):
                if tc == 0:
                    g.wait_ge(S["sA"], 16 * HPC * NQC)
                if tc >= 2:
                    g.wait_ge(S["mW"], 16 * (tc - 1))
                g.dma_start(out=od_buf[tc % 2].rearrange("p (dh t) -> p dh t", t=512),
                            in_=od_sc[:, :, tc * 512:(tc + 1) * 512].rearrange("h p t -> p h t"),
                            ).then_inc(S["gO"], 16)
                for j in range(9 if LEVEL >= 5 else 0):
                    p = tc * 9 + j
                    ic = p // 3
                    srcw, dstw = ((dwt, dw8_sc), (gw, gw8_sc), (uw, uw8_sc))[p % 3]
                    if p >= 2:
                        g.wait_ge(S["gD"], 32 * (p - 1))
                    g.dma_start(out=pp8[p % 2][:],
                                in_=srcw[ic].rearrange("p a b -> p (a b)")).then_inc(S["gD"], 16)
                    g.wait_ge(S["gD"], 32 * p + 16)
                    g.dma_start(out=dstw[ic], in_=pp8[p % 2][:]).then_inc(S["gD"], 16)

            if LEVEL >= 4 and not SKIPRS:
                g.wait_ge(S["sR"], 16 * NWTC * 4)
                if LEVEL >= 5:
                    g.wait_ge(S["gD"], 32 * 9 * NWTC)
                g.collective_compute(
                    "ReduceScatter", ALU.add,
                    replica_groups=[[0, 1, 2, 3], [4, 5, 6, 7]],
                    ins=[rs_in_t.ap().opt()], outs=[rs_out_t.ap().opt()],
                ).then_inc(S["CC"], 1)
            if LEVEL >= 6:
                g.wait_ge(S["gD"], 32 * 9 * NWTC)
                g.wait_ge(S["CC"], 1)
            for tc in range(NTC if LEVEL >= 6 else 0):
                for pi, (h0, h1) in enumerate(FB_PASSES):
                    gidx = tc * 3 + pi
                    w = (h1 - h0) * 128
                    for cg in range(12):
                        CH = gidx * 12 + cg
                        if CH >= 2:
                            g.wait_ge(S["mBc"], CH - 1)
                        g.dma_start(
                            out=dw8b[CH % 2][:, 0:4 * w].rearrange("p (i t) -> p i t", t=w),
                            in_=dw8_sc[cg * 4:cg * 4 + 4, :, h0 * 128:h1 * 128].rearrange("i p t -> p i t"),
                        ).then_inc(S["gB"], 16)

        @block.tensor
        def _(te):
            te.wait_ge(S["gC"], 16 * 8)
            for h in range(HPC if LEVEL >= 2 else 0):
                hb = h % 2
                for sc in range(NSC):
                    Sg = h * NSC + sc
                    te.wait_ge(S["sT"], 16 * 18 * (Sg + 1))
                    te.wait_ge(S["gQ"], 16 * 3 * (h + 1))
                    if h >= 1 and sc == 0:
                        te.wait_ge(S["vF"], NQC * h)
                    if Sg >= 1:
                        te.wait_ge(S["aQ"], 4 * (Sg - 1) + 3)
                    for wsb, pb in ((wq_sb, P[0]), (wk_sb, P[1]), (wv_sb, P[2])):
                        for ec in range(NEC):
                            ins = te.matmul(pb[:], wsb[hb][:, ec * 128:(ec + 1) * 128],
                                            xnt[Sg % 2][ec], start=(ec == 0), stop=(ec == NEC - 1))
                        ins.then_inc(S["mQ"], 1)
                    te.wait_ge(S["aQ"], 4 * Sg + 1)
                    if Sg >= 1:
                        te.wait_ge(S["vR"], 2 * (Sg - 1) + 1)
                    te.matmul(P[3][:], swp_s[:], q_sb, start=True, stop=True).then_inc(S["mQ"], 1)
                    te.wait_ge(S["aQ"], 4 * Sg + 2)
                    if Sg >= 1:
                        te.wait_ge(S["vR"], 2 * Sg)
                    te.matmul(P[4][:], swp_s[:], k_sb, start=True, stop=True).then_inc(S["mQ"], 1)
                    te.wait_ge(S["aQ"], 4 * Sg + 3)
                    if Sg >= 1:
                        te.wait_ge(S["aQ"], 4 * (Sg - 1) + 4)
                    for j in range(4):
                        ins = te.transpose(PT[0][:, j * 128:(j + 1) * 128],
                                           v_tmp[Sg % 2][:, j * 128:(j + 1) * 128],
                                           eyeb_s[:])
                    ins.then_inc(S["mQ"], 1)
                T0 = NT_H * h
                def emit_st(t, h=h):
                    T = T0 + t
                    qc, jb = ATT_TILES[t]
                    if t == 0:
                        te.wait_ge(S["vR"], 2 * NSC * (h + 1))
                    if T >= 3:
                        te.wait_ge(S["aS"], T - 2)
                    te.wait_ge(S["vR"], 2 * NSC * h + 2 * (qc + 1))
                    te.matmul(P[4 + T % 3][:], k_bf[:, jb * 128:(jb + 1) * 128],
                              q_bf[:, qc * 512:(qc + 1) * 512],
                              start=True, stop=True).then_inc(S["mS"], 1)
                emit_st(0)
                emit_st(1)
                for t, (qc, jb) in enumerate(ATT_TILES):
                    T = T0 + t
                    buf = T % 2
                    if t + 2 < NT_H:
                        emit_st(t + 2)
                    te.wait_ge(S["vS"], T + 1)
                    first = (jb == 0)
                    last = (jb == 4 * (qc + 1) - 1)
                    if first:
                        gq = h * NQC + qc
                        if gq > 0:
                            te.wait_ge(S["vF"], gq)
                        if qc == 0:
                            te.wait_ge(S["aQ"], 64 * h + 63)
                    if jb % 4 == 0:
                        te.wait_ge(S["aQ"], 64 * h + 4 * (jb // 4 + 1))
                    te.matmul(P[2][:], v_tm[:, jb * 128:(jb + 1) * 128], s4[buf],
                              start=first, stop=last)
                    te.matmul(P[1][:], ones_s[:], s4[buf],
                              start=first, stop=last).then_inc(S["mV"], 1)
            if LEVEL >= 3:
                te.wait_ge(S["vF"], HPC * NQC)
                te.wait_ge(S["gWo"], 16)
            for tc in range(NWTC if LEVEL >= 3 else 0):
                te.wait_ge(S["gO"], 16 * (tc + 1))
                for fc in range(16):
                    idx = 16 * tc + fc
                    bank = P[fc % 2]
                    if idx >= 2:
                        te.wait_ge(S["aW"], idx - 1)
                    for dh in range(HPC):
                        ins = te.matmul(bank[:],
                                        wo_sb[:, dh * 2048 + fc * 128: dh * 2048 + fc * 128 + 128],
                                        od_buf[tc % 2][:, dh * 512:(dh + 1) * 512],
                                        start=(dh == 0), stop=(dh == HPC - 1))
                    ins.then_inc(S["mW"], 1)
                if WOPART < 2:
                    continue
                te.wait_ge(S["aW"], 16 * tc + 16)
                for tt in range(4):
                    for fcg in range(4):
                        gi = 16 * tc + 4 * tt + fcg
                        if gi >= 2 and WOPART >= 3:
                            te.wait_ge(S["aG"], gi - 1)
                        bank = P[2] if gi % 2 == 0 else P[3]
                        for j in range(4):
                            fc = 4 * fcg + j
                            ins = te.transpose(bank[:, j * 128:(j + 1) * 128],
                                               attn_T[fc][:, tt * 128:(tt + 1) * 128],
                                               eyef_s[:])
                        ins.then_inc(S["mT"], 1)
            if LEVEL >= 5:
                te.wait_ge(S["sTF"], 16 * NEC)
                if LEVEL >= 3:
                    te.wait_ge(S["aW"], 256)
                    te.wait_ge(S["aG"], 256)
            for ic in range(NIC if LEVEL >= 5 else 0):
                te.wait_ge(S["gF"], 32 * (ic + 1))
                for tc in range(NTC):
                    k = ic * NTC + tc
                    pg, pu = (P[0], P[1]) if k % 2 == 0 else (P[2], P[3])
                    if k >= 2:
                        te.wait_ge(S["aA"], k - 1)
                        te.wait_ge(S["vA"], k - 1)
                    for gj, pb in ((0, pg), (1, pu)):
                        for ec in range(NEC):
                            ins = te.matmul(pb[:], gu8[ic % 2][gj][:, ec * 128:(ec + 1) * 128],
                                            xnoT[ec][:, tc * 512:(tc + 1) * 512],
                                            start=(ec == 0), stop=(ec == NEC - 1))
                        ins.then_inc(S["mA"], 1)
            for tc in range(NTC if LEVEL >= 6 else 0):
                te.wait_ge(S["sFB"], 16 * 6 * (tc + 1))
                for pi, (h0, h1) in enumerate(FB_PASSES):
                    gidx = tc * 3 + pi
                    nh = h1 - h0
                    w = nh * 128
                    if gidx == 0:
                        te.wait_ge(S["aA"], NIC * NTC)
                        te.wait_ge(S["vA"], NIC * NTC)
                    else:
                        te.wait_ge(S["aB"], gidx)
                    for cg in range(12):
                        CH = gidx * 12 + cg
                        te.wait_ge(S["gB"], 16 * (CH + 1))
                        for j in range(4):
                            ic = cg * 4 + j
                            for hi in range(nh):
                                ins = te.matmul(P[hi][:],
                                                dw8b[CH % 2][:, j * w + hi * 128: j * w + hi * 128 + 128],
                                                fuse_tc[ic],
                                                start=(ic == 0), stop=(ic == NIC - 1))
                        ins.then_inc(S["mBc"], 1)
            if LEVEL >= 7:
                te.wait_ge(S["aB"], 3 * NTC)
            for tt in range(16 if LEVEL >= 7 else 0):
                for fcg in range(4):
                    gi = 4 * tt + fcg
                    pt = PT[0]
                    if gi >= 1 and COMBPART >= 1:
                        te.wait_ge(S["aC"], gi)
                    for j in range(4):
                        hc = fcg * 4 + j
                        ins = te.transpose(pt[:, j * 128:(j + 1) * 128],
                                           ffn_T[hc][:, tt * 128:(tt + 1) * 128],
                                           eyeb_s[:])
                    ins.then_inc(S["mC"], 1)

        @block.scalar
        def _(sc_):
            sc_.wait_ge(S["gC"], 16 * 8)
            for ti in range(NLA):
                sc_.wait_ge(S["D0"], ti + 1)
                if ti >= 2:
                    sc_.wait_ge(S["V0"], ti - 1)
                sc_.activation(stdv[ti % 2][:], veps[ti % 2][:], AF.Sqrt).then_inc(S["A0"], 1)
            for h in range(HPC if LEVEL >= 2 else 0):
                for sc in range(NSC):
                    Sg = h * NSC + sc
                    sc_.wait_ge(S["mQ"], 6 * Sg + 1)
                    if Sg >= 1:
                        sc_.wait_ge(S["vR"], 2 * Sg - 1)
                    sc_.activation(q_sb, P[0][:], AF.Copy).then_inc(S["aQ"], 1)
                    sc_.wait_ge(S["mQ"], 6 * Sg + 2)
                    if Sg >= 1:
                        sc_.wait_ge(S["vR"], 2 * Sg)
                    sc_.activation(k_sb, P[1][:], AF.Copy).then_inc(S["aQ"], 1)
                    sc_.wait_ge(S["mQ"], 6 * Sg + 3)
                    sc_.activation(v_tmp[Sg % 2], P[2][:], AF.Copy).then_inc(S["aQ"], 1)
                    sc_.wait_ge(S["mQ"], 6 * Sg + 6)
                    sc_.activation(v_tm[:, sc * 512:(sc + 1) * 512], PT[0], AF.Copy).then_inc(S["aQ"], 1)
                T0 = NT_H * h
                for t in range(NT_H):
                    T = T0 + t
                    buf = T % 2
                    qc, jb = ATT_TILES[t]
                    sc_.wait_ge(S["mS"], T + 1)
                    if t == 0:
                        sc_.wait_ge(S["vR"], 2 * NSC * (h + 1))
                    if T >= 2:
                        sc_.wait_ge(S["vS"], T - 1)
                    sc_.activation(s2[buf], P[4 + T % 3][:], AF.Square).then_inc(S["aS"], 1)

            for tc in range(NWTC if LEVEL >= 3 else 0):
                for fc in range(16):
                    idx = 16 * tc + fc
                    sc_.wait_ge(S["mW"], idx + 1)
                    if tc >= 1 and fc == 0 and WOPART >= 2:
                        sc_.wait_ge(S["mT"], 16 * tc)
                    sc_.activation(attn_T[fc], P[fc % 2][:], AF.Copy).then_inc(S["aW"], 1)
                for tt in range(4 if WOPART >= 3 else 0):
                    TT = 4 * tc + tt
                    for fcg in range(4):
                        gi = 16 * tc + 4 * tt + fcg
                        sc_.wait_ge(S["mT"], gi + 1)
                        if fcg == 0 and TT >= 2:
                            sc_.wait_ge(S["sR"], 16 * (TT - 1))
                        sc_.activation(stage[TT % 2][:, fcg * 512:(fcg + 1) * 512],
                                       P[2][:] if gi % 2 == 0 else P[3][:],
                                       AF.Copy).then_inc(S["aG"], 1)
            for ic in range(NIC if LEVEL >= 5 else 0):
                for tc in range(NTC):
                    k = ic * NTC + tc
                    pg = P[0] if k % 2 == 0 else P[2]
                    sc_.wait_ge(S["mA"], 2 * k + 1)
                    if k >= 2:
                        sc_.wait_ge(S["vA"], k - 1)
                    sc_.activation(g_s[k % 2], pg[:], AF.Silu,
                                   scale=gs_sb[:, ic:ic + 1]).then_inc(S["aA"], 1)
            for tc in range(NTC if LEVEL >= 6 else 0):
                for pi, (h0, h1) in enumerate(FB_PASSES):
                    gidx = tc * 3 + pi
                    sc_.wait_ge(S["mBc"], 12 * (gidx + 1))
                    for hi, hc in enumerate(range(h0, h1)):
                        ins = sc_.activation(ffn_T[hc][:, tc * 512:(tc + 1) * 512], P[hi][:],
                                             AF.Identity, scale=ds_sb[:, hc:hc + 1])
                    ins.then_inc(S["aB"], 1)
            for tt in range(16 if (LEVEL >= 7 and COMBPART >= 1) else 0):
                for fcg in range(4):
                    gi = 4 * tt + fcg
                    sc_.wait_ge(S["mC"], gi + 1)
                    if gi >= 2 and COMBPART >= 2:
                        pg = gi - 2
                        sc_.wait_ge(S["vC"], 5 * (pg // 4) + (pg % 4) + 2)
                    sc_.activation(ftm[gi % 2], PT[0], AF.Copy).then_inc(S["aC"], 1)

        @block.vector
        def _(ve):
            ve.wait_ge(S["gC"], 16 * 8)
            ve.memset(onesf[:], 1.0)
            for ti in range(NLA):
                ve.wait_ge(S["sL"], 16 * (ti + 1))
                p = ti % 2
                for kk in range(4):
                    ve.bn_stats(stat6[p][:, 6 * kk:6 * kk + 6],
                                x_sb[p][:, 512 * kk:512 * (kk + 1)])
                for kk in range(4):
                    ve.bn_aggr(mvc[p][:, 2 * kk:2 * kk + 2], stat6[p][:, 6 * kk:6 * kk + 6])
                ve.tensor_add(pq8[p][:, 0:2], mvc[p][:, 0:2], mvc[p][:, 2:4])
                ve.tensor_add(pq8[p][:, 2:4], mvc[p][:, 4:6], mvc[p][:, 6:8])
                ve.tensor_mul(sq8[p][:], mvc[p][:], mvc[p][:])
                ve.tensor_add(pq8[p][:, 4:6], pq8[p][:, 0:2], pq8[p][:, 2:4])
                ve.tensor_add(tmp4[p][:, 0:2], sq8[p][:, 0:2], sq8[p][:, 2:4])
                ve.tensor_add(tmp4[p][:, 2:4], sq8[p][:, 4:6], sq8[p][:, 6:8])
                ve.tensor_scalar_mul(mv[p][:, 0:1], pq8[p][:, 4:5], 0.25)
                ve.tensor_add(pq8[p][:, 6:8], tmp4[p][:, 0:2], tmp4[p][:, 2:4])
                ve.tensor_mul(tmp4[p][:, 0:1], mv[p][:, 0:1], mv[p][:, 0:1])
                ve.tensor_add(pq8[p][:, 0:1], pq8[p][:, 5:6], pq8[p][:, 6:7])
                ve.tensor_copy(padb[:], gs_sb[:, 0:1])
                ve.scalar_tensor_tensor(tmp4[p][:, 1:2], pq8[p][:, 0:1], 0.25,
                                        tmp4[p][:, 0:1], ALU.mult, ALU.subtract)
                ve.tensor_copy(padb[:], gs_sb[:, 0:1])
                ve.tensor_scalar_add(veps[p][:], tmp4[p][:, 1:2], EPS_LN).then_inc(S["D0"], 1)
                ve.wait_ge(S["A0"], ti + 1)
                ve.reciprocal(rstd[ti % 2][:], stdv[ti % 2][:])
                if ti >= 2:
                    ve.wait_ge(S["sW"], 16 * (ti - 1))
                ve.tensor_copy(padb[:], gs_sb[:, 0:1])
                ve.tensor_scalar(xn_sb[ti % 2], x_sb[ti % 2], mv[ti % 2][:, 0:1],
                                 rstd[ti % 2][:], ALU.subtract, ALU.mult).then_inc(S["V0"], 1)
            for h in range(HPC if LEVEL >= 2 else 0):
                for sc in range(NSC):
                    Sg = h * NSC + sc
                    ve.wait_ge(S["sT"], 16 * 18 * (Sg + 1))
                    if sc == 0 and h >= 1:
                        ve.wait_ge(S["mV"], NT_H * h)
                    ve.wait_ge(S["aQ"], 4 * Sg + 1)
                    ve.wait_ge(S["mQ"], 6 * Sg + 4)
                    ve.tensor_mul(t1, q_sb, cs_buf[Sg % 2])
                    ve.tensor_mul(t2, P[3][:], sn_buf[Sg % 2])
                    ve.tensor_add(q_bf[:, sc * 512:(sc + 1) * 512], t1, t2).then_inc(S["vR"], 1)
                    ve.wait_ge(S["aQ"], 4 * Sg + 2)
                    ve.wait_ge(S["mQ"], 6 * Sg + 5)
                    ve.tensor_mul(t1, k_sb, cs_buf[Sg % 2])
                    ve.tensor_mul(t2, P[4][:], sn_buf[Sg % 2])
                    ve.tensor_add(k_bf[:, sc * 512:(sc + 1) * 512], t1, t2).then_inc(S["vR"], 1)
                T0 = NT_H * h
                for t, (qc, jb) in enumerate(ATT_TILES):
                    T = T0 + t
                    buf = T % 2
                    ve.wait_ge(S["aS"], T + 1)
                    if T >= 2:
                        ve.wait_ge(S["mV"], T - 1)
                    if jb >= 4 * qc:
                        r = jb - 4 * qc
                        ve.tensor_mul(s2[buf], s2[buf], masks[:, r * 512:(r + 1) * 512])
                    ve.tensor_mul(s4[buf], s2[buf], s2[buf]).then_inc(S["vS"], 1)
                    if jb == 4 * (qc + 1) - 1:
                        gq = h * NQC + qc
                        ve.wait_ge(S["mV"], T + 1)
                        ve.tensor_scalar_max(recip, P[1][:], EPS_AT)
                        ve.tensor_copy(padb[:], gs_sb[:, 0:1])
                        ve.reciprocal(recip, recip)
                        if gq >= 2:
                            ve.wait_ge(S["sA"], 16 * (gq - 1))
                        ve.tensor_copy(padb[:], gs_sb[:, 0:1])
                        ve.tensor_mul(e_out[gq % 2], P[2][:], recip).then_inc(S["vF"], 1)
            for ic in range(NIC if LEVEL >= 5 else 0):
                for tc in range(NTC):
                    k = ic * NTC + tc
                    pu = P[1] if k % 2 == 0 else P[3]
                    ve.wait_ge(S["mA"], 2 * k + 2)
                    ve.wait_ge(S["aA"], k + 1)
                    if k >= 2:
                        ve.wait_ge(S["sFu"], 16 * (k - 1))
                    ve.scalar_tensor_tensor(fuse_sb[k % 2], pu[:], us_sb[:, ic:ic + 1],
                                            g_s[k % 2], ALU.mult, ALU.mult).then_inc(S["vA"], 1)
            for tt in range(16 if (LEVEL >= 7 and COMBPART >= 2) else 0):
                ve.wait_ge(S["sX"], 32 * (tt + 1))
                if tt >= 2 and COMBPART >= 3:
                    ve.wait_ge(S["ST"], 16 * (tt - 1))
                ve.tensor_add(ycomb[tt % 2], xcomb[tt % 2], rcomb[tt % 2]).then_inc(S["vC"], 1)
                for fcg in range(4):
                    gi = 4 * tt + fcg
                    ve.wait_ge(S["aC"], gi + 1)
                    ve.tensor_add(ycomb[tt % 2][:, fcg * 512:(fcg + 1) * 512],
                                  ycomb[tt % 2][:, fcg * 512:(fcg + 1) * 512],
                                  ftm[gi % 2]).then_inc(S["vC"], 1)

    return nc



def _layout_inputs(inputs):
    x = np.ascontiguousarray(np.asarray(inputs["L0_attn_x"], dtype=np.float32))
    Wq = np.asarray(inputs["L0_attn_Wq"], dtype=np.float32)
    Wk = np.asarray(inputs["L0_attn_Wk"], dtype=np.float32)
    Wv = np.asarray(inputs["L0_attn_Wv"], dtype=np.float32)
    Wo = np.asarray(inputs["L0_attn_Wo"], dtype=np.float32)
    gwf = np.asarray(inputs["L0_ffn_gate_w"], dtype=np.float32)
    uwf = np.asarray(inputs["L0_ffn_up_w"], dtype=np.float32)
    dwf = np.asarray(inputs["L0_ffn_down_w"], dtype=np.float32)
    gs = np.asarray(inputs["L0_ffn_gate_scale"], dtype=np.float32)
    us = np.asarray(inputs["L0_ffn_up_scale"], dtype=np.float32)
    ds = np.asarray(inputs["L0_ffn_down_scale"], dtype=np.float32)

    gw_t = np.ascontiguousarray(gwf.reshape(16, 128, 48, 128).transpose(2, 1, 0, 3))
    uw_t = np.ascontiguousarray(uwf.reshape(16, 128, 48, 128).transpose(2, 1, 0, 3))
    dw_t = np.ascontiguousarray(dwf.reshape(48, 128, 16, 128))
    gs_t = np.ascontiguousarray(gs.reshape(48, 128).T)
    us_t = np.ascontiguousarray(us.reshape(48, 128).T)
    ds_t = np.ascontiguousarray(ds.reshape(16, 128).T)

    half = 64
    theta = np.power(np.float32(10000.0),
                     -(np.arange(half, dtype=np.float32) * np.float32(2.0) / np.float32(128)))
    pos = np.arange(N, dtype=np.float32)
    ang = (theta[:, None] * pos[None, :]).astype(np.float32)
    cos = np.cos(ang).astype(np.float32)
    sin = np.sin(ang).astype(np.float32)
    cst = np.ascontiguousarray(np.vstack([cos, cos]))
    snt = np.ascontiguousarray(np.vstack([-sin, sin]))

    ones = np.ones((128, 128), dtype=ml_dtypes.bfloat16)
    eyeb = np.eye(128, dtype=ml_dtypes.bfloat16)
    eyef = np.eye(128, dtype=np.float32)
    swp = np.zeros((128, 128), dtype=np.float32)
    swp[(np.arange(128) + 64) % 128, np.arange(128)] = 1.0
    msk = np.zeros((4, 128, 512), dtype=ml_dtypes.bfloat16)
    jj = np.arange(128)[:, None]
    qq = np.arange(512)[None, :]
    for r in range(4):
        msk[r] = ((jj + 128 * r) <= qq).astype(ml_dtypes.bfloat16)

    in_maps = []
    for c in range(NCORE_G):
        b = c // 4
        l = c % 4
        wq_c = np.ascontiguousarray(
            Wq[512 * l:512 * (l + 1), :].T.reshape(16, 128, 4, 128).transpose(2, 1, 0, 3))
        wk_c = np.ascontiguousarray(
            Wk[512 * l:512 * (l + 1), :].T.reshape(16, 128, 4, 128).transpose(2, 1, 0, 3))
        wv_c = np.ascontiguousarray(
            Wv[512 * l:512 * (l + 1), :].T.reshape(16, 128, 4, 128).transpose(2, 1, 0, 3))
        wo_c = np.ascontiguousarray(
            Wo.T[512 * l:512 * (l + 1), :].reshape(4, 128, 2048).transpose(1, 0, 2))
        in_maps.append({
            "xb": x[b], "x_own": np.ascontiguousarray(x[b, 2048 * l:2048 * (l + 1)]),
            "wq": wq_c, "wk": wk_c, "wv": wv_c, "wo": wo_c,
            "gw": gw_t, "uw": uw_t, "dwt": dw_t,
            "gsc": gs_t, "usc": us_t, "dsc": ds_t,
            "cst": cst, "snt": snt,
            "onesd": ones, "eyebd": eyeb, "eyefd": eyef, "swpd": swp,
            "maskd": msk,
        })
    return in_maps


NCORE_G = 8


def kernel(**inputs):
    in_maps = _layout_inputs(inputs)
    nc = build()
    res = run_bass_kernel_spmd(nc, in_maps, core_ids=list(range(NCORE_G)))
    y = np.empty((B, N, DM), dtype=np.float32)
    for c in range(NCORE_G):
        b = c // 4
        l = c % 4
        y[b, 2048 * l:2048 * (l + 1), :] = res.results[c]["y_own"]
    return y
